# version 01
# baseline (speedup 1.0000x reference)
import contextlib
import numpy as np
import ml_dtypes
import concourse.bass as bass
import concourse.mybir as mybir
from concourse.bass_utils import run_bass_kernel_spmd

F32 = mybir.dt.float32
BF16 = mybir.dt.bfloat16
AF = mybir.ActivationFunctionType
ALU = mybir.AluOpType
AX = mybir.AxisListType

ENGINES = ("tensor", "vector", "scalar", "gpsimd", "sync")
DRAIN_ENGINES = {"vector", "scalar", "gpsimd"}


class Buf:
    __slots__ = ("name", "last_writer", "readers", "is_psum")

    def __init__(self, name, is_psum=False):
        self.name = name
        self.last_writer = None
        self.readers = []
        self.is_psum = is_psum

    def __repr__(self):
        return f"Buf({self.name})"


class Step:
    __slots__ = ("eng", "emit", "reads", "writes", "tag", "kind", "idx",
                 "needs_inc", "inc_index", "waits", "drain_before")

    def __init__(self, eng, emit, reads, writes, tag, kind):
        self.eng = eng
        self.emit = emit
        self.reads = reads
        self.writes = writes
        self.tag = tag
        self.kind = kind
        self.idx = -1
        self.needs_inc = False
        self.inc_index = None
        self.waits = []
        self.drain_before = False


class Prog:
    def __init__(self, nc):
        self.nc = nc
        self.steps = []
        self.bufs = []
        self.n_barriers = 0

    def buf(self, name, is_psum=False):
        b = Buf(name, is_psum)
        self.bufs.append(b)
        return b

    def op(self, eng, emit, reads=(), writes=(), tag=""):
        s = Step(eng, emit, list(reads), list(writes), tag, "op")
        self.steps.append(s)
        return s

    def dma(self, queue, emit, reads=(), writes=(), tag=""):
        s = Step(queue, emit, list(reads), list(writes), tag, "dma")
        self.steps.append(s)
        return s

    def barrier(self, tag="barrier"):
        s = Step(None, None, [], [], tag, "barrier")
        self.steps.append(s)
        self.n_barriers += 1
        return s

    def finalize(self):
        drained_upto = {e: -1 for e in ENGINES}
        last_op_idx = {e: -1 for e in ENGINES}
        ensured = {e: {} for e in ENGINES}

        for i, s in enumerate(self.steps):
            s.idx = i

        for s in self.steps:
            if s.kind == "barrier":
                for b in self.bufs:
                    b.last_writer = None
                    b.readers = []
                for e in ENGINES:
                    drained_upto[e] = s.idx
                    ensured[e] = {"_barrier": s.idx}
                continue

            deps = {}
            for b in s.reads:
                if b.last_writer is not None:
                    deps[b.last_writer.idx] = b.last_writer
                if b.is_psum and s.eng in ("vector", "scalar"):
                    for r in b.readers:
                        if r.eng != s.eng and r.eng in ("vector", "scalar") and r.kind == "op":
                            deps[r.idx] = r
            for b in s.writes:
                if b.last_writer is not None:
                    deps[b.last_writer.idx] = b.last_writer
                for r in b.readers:
                    deps[r.idx] = r

            for d in deps.values():
                if d is s:
                    continue
                if d.kind == "dma":
                    d.needs_inc = True
                    s.waits.append(d)
                elif d.eng == s.eng:
                    if s.eng == "tensor":
                        pass
                    else:
                        if d.idx > drained_upto[s.eng]:
                            s.drain_before = True
                else:
                    d.needs_inc = True
                    s.waits.append(d)

            if s.drain_before:
                drained_upto[s.eng] = last_op_idx[s.eng]
            if s.kind == "op":
                last_op_idx[s.eng] = s.idx

            for b in s.reads:
                b.readers.append(s)
            for b in s.writes:
                b.last_writer = s
                b.readers = []

        DMA_POOL = 8
        self.DMA_POOL = DMA_POOL
        self.prog_counts = {e: 0 for e in ENGINES}
        self.dma_counts = {e: 0 for e in ENGINES}
        for s in self.steps:
            if s.kind == "dma":
                j = self.dma_counts[s.eng]
                self.dma_counts[s.eng] += 1
                slot = j % DMA_POOL
                rnd = j // DMA_POOL + 1
                s.inc_index = (slot, rnd)
                s.needs_inc = True
            elif s.kind == "op" and s.needs_inc:
                self.prog_counts[s.eng] += 1
                s.inc_index = self.prog_counts[s.eng]
            elif s.kind == "barrier":
                for e in ENGINES:
                    self.prog_counts[e] += 1
                s.inc_index = (dict(self.prog_counts), dict(self.dma_counts))

        ensured = {e: {} for e in ENGINES}

        def need_wait(eng, key, val):
            cur = ensured[eng].get(key, 0)
            if val <= cur:
                return False
            ensured[eng][key] = val
            return True

        def dma_keys_at(counts, eng):
            out = []
            n = counts[eng]
            for slot in range(min(n, DMA_POOL)):
                rounds = (n - slot - 1) // DMA_POOL + 1
                out.append((("d", eng, slot), 16 * rounds))
            return out

        self._dma_keys_at = dma_keys_at
        for s in self.steps:
            if s.kind == "barrier":
                pc, dc = s.inc_index
                for e in ENGINES:
                    for e2 in ENGINES:
                        ensured[e][("p", e2)] = pc[e2]
                        for key, val in dma_keys_at(dc, e2):
                            ensured[e][key] = val
                continue
            resolved = []
            if s.kind == "dma":
                slot, rnd = s.inc_index
                if rnd > 1 and need_wait(s.eng, ("d", s.eng, slot), 16 * (rnd - 1)):
                    resolved.append((("d", s.eng, slot), 16 * (rnd - 1)))
            for d in s.waits:
                if d.kind == "dma":
                    slot, rnd = d.inc_index
                    key = ("d", d.eng, slot)
                    val = 16 * rnd
                else:
                    key = ("p", d.eng)
                    val = d.inc_index
                if need_wait(s.eng, key, val):
                    resolved.append((key, val))
            s.waits = resolved

    def emit(self, block, prog_sems, dma_sems):

        def sem_for(key):
            if key[0] == "p":
                return prog_sems[key[1]]
            return dma_sems[key[1]][key[2]]

        def emit_for(eng_name, eng):
            for s in self.steps:
                if s.kind == "barrier":
                    pc, dc = s.inc_index
                    if eng_name != "sync":
                        eng.drain()
                    eng.nop().then_inc(prog_sems[eng_name], 1)
                    for e2 in ENGINES:
                        if e2 != eng_name:
                            eng.wait_ge(prog_sems[e2], pc[e2])
                        for key, val in self._dma_keys_at(dc, e2):
                            eng.wait_ge(sem_for(key), val)
                    continue
                if s.eng != eng_name:
                    continue
                if s.drain_before:
                    eng.drain()
                for key, val in s.waits:
                    eng.wait_ge(sem_for(key), val)
                inst = s.emit(eng)
                if s.kind == "dma":
                    slot, _ = s.inc_index
                    inst.then_inc(dma_sems[eng_name][slot], 16)
                elif s.needs_inc:
                    if inst is None:
                        raise RuntimeError(f"step {s.tag} needs inc but emitted no instruction")
                    inst.then_inc(prog_sems[eng_name], 1)

        @block.tensor
        def _(e):
            emit_for("tensor", e)

        @block.vector
        def _(e):
            emit_for("vector", e)

        @block.scalar
        def _(e):
            emit_for("scalar", e)

        @block.gpsimd
        def _(e):
            emit_for("gpsimd", e)

        @block.sync
        def _(e):
            emit_for("sync", e)


NCORES = 8
B, T, D = 8, 2048, 1024
TPC = T
NTILES = 16


class Ctx:

    def __init__(self, nc, prog, stack):
        self.nc = nc
        self.prog = prog
        self.stack = stack
        self._scopes = [stack]
        self.name_i = 0
        self.banks = []
        for i in range(8):
            t = stack.enter_context(nc.psum_tensor(f"bank{i}", [128, 512], F32))
            self.banks.append((t, prog.buf(f"bank{i}", is_psum=True)))
        self._bank_rr = 0
        self.ident = stack.enter_context(nc.sbuf_tensor("ident", [128, 128], BF16))
        self.ident_buf = prog.buf("ident")
        self.ident32 = stack.enter_context(nc.sbuf_tensor("ident32", [128, 128], F32))
        self.ident32_buf = prog.buf("ident32")
        self.ones_col = stack.enter_context(nc.sbuf_tensor("ones_col", [128, 1], F32))
        self.ones_buf = prog.buf("ones_col")
        self.ones_row = stack.enter_context(nc.sbuf_tensor("ones_row", [1, 128], F32))
        self.onesr_buf = prog.buf("ones_row")
        self.ones_row_bf = stack.enter_context(nc.sbuf_tensor("ones_row_bf", [1, 128], BF16))
        self.onesrbf_buf = prog.buf("ones_row_bf")
        self.consts = stack.enter_context(nc.sbuf_tensor("consts", [128, 8], F32))
        self.consts_buf = prog.buf("consts")
        self.const_vals = {}
        p = prog
        p.op("gpsimd", lambda e: e.memset(self.ident[:], 0.0), writes=[self.ident_buf], tag="ident0")
        p.op("gpsimd", lambda e: e.affine_select(
            out=self.ident[:], in_=self.ident[:], compare_op=ALU.not_equal,
            fill=1.0, base=0, pattern=[[-1, 128]], channel_multiplier=1),
            reads=[self.ident_buf], writes=[self.ident_buf], tag="ident1")
        p.op("gpsimd", lambda e: e.memset(self.ident32[:], 0.0), writes=[self.ident32_buf], tag="id32a")
        p.op("gpsimd", lambda e: e.affine_select(
            out=self.ident32[:], in_=self.ident32[:], compare_op=ALU.not_equal,
            fill=1.0, base=0, pattern=[[-1, 128]], channel_multiplier=1),
            reads=[self.ident32_buf], writes=[self.ident32_buf], tag="id32b")
        p.op("gpsimd", lambda e: e.memset(self.ones_col[:], 1.0), writes=[self.ones_buf], tag="ones")
        p.op("gpsimd", lambda e: e.memset(self.ones_row[:], 1.0), writes=[self.onesr_buf], tag="onesr")
        p.op("gpsimd", lambda e: e.memset(self.ones_row_bf[:], 1.0), writes=[self.onesrbf_buf], tag="onesrbf")

    def causal_masks(self):
        if hasattr(self, "_masks"):
            return self._masks
        masks, masks_b = self.sb("cmask", [128, 4, 512], BF16)
        for m in range(4):
            self.prog.op("gpsimd", lambda e, m=m: e.memset(masks[:, m, :], 0.0),
                         writes=[masks_b], tag="mask0")
            self.prog.op("gpsimd", lambda e, m=m: e.affine_select(
                out=masks[:, m, :], in_=masks[:, m, :], compare_op=ALU.is_ge,
                fill=-1e30, base=128 * m, pattern=[[-1, 512]], channel_multiplier=1),
                reads=[masks_b], writes=[masks_b], tag="mask1")
        self._masks = (masks, masks_b)
        return self._masks

    def const_col(self, val):
        if val in self.const_vals:
            return self.const_vals[val]
        i = len(self.const_vals)
        assert i < 8
        ap = self.consts[:, i:i + 1]
        self.prog.op("gpsimd", lambda e, ap=ap, v=val: e.memset(ap, v),
                     writes=[self.consts_buf], tag=f"const{val}")
        self.const_vals[val] = ap
        return ap

    def uname(self, base):
        self.name_i += 1
        return f"{base}_{self.name_i}"

    def sb(self, base, shape, dtype=F32):
        t = self._scopes[-1].enter_context(self.nc.sbuf_tensor(self.uname(base), shape, dtype))
        return t, self.prog.buf(base)

    def ln_scratch(self, d):
        if not hasattr(self, "_ln_cache"):
            self._ln_cache = {}
        key = (len(self._scopes), d)
        if key not in self._ln_cache:
            negmu = self.sb("negmu", [128, 1])
            ssq = self.sb("ssq", [128, 1])
            rstd = self.sb("rstd", [128, 1])
            sqscr = self.sb("sqscr", [128, d])
            xbf = self.sb("xbf", [128, d], BF16)
            self._ln_cache[key] = (negmu, ssq, rstd, sqscr, xbf)
        return self._ln_cache[key]

    @contextlib.contextmanager
    def scope(self, tag="scope"):
        st = contextlib.ExitStack()
        self._scopes.append(st)
        try:
            yield
        finally:
            depth = len(self._scopes)
            self._scopes.pop()
            if hasattr(self, "_ln_cache"):
                for k in [k for k in self._ln_cache if k[0] >= depth]:
                    del self._ln_cache[k]
            self.prog.barrier(f"scope_end_{tag}")
            st.close()

    def take_bank(self, hold=False):
        if not hasattr(self, "_held"):
            self._held = set()
        for _ in range(16):
            i = self._bank_rr % 8
            self._bank_rr += 1
            if i not in self._held:
                t, b = self.banks[i]
                if hold:
                    self._held.add(i)
                    return t, b, i
                return t, b
        raise RuntimeError("all banks held")

    def release_bank(self, i):
        self._held.discard(i)

    def matmul_acc(self, psum_ap, psum_buf, mms, tag="mm", extra_reads=()):
        n = len(mms)
        for j, (lhsT, rhs, rbufs) in enumerate(mms):
            st = (j == 0)
            sp = (j == n - 1)
            self.prog.op(
                "tensor",
                lambda e, lhsT=lhsT, rhs=rhs, st=st, sp=sp: e.matmul(psum_ap, lhsT, rhs, start=st, stop=sp),
                reads=list(rbufs) + list(extra_reads),
                writes=[psum_buf], tag=tag)

    def evac(self, dst_ap, dst_buf, src_ap, src_buf, eng="vector", tag="evac"):
        self.prog.op(eng, lambda e: e.tensor_copy(dst_ap, src_ap),
                     reads=[src_buf], writes=[dst_buf], tag=tag)

    def ln_tiles(self, x_t, x_bufs, xn_t, xn_bufs, ntiles=NTILES, eps=1e-5, d=1024, dbg=None):
        p = self.prog
        negmu, negmu_b = self.sb("negmu", [128, 1])
        ssq, ssq_b = self.sb("ssq", [128, 1])
        rstd, rstd_b = self.sb("rstd", [128, 1])
        sqscr, sqscr_b = self.sb("sqscr", [128, d])
        epsc = self.const_col(eps)
        for i in range(ntiles):
            p.op("vector", lambda e, i=i: e.tensor_reduce(
                negmu[:], x_t[:, i, :], axis=AX.X, op=ALU.add, negate=True),
                 reads=[x_bufs[i]], writes=[negmu_b], tag="lnsum")
            p.op("vector", lambda e: e.tensor_scalar(
                negmu[:], negmu[:], 1.0 / d, None, op0=ALU.mult),
                 reads=[negmu_b], writes=[negmu_b], tag="lnmu")
            p.op("scalar", lambda e, i=i: e.activation(
                sqscr[:], x_t[:, i, :], AF.Square, bias=negmu[:], scale=1.0, accum_out=ssq[:]),
                 reads=[x_bufs[i], negmu_b], writes=[sqscr_b, ssq_b], tag="lnsq")
            p.op("scalar", lambda e: e.activation(rstd[:], ssq[:], AF.Ln, bias=epsc, scale=1.0 / d),
                 reads=[ssq_b, self.consts_buf], writes=[rstd_b], tag="lnln")
            p.op("scalar", lambda e: e.activation(rstd[:], rstd[:], AF.Exp, bias=0.0, scale=-0.5),
                 reads=[rstd_b], writes=[rstd_b], tag="lnexp")
            p.op("vector", lambda e, i=i: e.tensor_scalar(
                xn_t[:, i, :], x_t[:, i, :], negmu[:], rstd[:], op0=ALU.add, op1=ALU.mult),
                 reads=[x_bufs[i], negmu_b, rstd_b], writes=[xn_bufs[i]], tag="lnapply")
            if dbg is not None:
                dt, db = dbg
                p.op("vector", lambda e, i=i: e.tensor_copy(dt[:, i, 0:1], negmu[:]),
                     reads=[negmu_b], writes=[db], tag="dbgmv")
                p.op("vector", lambda e, i=i: e.tensor_copy(dt[:, i, 2:3], rstd[:]),
                     reads=[rstd_b], writes=[db], tag="dbgrstd")

    def transpose_to_featmajor(self, xn_t, xn_bufs, xnT_t, xnT_buf, ntiles=NTILES, nchunk=8, cast_src=None):
        p = self.prog
        xbf, xbf_b = self.sb("xbf", [128, nchunk * 128], BF16)
        for i in range(ntiles):
            p.op("vector", lambda e, i=i: e.tensor_copy(xbf[:], xn_t[:, i, :]),
                 reads=[xn_bufs[i]], writes=[xbf_b], tag="castbf")
            for g in range(0, nchunk, 4):
                bank, bank_b = self.take_bank()
                bview = bank.ap().bitcast(BF16)
                for c in range(g, min(g + 4, nchunk)):
                    p.op("tensor", lambda e, c=c, g=g, bview=bview: e.transpose(
                        bview[:, 128 * (c - g):128 * (c - g) + 128],
                        xbf[:, 128 * c:128 * c + 128], self.ident[:]),
                         reads=[xbf_b, self.ident_buf], writes=[bank_b], tag="xpose")
                nc_ = min(g + 4, nchunk) - g
                p.op("vector", lambda e, i=i, g=g, nc_=nc_, bview=bview: e.tensor_copy(
                    xnT_t[:, g:g + nc_, 128 * i:128 * i + 128],
                    bview[:, 0:128 * nc_].rearrange("p (a b) -> p a b", b=128)),
                     reads=[bank_b], writes=[xnT_buf], tag="xpevac")


class Launch:

    def __init__(self, name="prog"):
        self.nc = bass.Bass("TRN2", target_bir_lowering=False, debug=False)
        self.stack = contextlib.ExitStack()
        self.prog = Prog(self.nc)
        self.name = name

    def build(self, body, in_specs, out_specs):
        nc = self.nc
        self.din = {n: nc.dram_tensor(n, list(s), F32, kind="ExternalInput").ap()
                    for n, s in in_specs.items()}
        self.dout = {n: nc.dram_tensor(n, list(s), F32, kind="ExternalOutput").ap()
                     for n, s in out_specs.items()}
        ctx = Ctx(nc, self.prog, self.stack)
        body(ctx, self.din, self.dout)
        self.prog.barrier("final")
        self.prog.finalize()
        prog_sems = {e: self.stack.enter_context(nc.semaphore(f"ps_{e}")) for e in ENGINES}
        dma_sems = {e: [self.stack.enter_context(nc.semaphore(f"ds_{e}_{i}"))
                        for i in range(self.prog.DMA_POOL)] for e in ENGINES}
        with nc.Block() as block:
            self.prog.emit(block, prog_sems, dma_sems)
        self.stack.close()
        return self.nc



def load_stream(ctx, dram_ap, ntiles=NTILES, d=1024, name="x"):
    t, _ = ctx.sb(name, [128, ntiles, d])
    bufs = [ctx.prog.buf(f"{name}{i}") for i in range(ntiles)]
    src = dram_ap.rearrange("(n p) d -> p n d", p=128)
    ctx.prog.dma("sync", lambda e: e.dma_start(out=t[:], in_=src),
                 writes=bufs, tag=f"ld_{name}")
    return t, bufs


def store_stream(ctx, dram_ap, t, bufs, ntiles=NTILES, d=1024):
    dst = dram_ap.rearrange("(n p) d -> p n d", p=128)
    ctx.prog.dma("sync", lambda e: e.dma_start(out=dst, in_=t[:]),
                 reads=bufs, tag="st_stream")


def ln_to_featmajor(ctx, x_t, x_bufs, xnT, xnT_buf, ntiles=NTILES, d=1024, eps=1e-5,
                    xn_out=None, affine=None, tile_list=None, dst_off=0):
    with ctx.scope("ln"):
        _ln_to_featmajor(ctx, x_t, x_bufs, xnT, xnT_buf, ntiles, d, eps, xn_out, affine, tile_list, dst_off)


def _ln_to_featmajor(ctx, x_t, x_bufs, xnT, xnT_buf, ntiles=NTILES, d=1024, eps=1e-5,
                    xn_out=None, affine=None, tile_list=None, dst_off=0):
    p = ctx.prog
    nchunk = d // 128
    (negmu, negmu_b), (ssq, ssq_b), (rstd, rstd_b), (sqscr, sqscr_b), (xbf, xbf_b) = ctx.ln_scratch(d)
    if xn_out is None:
        xn_t = sqscr.ap().rearrange("p (a b) -> p a b", a=1)
        xn_bufs = [sqscr_b]
        mode_scratch = True
    else:
        xn_t, xn_bufs = xn_out
        mode_scratch = False
    epsc = ctx.const_col(eps)
    if tile_list is None:
        tile_list = list(range(ntiles))
    for ii, i in enumerate(tile_list):
        j = 0 if mode_scratch else (ii if xn_t.shape[1] < ntiles else i)
        xnb = xn_bufs[j if j < len(xn_bufs) else 0]
        p.op("vector", lambda e, i=i: e.tensor_reduce(
            negmu[:], x_t[:, i, :], axis=AX.X, op=ALU.add, negate=True),
             reads=[x_bufs[i]], writes=[negmu_b], tag="lnsum")
        p.op("vector", lambda e: e.tensor_scalar(
            negmu[:], negmu[:], 1.0 / d, None, op0=ALU.mult),
             reads=[negmu_b], writes=[negmu_b], tag="lnmu")
        p.op("scalar", lambda e, i=i: e.activation(
            sqscr[:], x_t[:, i, :], AF.Square, bias=negmu[:], scale=1.0, accum_out=ssq[:]),
             reads=[x_bufs[i], negmu_b], writes=[sqscr_b, ssq_b], tag="lnsq")
        p.op("scalar", lambda e: e.activation(rstd[:], ssq[:], AF.Ln, bias=epsc, scale=1.0 / d),
             reads=[ssq_b, ctx.consts_buf], writes=[rstd_b], tag="lnln")
        p.op("scalar", lambda e: e.activation(rstd[:], rstd[:], AF.Exp, bias=0.0, scale=-0.5),
             reads=[rstd_b], writes=[rstd_b], tag="lnexp")
        p.op("vector", lambda e, i=i, j=j: e.tensor_scalar(
            xn_t[:, j, :], x_t[:, i, :], negmu[:], rstd[:], op0=ALU.add, op1=ALU.mult),
             reads=[x_bufs[i], negmu_b, rstd_b], writes=[xnb], tag="lnapply")
        if affine is not None:
            wbc, wbc_b, bbc, bbc_b = affine
            p.op("vector", lambda e, j=j: e.tensor_tensor(xn_t[:, j, :], xn_t[:, j, :], wbc[:], op=ALU.mult),
                 reads=[xnb, wbc_b], writes=[xnb], tag="lnaffw")
            p.op("vector", lambda e, j=j: e.tensor_tensor(xn_t[:, j, :], xn_t[:, j, :], bbc[:], op=ALU.add),
                 reads=[xnb, bbc_b], writes=[xnb], tag="lnaffb")
        if xnT is None:
            continue
        p.op("vector", lambda e, j=j: e.tensor_copy(xbf[:], xn_t[:, j, :]),
             reads=[xnb], writes=[xbf_b], tag="castbf")
        for g in range(0, nchunk, 4):
            bank, bank_b = ctx.take_bank()
            bview = bank.ap().bitcast(BF16)
            nc_ = min(g + 4, nchunk) - g
            for c in range(g, g + nc_):
                p.op("tensor", lambda e, c=c, g=g, bview=bview: e.transpose(
                    bview[:, 128 * (c - g):128 * (c - g) + 128],
                    xbf[:, 128 * c:128 * c + 128], ctx.ident[:]),
                     reads=[xbf_b, ctx.ident_buf], writes=[bank_b], tag="xpose")
            p.op("vector", lambda e, ii=ii, g=g, nc_=nc_, bview=bview: e.tensor_copy(
                xnT[:, g:g + nc_, 128 * (ii + dst_off):128 * (ii + dst_off) + 128],
                bview[:, 0:128 * nc_].rearrange("p (a b) -> p a b", b=128)),
                 reads=[bank_b], writes=[xnT_buf], tag="xpevac")


def emit_L1_ffn(ctx, din, x_t, x_bufs, want_dbg=False):
    p = ctx.prog
    NE, DFF = 8, 4096
    xnT, _x = ctx.sb("xnT", [128, 8, 2048], BF16)
    xnT_buf = ctx.prog.buf("xnT")
    ln_to_featmajor(ctx, x_t, x_bufs, xnT, xnT_buf)

    gw, gw_b = ctx.sb("gw", [128, 8, 8], BF16)
    p.dma("gpsimd", lambda e: e.dma_start(out=gw[:], in_=din["l1f_gate_w"].rearrange("(c p) n -> p c n", p=128)),
          writes=[gw_b], tag="ld_gw")
    gb, gb_b = ctx.sb("gb", [1, 8])
    p.dma("sync", lambda e: e.dma_start(out=gb[:], in_=din["l1f_gate_b"]), writes=[gb_b], tag="ld_gb")
    b2r, b2r_b = ctx.sb("b2r", [8, 1024])
    p.dma("sync", lambda e: e.dma_start(out=b2r[:], in_=din["l1f_b2"]), writes=[b2r_b], tag="ld_b2")
    b1c, b1c_b = ctx.sb("b1c", [128, 8, 32])
    p.dma("sync", lambda e: e.dma_start(out=b1c[:], in_=din["l1f_b1c"]), writes=[b1c_b], tag="ld_b1")

    gates, gates_b = ctx.sb("gates", [128, 16, 8])
    lgt, lgt_b = ctx.sb("lgt", [128, 8])
    m1, m1_b = ctx.sb("m1", [128, 1])
    m2, m2_b = ctx.sb("m2", [128, 1])
    eq1, eq1_b = ctx.sb("eq1", [128, 8])
    eq2, eq2_b = ctx.sb("eq2", [128, 8])
    wt1, wt1_b = ctx.sb("wt1", [128, 1])
    wt2, wt2_b = ctx.sb("wt2", [128, 1])
    negm, negm_b = ctx.sb("negm", [128, 2])
    gT, gT_b = ctx.sb("gT", [8, 128])
    for i in range(NTILES):
        bank, bank_b = ctx.take_bank()
        mm = [(xnT[:, c, 128 * i:128 * i + 128], gw[:, c, :], [xnT_buf, gw_b]) for c in range(8)]
        mm.append((ctx.ones_row[0:1, 0:128], gb[:], [ctx.onesr_buf, gb_b]))
        ctx.matmul_acc(bank.ap()[:, 0:8], bank_b, mm, tag="gate_mm")
        p.op("vector", lambda e, bank=bank: e.tensor_copy(lgt[:], bank.ap()[:, 0:8]),
             reads=[bank_b], writes=[lgt_b], tag="gate_evac")
        p.op("vector", lambda e: e.tensor_reduce(m1[:], lgt[:], axis=AX.X, op=ALU.max),
             reads=[lgt_b], writes=[m1_b], tag="g_m1")
        p.op("vector", lambda e: e.tensor_scalar(eq1[:], lgt[:], m1[:], None, op0=ALU.is_ge),
             reads=[lgt_b, m1_b], writes=[eq1_b], tag="g_eq1")
        p.op("vector", lambda e: e.tensor_scalar(eq2[:], eq1[:], -1e30, None, op0=ALU.mult),
             reads=[eq1_b], writes=[eq2_b], tag="g_mask")
        p.op("vector", lambda e: e.tensor_tensor(eq2[:], lgt[:], eq2[:], op=ALU.add),
             reads=[lgt_b, eq2_b], writes=[eq2_b], tag="g_lgt2")
        p.op("vector", lambda e: e.tensor_reduce(m2[:], eq2[:], axis=AX.X, op=ALU.max),
             reads=[eq2_b], writes=[m2_b], tag="g_m2")
        p.op("vector", lambda e: e.tensor_scalar(eq2[:], eq2[:], m2[:], None, op0=ALU.is_ge),
             reads=[eq2_b, m2_b], writes=[eq2_b], tag="g_eq2")
        p.op("scalar", lambda e: e.activation(negm[:, 0:1], m1[:], AF.Copy, bias=0.0, scale=-1.0),
             reads=[m1_b], writes=[negm_b], tag="g_neg1")
        p.op("scalar", lambda e: e.activation(negm[:, 1:2], m2[:], AF.Copy, bias=0.0, scale=-1.0),
             reads=[m2_b], writes=[negm_b], tag="g_neg2")
        p.op("scalar", lambda e: e.activation(wt1[:], m1[:], AF.Sigmoid, bias=negm[:, 1:2], scale=1.0),
             reads=[m1_b, negm_b], writes=[wt1_b], tag="g_w1")
        p.op("scalar", lambda e: e.activation(wt2[:], m2[:], AF.Sigmoid, bias=negm[:, 0:1], scale=1.0),
             reads=[m2_b, negm_b], writes=[wt2_b], tag="g_w2")
        p.op("vector", lambda e: e.tensor_scalar(eq1[:], eq1[:], wt1[:], None, op0=ALU.mult),
             reads=[eq1_b, wt1_b], writes=[eq1_b], tag="g_s1")
        p.op("vector", lambda e: e.tensor_scalar(eq2[:], eq2[:], wt2[:], None, op0=ALU.mult),
             reads=[eq2_b, wt2_b], writes=[eq2_b], tag="g_s2")
        p.op("vector", lambda e, i=i: e.tensor_tensor(gates[:, i, :], eq1[:], eq2[:], op=ALU.add),
             reads=[eq1_b, eq2_b], writes=[gates_b], tag="g_gates")
        bank2, bank2_b = ctx.take_bank()
        p.op("tensor", lambda e, i=i, bank2=bank2: e.transpose(
            bank2.ap()[0:8, 0:128], gates[:, i, :], ctx.ident32[:]),
             reads=[gates_b, ctx.ident32_buf], writes=[bank2_b], tag="gT")
        p.op("vector", lambda e, bank2=bank2: e.tensor_copy(gT[:], bank2.ap()[0:8, 0:128]),
             reads=[bank2_b], writes=[gT_b], tag="gT_evac")
        for dh in range(2):
            bank3, bank3_b = ctx.take_bank()
            ctx.matmul_acc(bank3.ap(), bank3_b,
                           [(gT[:], b2r[:, 512 * dh:512 * dh + 512], [gT_b, b2r_b])], tag="b2mm")
            p.op("vector", lambda e, i=i, dh=dh, bank3=bank3: e.tensor_tensor(
                x_t[:, i, 512 * dh:512 * dh + 512], x_t[:, i, 512 * dh:512 * dh + 512], bank3.ap(), op=ALU.add),
                 reads=[bank3_b, x_bufs[i]], writes=[x_bufs[i]], tag="b2add")

    w1h, w1h_b = ctx.sb("w1h", [128, 8, 2048], BF16)
    w2h, w2h_b = ctx.sb("w2h", [128, 16, 1024], BF16)
    hT, hT_b = ctx.sb("hT", [128, 16, 1024], BF16)
    tmp, tmp_b = ctx.sb("l1ftmp", [128, 512])
    for e_i in range(NE):
        for fh in range(2):
            p.dma("gpsimd", lambda e, e_i=e_i, fh=fh: e.dma_start(
                out=w1h[:], in_=din["l1f_w1"][e_i, :, 2048 * fh:2048 * fh + 2048].rearrange("(c p) f -> p c f", p=128)),
                  writes=[w1h_b], tag="ld_w1h")
            p.dma("gpsimd", lambda e, e_i=e_i, fh=fh: e.dma_start(
                out=w2h[:], in_=din["l1f_w2"][e_i, 2048 * fh:2048 * fh + 2048, :].rearrange("(c p) f -> p c f", p=128)),
                  writes=[w2h_b], tag="ld_w2h")
            for th in range(2):
                for fs in range(16):
                    bank, bank_b = ctx.take_bank()
                    for ts in range(2):
                        mm = [(w1h[:, c, 128 * fs:128 * fs + 128],
                               xnT[:, c, 1024 * th + 512 * ts: 1024 * th + 512 * ts + 512],
                               [w1h_b, xnT_buf]) for c in range(8)]
                        ctx.matmul_acc(bank.ap()[:, 0:512], bank_b, mm, tag="moe_h")
                        p.op("scalar", lambda e, fs=fs, ts=ts, e_i=e_i, fh=fh, bank=bank: e.activation(
                            hT[:, fs, 512 * ts:512 * ts + 512], bank.ap()[:, 0:512], AF.Silu,
                            bias=b1c[:, e_i, 16 * fh + fs:16 * fh + fs + 1], scale=1.0),
                             reads=[bank_b, b1c_b], writes=[hT_b], tag="moe_silu")
                for ts in range(8):
                    ti = 8 * th + ts
                    for dh in range(2):
                        bank, bank_b = ctx.take_bank()
                        mm = [(hT[:, fs, 128 * ts:128 * ts + 128],
                               w2h[:, fs, 512 * dh:512 * dh + 512],
                               [hT_b, w2h_b]) for fs in range(16)]
                        ctx.matmul_acc(bank.ap(), bank_b, mm, tag="moe_y")
                        p.op("vector", lambda e, e_i=e_i, ti=ti, bank=bank: e.tensor_scalar(
                            tmp[:], bank.ap(), gates[:, ti, e_i:e_i + 1], None, op0=ALU.mult),
                             reads=[bank_b, gates_b], writes=[tmp_b], tag="moe_scale")
                        p.op("vector", lambda e, ti=ti, dh=dh: e.tensor_tensor(
                            x_t[:, ti, 512 * dh:512 * dh + 512], x_t[:, ti, 512 * dh:512 * dh + 512],
                            tmp[:], op=ALU.add),
                             reads=[tmp_b, x_bufs[ti]], writes=[x_bufs[ti]], tag="moe_add")
    if want_dbg:
        return {"gates": (gates, gates_b), "xnT": (xnT, xnT_buf)}


def emit_L1_attn(ctx, din, x_t, x_bufs):
    p = ctx.prog
    xnT, xnT_buf = ctx.sb("xnT_a", [128, 8, 2048], BF16)
    ln_to_featmajor(ctx, x_t, x_bufs, xnT, xnT_buf)

    cost, cost_b = ctx.sb("ropecos", [128, 16, 32])
    sint, sint_b = ctx.sb("ropesin", [128, 16, 32])
    p.dma("sync", lambda e: e.dma_start(out=cost[:], in_=din["ropecos"].rearrange("(n p) f -> p n f", p=128)),
          writes=[cost_b], tag="ld_cos")
    p.dma("sync", lambda e: e.dma_start(out=sint[:], in_=din["ropesin"].rearrange("(n p) f -> p n f", p=128)),
          writes=[sint_b], tag="ld_sin")
    masks, masks_b = ctx.causal_masks()
    brow, brow_b = ctx.sb("l1a_b", [1, 3 * 1024 + 1024], BF16)
    p.dma("gpsimd", lambda e: e.dma_start(out=brow[:, 0:1024], in_=din["l1a_bq"]), writes=[brow_b], tag="ld_bq")
    p.dma("gpsimd", lambda e: e.dma_start(out=brow[:, 1024:2048], in_=din["l1a_bk"]), writes=[brow_b], tag="ld_bk")
    p.dma("gpsimd", lambda e: e.dma_start(out=brow[:, 2048:3072], in_=din["l1a_bv"]), writes=[brow_b], tag="ld_bv")
    p.dma("gpsimd", lambda e: e.dma_start(out=brow[:, 3072:4096], in_=din["l1a_bp"]), writes=[brow_b], tag="ld_bp")

    wp, wp_b = ctx.sb("wp", [128, 8, 1024], BF16)
    p.dma("gpsimd", lambda e: e.dma_start(out=wp[:], in_=din["l1a_wp"].rearrange("(c p) d -> p c d", p=128)),
          writes=[wp_b], tag="ld_wp")

    ctxT, ctxT_buf = ctx.sb("ctxT", [128, 8, 2048], BF16)

    wqkv = [ctx.sb("wqkv", [128, 8, 384], BF16)]
    q32, q32_b = ctx.sb("q32", [128, 4, 128])
    k32, k32_b = ctx.sb("k32", [128, 4, 128])
    qbf, qbf_b = ctx.sb("qbf", [128, 16, 128], BF16)
    kbf, kbf_b = ctx.sb("kbf", [128, 16, 128], BF16)
    vtok, vtok_b = ctx.sb("vtok", [128, 16, 128], BF16)
    qT, qT_b = ctx.sb("qT", [128, 2048], BF16)
    kT, kT_b = ctx.sb("kT", [128, 2048], BF16)
    srow, srow_b = ctx.sb("srow", [128, 2048])
    pbf, pbf_b = ctx.sb("pbf", [128, 2048], BF16)
    psc, psc_b = pbf, pbf_b
    pts, pts_b = ctx.sb("pts", [128, 128], BF16)
    nmax, nmax_b = ctx.sb("nmaxa", [128, 1])
    rsum, rsum_b = ctx.sb("rsuma", [128, 1])
    ssum, ssum_b = ctx.sb("ssuma", [128, 1])

    t0, t0_b = ctx.sb("ropet0", [128, 4, 32])

    def rope_apply(src32, src_b, dstbf, dst_b, g):
        cs = cost[:, 4 * g:4 * g + 4, :]
        sn = sint[:, 4 * g:4 * g + 4, :]
        for h in range(2):
            v = src32[:].rearrange("p a (h d two) -> p a h d two", h=2, two=2)
            x0 = v[:, :, h, :, 0]
            x1 = v[:, :, h, :, 1]
            dv = dstbf[:, 4 * g:4 * g + 4, :].rearrange("p a (h r d) -> p a h r d", h=2, r=2)
            r0 = dv[:, :, h, 0, :]
            r1 = dv[:, :, h, 1, :]
            p.op("vector", lambda e, x0=x0, cs=cs: e.tensor_tensor(t0[:], x0, cs, op=ALU.mult),
                 reads=[src_b, cost_b], writes=[t0_b], tag="rp0")
            p.op("gpsimd", lambda e, x1=x1, sn=sn, r0=r0: e.tensor_tensor(r0, x1, sn, op=ALU.mult),
                 reads=[src_b, sint_b], writes=[dst_b], tag="rp1")
            p.op("vector", lambda e, r0=r0: e.tensor_tensor(r0, t0[:], r0, op=ALU.subtract),
                 reads=[t0_b, dst_b], writes=[dst_b], tag="rp2")
            p.op("vector", lambda e, x0=x0, sn=sn: e.tensor_tensor(t0[:], x0, sn, op=ALU.mult),
                 reads=[src_b, sint_b], writes=[t0_b], tag="rp3")
            p.op("gpsimd", lambda e, x1=x1, cs=cs, r1=r1: e.tensor_tensor(r1, x1, cs, op=ALU.mult),
                 reads=[src_b, cost_b], writes=[dst_b], tag="rp4")
            p.op("vector", lambda e, r1=r1: e.tensor_tensor(r1, t0[:], r1, op=ALU.add),
                 reads=[t0_b, dst_b], writes=[dst_b], tag="rp5")

    for pr in range(8):
        wt, wt_b = wqkv[0]
        p.dma("gpsimd", lambda e, pr=pr: e.dma_start(
            out=wt[:, :, 0:128], in_=din["l1a_wq"][:, 128 * pr:128 * pr + 128].rearrange("(c p) d -> p c d", p=128)),
              writes=[wt_b], tag="ld_wq")
        p.dma("gpsimd", lambda e, pr=pr: e.dma_start(
            out=wt[:, :, 128:256], in_=din["l1a_wk"][:, 128 * pr:128 * pr + 128].rearrange("(c p) d -> p c d", p=128)),
              writes=[wt_b], tag="ld_wk")
        p.dma("gpsimd", lambda e, pr=pr: e.dma_start(
            out=wt[:, :, 256:384], in_=din["l1a_wv"][:, 128 * pr:128 * pr + 128].rearrange("(c p) d -> p c d", p=128)),
              writes=[wt_b], tag="ld_wv")
        for g in range(4):
            bq_ = [ctx.take_bank() for _ in range(2)]
            bk_ = [ctx.take_bank() for _ in range(2)]
            for j in range(4):
                ti = 4 * g + j
                bnq, bnq_b = bq_[j % 2]
                bnk, bnk_b = bk_[j % 2]
                mmq = [(xnT[:, c, 128 * ti:128 * ti + 128], wt[:, c, 0:128], [xnT_buf, wt_b]) for c in range(8)]
                mmq.append((ctx.ones_row_bf[0:1, :], brow[:, 128 * pr:128 * pr + 128], [ctx.onesrbf_buf, brow_b]))
                ctx.matmul_acc(bnq.ap()[:, 0:128], bnq_b, mmq, tag="qmm")
                p.op("vector", lambda e, j=j, bnq=bnq: e.tensor_copy(q32[:, j, :], bnq.ap()[:, 0:128]),
                     reads=[bnq_b], writes=[q32_b], tag="qevac")
                mmk = [(xnT[:, c, 128 * ti:128 * ti + 128], wt[:, c, 128:256], [xnT_buf, wt_b]) for c in range(8)]
                mmk.append((ctx.ones_row_bf[0:1, :], brow[:, 1024 + 128 * pr:1024 + 128 * pr + 128], [ctx.onesrbf_buf, brow_b]))
                ctx.matmul_acc(bnk.ap()[:, 0:128], bnk_b, mmk, tag="kmm")
                p.op("vector", lambda e, j=j, bnk=bnk: e.tensor_copy(k32[:, j, :], bnk.ap()[:, 0:128]),
                     reads=[bnk_b], writes=[k32_b], tag="kevac")
                bnv, bnv_b = ctx.take_bank()
                mmv = [(xnT[:, c, 128 * ti:128 * ti + 128], wt[:, c, 256:384], [xnT_buf, wt_b]) for c in range(8)]
                mmv.append((ctx.ones_row_bf[0:1, :], brow[:, 2048 + 128 * pr:2048 + 128 * pr + 128], [ctx.onesrbf_buf, brow_b]))
                ctx.matmul_acc(bnv.ap()[:, 0:128], bnv_b, mmv, tag="vmm")
                p.op("scalar", lambda e, ti=ti, bnv=bnv: e.activation(
                    vtok[:, ti, :], bnv.ap()[:, 0:128], AF.Copy, bias=0.0, scale=1.0),
                     reads=[bnv_b], writes=[vtok_b], tag="vevac")
            rope_apply(q32, q32_b, qbf, qbf_b, g)
            rope_apply(k32, k32_b, kbf, kbf_b, g)
        for src, sb_, dstT, dT_b in ((qbf, qbf_b, qT, qT_b), (kbf, kbf_b, kT, kT_b)):
            for g in range(4):
                bank, bank_b = ctx.take_bank()
                bview = bank.ap().bitcast(BF16)
                for j in range(4):
                    p.op("tensor", lambda e, g=g, j=j, src=src, bview=bview: e.transpose(
                        bview[:, 128 * j:128 * j + 128], src[:, 4 * g + j, :], ctx.ident[:]),
                         reads=[sb_, ctx.ident_buf], writes=[bank_b], tag="qkxp")
                p.op("vector", lambda e, g=g, dstT=dstT, bview=bview: e.tensor_copy(
                    dstT[:, 512 * g:512 * g + 512].rearrange("p (a b) -> p a b", b=128),
                    bview[:, 0:512].rearrange("p (a b) -> p a b", b=128)),
                     reads=[bank_b], writes=[dT_b], tag="qkxpe")
        for h in range(2):
            hb = 64 * h
            ch = pr
            for qi in range(NTILES):
                nk5 = qi // 4 + 1
                sb_banks = [ctx.take_bank() for _ in range(2)]
                for kj in range(nk5):
                    w = min(512, (qi + 1) * 128 - 512 * kj)
                    bank, bank_b = sb_banks[kj % 2]
                    p.op("tensor", lambda e, qi=qi, kj=kj, w=w, bank=bank, hb=hb: e.matmul(
                        bank.ap()[:, 0:w], qT[hb:hb + 64, 128 * qi:128 * qi + 128],
                        kT[hb:hb + 64, 512 * kj:512 * kj + w], start=True, stop=True),
                         reads=[qT_b, kT_b], writes=[bank_b], tag="smm")
                    if w < 512 or 512 * kj + w > 128 * qi:
                        p.op("vector", lambda e, qi=qi, kj=kj, w=w, bank=bank: e.tensor_tensor(
                            srow[:, 512 * kj:512 * kj + w], bank.ap()[:, 0:w],
                            masks[:, qi % 4, 0:w], op=ALU.add),
                             reads=[bank_b, masks_b], writes=[srow_b], tag="smask")
                    else:
                        p.op("vector", lambda e, kj=kj, w=w, bank=bank: e.tensor_copy(
                            srow[:, 512 * kj:512 * kj + w], bank.ap()[:, 0:w]),
                             reads=[bank_b], writes=[srow_b], tag="sevac")
                W = (qi + 1) * 128
                p.op("vector", lambda e, W=W: e.tensor_reduce(
                    nmax[:], srow[:, 0:W], axis=AX.X, op=ALU.max, negate=True),
                     reads=[srow_b], writes=[nmax_b], tag="smax")
                p.op("vector", lambda e: e.tensor_scalar(
                    nmax[:], nmax[:], 0.125, None, op0=ALU.mult),
                     reads=[nmax_b], writes=[nmax_b], tag="smaxsc")
                p.op("scalar", lambda e, W=W: e.activation(
                    pbf[:, 0:W], srow[:, 0:W], AF.Exp, bias=nmax[:], scale=0.125, accum_out=ssum[:]),
                     reads=[srow_b, nmax_b], writes=[pbf_b, ssum_b], tag="sexp")
                p.op("vector", lambda e: e.reciprocal(rsum[:], ssum[:]),
                     reads=[ssum_b], writes=[rsum_b], tag="srec")
                p.op("vector", lambda e, W=W: e.tensor_scalar(
                    pbf[:, 0:W], pbf[:, 0:W], rsum[:], None, op0=ALU.mult),
                     reads=[pbf_b, rsum_b], writes=[pbf_b], tag="pscale")
                cbank, cbank_b, cbank_i = ctx.take_bank(hold=True)
                for kj in range(qi + 1):
                    tbank, tbank_b = ctx.take_bank()
                    tb = tbank.ap().bitcast(BF16)
                    p.op("tensor", lambda e, kj=kj, tb=tb: e.transpose(
                        tb[:, 0:128], psc[:, 128 * kj:128 * kj + 128], ctx.ident[:]),
                         reads=[psc_b, ctx.ident_buf], writes=[tbank_b], tag="pxp")
                    p.op("vector", lambda e, tb=tb: e.tensor_copy(pts[:], tb[:, 0:128]),
                         reads=[tbank_b], writes=[pts_b], tag="pxpe")
                    p.op("tensor", lambda e, kj=kj, qi=qi, hb=hb, cbank=cbank: e.matmul(
                        cbank.ap()[0:64, 0:128], vtok[:, kj, hb:hb + 64], pts[:],
                        start=(kj == 0), stop=(kj == qi)),
                         reads=[pts_b, vtok_b], writes=[cbank_b], tag="pvmm")
                p.op("vector", lambda e, qi=qi, hb=hb, ch=ch, cbank=cbank: e.tensor_copy(
                    ctxT[hb:hb + 64, ch, 128 * qi:128 * qi + 128], cbank.ap()[0:64, 0:128]),
                     reads=[cbank_b], writes=[ctxT_buf], tag="ctxevac")
                ctx.release_bank(cbank_i)

    for ti in range(NTILES):
        for dh in range(2):
            bank, bank_b = ctx.take_bank()
            mm = [(ctxT[:, c, 128 * ti:128 * ti + 128], wp[:, c, 512 * dh:512 * dh + 512],
                   [ctxT_buf, wp_b]) for c in range(8)]
            mm.append((ctx.ones_row_bf[0:1, :], brow[:, 3072 + 512 * dh:3072 + 512 * dh + 512],
                       [ctx.onesrbf_buf, brow_b]))
            ctx.matmul_acc(bank.ap(), bank_b, mm, tag="wpmm")
            p.op("vector", lambda e, ti=ti, dh=dh, bank=bank: e.tensor_tensor(
                x_t[:, ti, 512 * dh:512 * dh + 512], x_t[:, ti, 512 * dh:512 * dh + 512],
                bank.ap(), op=ALU.add),
                 reads=[bank_b, x_bufs[ti]], writes=[x_bufs[ti]], tag="wpadd")


def emit_L0_attn(ctx, din, x_t, x_bufs, dbg_head=None, dout=None, nheads=16, do_attn=True, do_kv=True, do_q=True, do_rms=True):
    p = ctx.prog
    EPS0 = 1.1920929e-07
    xnT, xnT_buf = ctx.sb("xnT_m", [128, 8, 2048], BF16)
    ln_to_featmajor(ctx, x_t, x_bufs, xnT, xnT_buf)
    masks, masks_b = ctx.causal_masks()
    epsc = ctx.const_col(EPS0)

    wdqT, wdqT_b = ctx.sb("wdqT", [128, 512], BF16)
    wuqT, wuqT_b = ctx.sb("wuqT", [128, 4, 64], BF16)
    wdkvT, wdkvT_b = ctx.sb("wdkvT", [128, 1024], BF16)
    wukvT, wukvT_b = ctx.sb("wukvT", [128, 8, 128], BF16)
    wqrT, wqrT_b = ctx.sb("wqrT", [64, 64], BF16)
    wkrT, wkrT_b = ctx.sb("wkrT", [128, 64], BF16)
    woT, woT_b = ctx.sb("woT", [64, 64], BF16)
    nw, nw_b = ctx.sb("normw", [64, 2])
    p.dma("gpsimd", lambda e: e.dma_start(out=wdqT[0:64, :], in_=din["l0a_wdqT"]), writes=[wdqT_b], tag="ldw")
    p.dma("gpsimd", lambda e: e.dma_start(out=wdqT[64:128, :], in_=din["l0a_wdqT"]), writes=[wdqT_b], tag="ldw")
    p.dma("gpsimd", lambda e: e.dma_start(out=wuqT[:], in_=din["l0a_wuqT"].rearrange("(c p) d -> p c d", p=128)), writes=[wuqT_b], tag="ldw")
    p.dma("gpsimd", lambda e: e.dma_start(out=wdkvT[0:64, :], in_=din["l0a_wdkvT"]), writes=[wdkvT_b], tag="ldw")
    p.dma("gpsimd", lambda e: e.dma_start(out=wdkvT[64:128, :], in_=din["l0a_wdkvT"]), writes=[wdkvT_b], tag="ldw")
    p.dma("gpsimd", lambda e: e.dma_start(out=wukvT[:], in_=din["l0a_wukvT"].rearrange("(c p) d -> p c d", p=128)), writes=[wukvT_b], tag="ldw")
    p.dma("gpsimd", lambda e: e.dma_start(out=wqrT[:], in_=din["l0a_wqrT"]), writes=[wqrT_b], tag="ldw")
    p.dma("gpsimd", lambda e: e.dma_start(out=wkrT[0:64, :], in_=din["l0a_wkrT"]), writes=[wkrT_b], tag="ldw")
    p.dma("gpsimd", lambda e: e.dma_start(out=wkrT[64:128, :], in_=din["l0a_wkrT"]), writes=[wkrT_b], tag="ldw")
    p.dma("gpsimd", lambda e: e.dma_start(out=woT[:], in_=din["l0a_woT"]), writes=[woT_b], tag="ldw")
    p.dma("sync", lambda e: e.dma_start(out=nw[:], in_=din["l0a_normw"]), writes=[nw_b], tag="ldnw")

    with ctx.scope("l0a"):
        cqT, cqT_b = ctx.sb("cqT", [128, 4, 2048], BF16)
        ckvT, ckvT_b = ctx.sb("ckvT", [128, 8, 2048], BF16)
        QT, QT_b = ctx.sb("QT0", [128, 2048], BF16)
        KT, KT_b = ctx.sb("KT0", [128, 2048], BF16)
        vtok, vtok_b = ctx.sb("vtok0", [128, 16, 64], BF16)
        vfs, vfs_b = ctx.sb("vfs", [64, 512], BF16)
        sqs, sqs_b = ctx.sb("sqs", [64, 512])
        qcs, qcs_b = ctx.sb("qcs", [64, 512])
        qcn32, qcn32_b = ctx.sb("qcn32", [64, 512])
        rrow, rrow_b = ctx.sb("rrow", [1, 512])
        srow, srow_b = ctx.sb("srow0", [128, 2048])
        pbf, pbf_b = ctx.sb("pbf0", [128, 2048], BF16)
        pts, pts_b = ctx.sb("pts0", [128, 128], BF16)
        ctxbf, ctxbf_b = ctx.sb("ctxbf", [64, 128], BF16)
        nmax, nmax_b = ctx.sb("nmax0", [128, 1])
        rsum, rsum_b = ctx.sb("rsum0", [128, 1])
        ssum, ssum_b = ctx.sb("ssum0", [128, 1])
        otmp, otmp_b = ctx.sb("otmp", [128, 64])

        import os as _os
        RMSLVL = int(_os.environ.get("RMSLVL", "7"))

        def rms_chain(psum_ap, psum_b, rows, out_ap, out_b, w_col, tq):
            p.op("vector", lambda e: e.tensor_copy(qcs[:], psum_ap),
                 reads=[psum_b], writes=[qcs_b], tag="rms_cp")
            p.op("scalar", lambda e: e.activation(sqs[:], qcs[:], AF.Square, bias=0.0, scale=1.0),
                 reads=[qcs_b], writes=[sqs_b], tag="rms_sq")
            if RMSLVL < 2:
                p.op("vector", lambda e, out_ap=out_ap: e.tensor_copy(out_ap, qcs[:]),
                     reads=[qcs_b], writes=[out_b], tag="rms_skip")
                return
            bsum, bsum_b = ctx.take_bank()
            p.op("tensor", lambda e, bsum=bsum: e.matmul(
                bsum.ap()[0:1, :], ctx.ones_col[0:64, :], sqs[:], start=True, stop=True),
                 reads=[sqs_b, ctx.ones_buf], writes=[bsum_b], tag="rms_sum")
            if RMSLVL < 3:
                p.op("vector", lambda e, out_ap=out_ap: e.tensor_copy(out_ap, qcs[:]),
                     reads=[qcs_b], writes=[out_b], tag="rms_skip")
                return
            p.op("scalar", lambda e, bsum=bsum: e.activation(
                rrow[:], bsum.ap()[0:1, :], AF.Ln, bias=epsc[0:1, :], scale=1.0 / 64),
                 reads=[bsum_b, ctx.consts_buf], writes=[rrow_b], tag="rms_ln")
            if RMSLVL < 4:
                p.op("vector", lambda e, out_ap=out_ap: e.tensor_copy(out_ap, qcs[:]),
                     reads=[qcs_b], writes=[out_b], tag="rms_skip")
                return
            p.op("scalar", lambda e: e.activation(rrow[:], rrow[:], AF.Exp, bias=0.0, scale=-0.5),
                 reads=[rrow_b], writes=[rrow_b], tag="rms_exp")
            bbc, bbc_b = ctx.take_bank()
            if RMSLVL < 5:
                p.op("vector", lambda e, out_ap=out_ap: e.tensor_copy(out_ap, qcs[:]),
                     reads=[qcs_b], writes=[out_b], tag="rms_skip")
                return
            p.op("tensor", lambda e, bbc=bbc: e.matmul(
                bbc.ap()[0:64, :], ctx.ones_row[0:1, 0:64], rrow[:], start=True, stop=True),
                 reads=[rrow_b, ctx.onesr_buf], writes=[bbc_b], tag="rms_bc")
            if RMSLVL < 6:
                p.op("vector", lambda e, out_ap=out_ap: e.tensor_copy(out_ap, qcs[:]),
                     reads=[qcs_b], writes=[out_b], tag="rms_skip")
                return
            p.op("vector", lambda e, bbc=bbc: e.tensor_tensor(qcn32[:], qcs[:], bbc.ap()[0:64, :], op=ALU.mult),
                 reads=[qcs_b, bbc_b], writes=[qcn32_b], tag="rms_mul")
            p.op("vector", lambda e, w_col=w_col, out_ap=out_ap: e.tensor_scalar(
                out_ap, qcn32[:], w_col, None, op0=ALU.mult),
                 reads=[qcn32_b, nw_b], writes=[out_b], tag="rms_w")

        for h in range(nheads):
            hc = h // 2
            hr = (h % 2) * 64
            xhT = lambda tq, w=512: xnT[hr:hr + 64, hc, 512 * tq:512 * tq + w]
            for lc in range(4):
                for tq in range(4):
                    bank, bank_b = ctx.take_bank()
                    p.op("tensor", lambda e, lc=lc, bank=bank, hr=hr, xh=xhT(tq): e.matmul(
                        bank.ap(), wdqT[hr:hr + 64, 128 * lc:128 * lc + 128], xh,
                        start=True, stop=True, tile_position=(hr, 0)),
                         reads=[wdqT_b, xnT_buf], writes=[bank_b], tag="cq_mm")
                    p.op("vector", lambda e, lc=lc, tq=tq, bank=bank: e.tensor_copy(
                        cqT[:, lc, 512 * tq:512 * tq + 512], bank.ap()),
                         reads=[bank_b], writes=[cqT_b], tag="cq_ev")
            for tq in range(4 if do_q else 0):
                bank, bank_b = ctx.take_bank()
                mm = [(wuqT[:, lc, :], cqT[:, lc, 512 * tq:512 * tq + 512], [wuqT_b, cqT_b]) for lc in range(4)]
                ctx.matmul_acc(bank.ap()[0:64, :], bank_b, mm, tag="qc_mm")
                if do_rms:
                    rms_chain(bank.ap()[0:64, :], bank_b, 64, QT[0:64, 512 * tq:512 * tq + 512], QT_b, nw[:, 0:1], tq)
                else:
                    p.op("vector", lambda e, tq=tq, bank=bank: e.tensor_copy(
                        QT[0:64, 512 * tq:512 * tq + 512], bank.ap()[0:64, :]),
                         reads=[bank_b], writes=[QT_b], tag="qc_ev")
                bqr, bqr_b = ctx.take_bank()
                p.op("tensor", lambda e, tq=tq, bqr=bqr: e.matmul(
                    bqr.ap()[0:64, :], wqrT[:], QT[0:64, 512 * tq:512 * tq + 512], start=True, stop=True),
                     reads=[wqrT_b, QT_b], writes=[bqr_b], tag="qr_mm")
                p.op("vector", lambda e, tq=tq, bqr=bqr: e.tensor_copy(
                    QT[64:128, 512 * tq:512 * tq + 512], bqr.ap()[0:64, :]),
                     reads=[bqr_b], writes=[QT_b], tag="qr_ev")
            for lc in range(8 if do_kv else 0):
                for tq in range(4):
                    bank, bank_b = ctx.take_bank()
                    p.op("tensor", lambda e, lc=lc, bank=bank, hr=hr, xh=xhT(tq): e.matmul(
                        bank.ap(), wdkvT[hr:hr + 64, 128 * lc:128 * lc + 128], xh,
                        start=True, stop=True, tile_position=(hr, 0)),
                         reads=[wdkvT_b, xnT_buf], writes=[bank_b], tag="ckv_mm")
                    p.op("vector", lambda e, lc=lc, tq=tq, bank=bank: e.tensor_copy(
                        ckvT[:, lc, 512 * tq:512 * tq + 512], bank.ap()),
                         reads=[bank_b], writes=[ckvT_b], tag="ckv_ev")
            for tq in range(4 if do_kv else 0):
                bank, bank_b, bank_i = ctx.take_bank(hold=True)
                mm = [(wukvT[:, lc, :], ckvT[:, lc, 512 * tq:512 * tq + 512], [wukvT_b, ckvT_b]) for lc in range(8)]
                ctx.matmul_acc(bank.ap(), bank_b, mm, tag="kv_mm")
                rms_chain(bank.ap()[0:64, :], bank_b, 64, KT[0:64, 512 * tq:512 * tq + 512], KT_b, nw[:, 1:2], tq)
                p.op("scalar", lambda e, tq=tq, bank=bank: e.activation(
                    vfs[:], bank.ap()[64:128, :], AF.Copy, bias=0.0, scale=1.0),
                     reads=[bank_b], writes=[vfs_b], tag="v_ev")
                for j in range(4):
                    bx, bx_b = ctx.take_bank()
                    bxv = bx.ap().bitcast(BF16)
                    p.op("tensor", lambda e, j=j, bxv=bxv: e.transpose(
                        bxv[:, 0:64], vfs[:, 128 * j:128 * j + 128], ctx.ident[0:64, 0:64]),
                         reads=[vfs_b, ctx.ident_buf], writes=[bx_b], tag="v_xp")
                    p.op("vector", lambda e, tq=tq, j=j, bxv=bxv: e.tensor_copy(
                        vtok[:, 4 * tq + j, :], bxv[:, 0:64]),
                         reads=[bx_b], writes=[vtok_b], tag="v_xpe")
                bkr, bkr_b = ctx.take_bank()
                p.op("tensor", lambda e, bkr=bkr, hr=hr, xh=xhT(tq): e.matmul(
                    bkr.ap()[0:64, :], wkrT[hr:hr + 64, :], xh,
                    start=True, stop=True, tile_position=(hr, 0)),
                     reads=[wkrT_b, xnT_buf], writes=[bkr_b], tag="kr_mm")
                p.op("vector", lambda e, tq=tq, bkr=bkr: e.tensor_copy(
                    KT[64:128, 512 * tq:512 * tq + 512], bkr.ap()[0:64, :]),
                     reads=[bkr_b], writes=[KT_b], tag="kr_ev")
                ctx.release_bank(bank_i)
            if dbg_head is not None and h == dbg_head:
                p.dma("gpsimd", lambda e: e.dma_start(out=dout["dbgQT"], in_=QT[:]), reads=[QT_b], tag="dbgQT")
                p.dma("gpsimd", lambda e: e.dma_start(out=dout["dbgKT"], in_=KT[:]), reads=[KT_b], tag="dbgKT")
                p.dma("gpsimd", lambda e: e.dma_start(out=dout["dbgV"], in_=vtok[:].rearrange("p a b -> p (a b)")), reads=[vtok_b], tag="dbgV")
                p.dma("gpsimd", lambda e: e.dma_start(out=dout["dbgcq"], in_=cqT[:, 0, :]), reads=[cqT_b], tag="dbgcq")
            for qi in range(NTILES if do_attn else 0):
                nk5 = qi // 4 + 1
                sb_banks = [ctx.take_bank() for _ in range(2)]
                for kj in range(nk5):
                    w = min(512, (qi + 1) * 128 - 512 * kj)
                    bank, bank_b = sb_banks[kj % 2]
                    p.op("tensor", lambda e, qi=qi, kj=kj, w=w, bank=bank: e.matmul(
                        bank.ap()[:, 0:w], QT[:, 128 * qi:128 * qi + 128],
                        KT[:, 512 * kj:512 * kj + w], start=True, stop=True),
                         reads=[QT_b, KT_b], writes=[bank_b], tag="smm0")
                    if 512 * kj + w > 128 * qi:
                        p.op("vector", lambda e, kj=kj, w=w, qi=qi, bank=bank: e.tensor_tensor(
                            srow[:, 512 * kj:512 * kj + w], bank.ap()[:, 0:w],
                            masks[:, qi % 4, 0:w], op=ALU.add),
                             reads=[bank_b, masks_b], writes=[srow_b], tag="smask0")
                    else:
                        p.op("vector", lambda e, kj=kj, w=w, bank=bank: e.tensor_copy(
                            srow[:, 512 * kj:512 * kj + w], bank.ap()[:, 0:w]),
                             reads=[bank_b], writes=[srow_b], tag="sevac0")
                W = (qi + 1) * 128
                p.op("vector", lambda e, W=W: e.tensor_reduce(
                    nmax[:], srow[:, 0:W], axis=AX.X, op=ALU.max, negate=True),
                     reads=[srow_b], writes=[nmax_b], tag="smax0")
                p.op("vector", lambda e: e.tensor_scalar(
                    nmax[:], nmax[:], 0.125, None, op0=ALU.mult),
                     reads=[nmax_b], writes=[nmax_b], tag="smaxsc0")
                p.op("scalar", lambda e, W=W: e.activation(
                    pbf[:, 0:W], srow[:, 0:W], AF.Exp, bias=nmax[:], scale=0.125, accum_out=ssum[:]),
                     reads=[srow_b, nmax_b], writes=[pbf_b, ssum_b], tag="sexp0")
                p.op("vector", lambda e: e.reciprocal(rsum[:], ssum[:]),
                     reads=[ssum_b], writes=[rsum_b], tag="srec0")
                cbank, cbank_b, cbank_i = ctx.take_bank(hold=True)
                for kj in range(qi + 1):
                    tbank, tbank_b = ctx.take_bank()
                    tb = tbank.ap().bitcast(BF16)
                    p.op("tensor", lambda e, kj=kj, tb=tb: e.transpose(
                        tb[:, 0:128], pbf[:, 128 * kj:128 * kj + 128], ctx.ident[:]),
                         reads=[pbf_b, ctx.ident_buf], writes=[tbank_b], tag="pxp0")
                    p.op("vector", lambda e, tb=tb: e.tensor_copy(pts[:], tb[:, 0:128]),
                         reads=[tbank_b], writes=[pts_b], tag="pxpe0")
                    p.op("tensor", lambda e, kj=kj, qi=qi, cbank=cbank: e.matmul(
                        cbank.ap()[0:64, 0:128], vtok[:, kj, :], pts[:],
                        start=(kj == 0), stop=(kj == qi)),
                         reads=[pts_b, vtok_b], writes=[cbank_b], tag="pvmm0")
                p.op("vector", lambda e, cbank=cbank: e.tensor_copy(ctxbf[:], cbank.ap()[0:64, 0:128]),
                     reads=[cbank_b], writes=[ctxbf_b], tag="ctxev0")
                ctx.release_bank(cbank_i)
                obank, obank_b = ctx.take_bank()
                p.op("tensor", lambda e, obank=obank: e.matmul(
                    obank.ap()[:, 0:64], ctxbf[:], woT[:], start=True, stop=True),
                     reads=[ctxbf_b, woT_b], writes=[obank_b], tag="owmm")
                p.op("vector", lambda e, obank=obank: e.tensor_scalar(
                    otmp[:], obank.ap()[:, 0:64], rsum[:], None, op0=ALU.mult),
                     reads=[obank_b, rsum_b], writes=[otmp_b], tag="oscale")
                p.op("vector", lambda e, qi=qi, h=h: e.tensor_tensor(
                    x_t[:, qi, 64 * h:64 * h + 64], x_t[:, qi, 64 * h:64 * h + 64], otmp[:], op=ALU.add),
                     reads=[otmp_b, x_bufs[qi]], writes=[x_bufs[qi]], tag="oadd")


def emit_L0_ffn(ctx, din, x_t, x_bufs):
    p = ctx.prog
    xnT, xnT_buf = ctx.sb("xnT_f0", [128, 8, 2048], BF16)
    ln_to_featmajor(ctx, x_t, x_bufs, xnT, xnT_buf)

    g1p, g1p_b = ctx.sb("g1p", [8, 2, 4, 4, 32])
    g2p, g2p_b = ctx.sb("g2p", [16, 2, 4, 16])
    br, br_b = ctx.sb("l0f_b", [1, 1024])
    sel8, sel8_b = ctx.sb("sel8", [8, 128])
    sel16, sel16_b = ctx.sb("sel16", [16, 128])
    p.dma("sync", lambda e: e.dma_start(out=g1p[:], in_=din["l0f_g1p"]), writes=[g1p_b], tag="ldg")
    p.dma("sync", lambda e: e.dma_start(out=g2p[:], in_=din["l0f_g2p"]), writes=[g2p_b], tag="ldg")
    p.dma("sync", lambda e: e.dma_start(out=br[:, 0:512], in_=din["l0f_br"]), writes=[br_b], tag="ldb")
    p.dma("sync", lambda e: e.dma_start(out=br[:, 512:1024], in_=din["l0f_bi"]), writes=[br_b], tag="ldb")
    p.dma("sync", lambda e: e.dma_start(out=sel8[:], in_=din["l0f_sel8"]), writes=[sel8_b], tag="lds")
    p.dma("sync", lambda e: e.dma_start(out=sel16[:], in_=din["l0f_sel16"]), writes=[sel16_b], tag="lds")

    with ctx.scope("l0f"):
        WT = {}
        rows8, rows8_b = ctx.sb("rows8", [8, 32, 16])
        rows16, rows16_b = ctx.sb("rows16", [16, 32, 16])
        asb, asb_b = ctx.sb("asb", [128, 512])
        prod, prod_b = ctx.sb("prodf0", [128, 512])
        prod2, prod2_b = ctx.sb("prodf02", [128, 512])
        for ri, nm in ((0, "R"), (1, "I")):
            wt, wt_b = ctx.sb(f"WT{nm}", [128, 4, 512], BF16)
            WT[nm] = (wt, wt_b)
            for k in range(4):
                for c in range(4):
                    p.op("gpsimd", lambda e, ri=ri, c=c, k=k: e.tensor_copy(
                        rows8[:], g1p[:, ri, c, k, :].unsqueeze(2).broadcast_to((8, 32, 16))),
                         reads=[g1p_b], writes=[rows8_b], tag="rows8")
                    p.op("gpsimd", lambda e, ri=ri, c=c: e.tensor_copy(
                        rows16[:], g2p[:, ri, c, :].unsqueeze(1).broadcast_to((16, 32, 16))),
                         reads=[g2p_b], writes=[rows16_b], tag="rows16")
                    ba, ba_b = ctx.take_bank()
                    p.op("tensor", lambda e, ba=ba: e.matmul(
                        ba.ap(), sel8[:], rows8[:].rearrange("p a b -> p (a b)"), start=True, stop=True),
                         reads=[sel8_b, rows8_b], writes=[ba_b], tag="amm")
                    bb, bb_b = ctx.take_bank()
                    p.op("tensor", lambda e, bb=bb: e.matmul(
                        bb.ap(), sel16[:], rows16[:].rearrange("p a b -> p (a b)"), start=True, stop=True),
                         reads=[sel16_b, rows16_b], writes=[bb_b], tag="bmm")
                    p.op("vector", lambda e, ba=ba: e.tensor_copy(asb[:], ba.ap()),
                         reads=[ba_b], writes=[asb_b], tag="aev")
                    if c == 0:
                        p.op("vector", lambda e, bb=bb: e.tensor_tensor(prod[:], asb[:], bb.ap(), op=ALU.mult),
                             reads=[asb_b, bb_b], writes=[prod_b], tag="wmul")
                    else:
                        p.op("vector", lambda e, bb=bb: e.tensor_tensor(prod2[:], asb[:], bb.ap(), op=ALU.mult),
                             reads=[asb_b, bb_b], writes=[prod2_b], tag="wmul2")
                        p.op("vector", lambda e: e.tensor_tensor(prod[:], prod[:], prod2[:], op=ALU.add),
                             reads=[prod2_b, prod_b], writes=[prod_b], tag="wacc")
                p.op("vector", lambda e, wt=wt, k=k: e.tensor_copy(wt[:, k, :], prod[:]),
                     reads=[prod_b], writes=[wt_b], tag="wstore")
        wtn, wtn_b = ctx.sb("WTIn", [128, 4, 512], BF16)
        for k in range(4):
            p.op("vector", lambda e, k=k: e.tensor_scalar(
                wtn[:, k, :], WT["I"][0][:, k, :], -1.0, None, op0=ALU.mult),
                 reads=[WT["I"][1]], writes=[wtn_b], tag="wneg")

        wr, wr_b = WT["R"]
        wi, wi_b = WT["I"]
        for ti in range(NTILES):
            for half in range(2):
                bank, bank_b = ctx.take_bank()
                if half == 0:
                    mm = [(xnT[:, c, 128 * ti:128 * ti + 128], wr[:, c, :], [xnT_buf, wr_b]) for c in range(4)]
                    mm += [(xnT[:, 4 + c, 128 * ti:128 * ti + 128], wtn[:, c, :], [xnT_buf, wtn_b]) for c in range(4)]
                else:
                    mm = [(xnT[:, c, 128 * ti:128 * ti + 128], wi[:, c, :], [xnT_buf, wi_b]) for c in range(4)]
                    mm += [(xnT[:, 4 + c, 128 * ti:128 * ti + 128], wr[:, c, :], [xnT_buf, wr_b]) for c in range(4)]
                mm.append((ctx.ones_row[0:1, :], br[:, 512 * half:512 * half + 512], [ctx.onesr_buf, br_b]))
                ctx.matmul_acc(bank.ap(), bank_b, mm, tag="f0mm")
                p.op("vector", lambda e, ti=ti, half=half, bank=bank: e.tensor_tensor(
                    x_t[:, ti, 512 * half:512 * half + 512], x_t[:, ti, 512 * half:512 * half + 512],
                    bank.ap(), op=ALU.add),
                     reads=[bank_b, x_bufs[ti]], writes=[x_bufs[ti]], tag="f0add")


def emit_L2_attn(ctx, din, x_t, x_bufs):
    p = ctx.prog
    xnT, xnT_buf = ctx.sb("xnT_2a", [128, 8, 2048], BF16)
    ln_to_featmajor(ctx, x_t, x_bufs, xnT, xnT_buf)

    bvo, bvo_b = ctx.sb("bvo", [1, 2048], BF16)
    p.dma("gpsimd", lambda e: e.dma_start(out=bvo[:, 0:1024], in_=din["l2a_bv"]), writes=[bvo_b], tag="ldbv")
    p.dma("gpsimd", lambda e: e.dma_start(out=bvo[:, 1024:2048], in_=din["l2a_bo"]), writes=[bvo_b], tag="ldbo")
    ctxT, ctxT_buf = ctx.sb("ctxT2", [128, 8, 2048], BF16)

    with ctx.scope("l2a"):
        wvT, wvT_b = ctx.sb("wvT2", [128, 8, 1024], BF16)
        p.dma("gpsimd", lambda e: e.dma_start(out=wvT[:], in_=din["l2a_wvT"].rearrange("(c p) d -> p c d", p=128)), writes=[wvT_b], tag="ldwv")
        Tb, Tb_b = ctx.sb("trilT", [128, 16, 64], BF16)
        p.op("gpsimd", lambda e: e.memset(Tb[:], 0.0), writes=[Tb_b], tag="T0")
        for i in range(64):
            a = i * (i + 1) // 2
            p.dma("gpsimd", lambda e, i=i, a=a: e.dma_start(
                out=Tb[i:i + 1, :, 0:i + 1], in_=din["l2a_params"][:, a:a + i + 1].unsqueeze(0),
                allow_slow_non_contiguous=True),
                  writes=[Tb_b], tag="Trow")
            p.dma("gpsimd", lambda e, i=i, a=a: e.dma_start(
                out=Tb[64 + i:64 + i + 1, :, 0:i + 1], in_=din["l2a_params"][:, a:a + i + 1].unsqueeze(0),
                allow_slow_non_contiguous=True),
                  writes=[Tb_b], tag="Trow")
        vtok, vtok_b = ctx.sb("vtok2", [128, 16, 1024], BF16)
        for ti in range(NTILES):
            for dh in range(2):
                bank, bank_b = ctx.take_bank()
                mm = [(xnT[:, c, 128 * ti:128 * ti + 128], wvT[:, c, 512 * dh:512 * dh + 512],
                       [xnT_buf, wvT_b]) for c in range(8)]
                mm.append((ctx.ones_row_bf[0:1, :], bvo[:, 512 * dh:512 * dh + 512], [ctx.onesrbf_buf, bvo_b]))
                ctx.matmul_acc(bank.ap(), bank_b, mm, tag="vmm2")
                p.op("scalar", lambda e, ti=ti, dh=dh, bank=bank: e.activation(
                    vtok[:, ti, 512 * dh:512 * dh + 512], bank.ap(), AF.Copy, bias=0.0, scale=1.0),
                     reads=[bank_b], writes=[vtok_b], tag="vev2")
        uT, uT_b = ctx.sb("uT2", [64, 2048], BF16)
        srow, srow_b = ctx.sb("srow2", [128, 2048])
        pbf, pbf_b = ctx.sb("pbf2", [128, 2048], BF16)
        pts, pts_b = ctx.sb("pts2", [128, 128], BF16)
        nmax, nmax_b = ctx.sb("nmax2", [128, 1])
        rsum, rsum_b = ctx.sb("rsum2", [128, 1])
        ssum, ssum_b = ctx.sb("ssum2", [128, 1])
        for h in range(16):
            hc = h // 2
            hr = (h % 2) * 64
            for tq in range(4):
                bank, bank_b = ctx.take_bank()
                p.op("tensor", lambda e, bank=bank, hr=hr, h=h,
                     xh=xnT[hr:hr + 64, hc, :].rearrange("p (a b) -> p a b", b=512)[:, tq, :]: e.matmul(
                    bank.ap()[0:64, :], Tb[hr:hr + 64, h, :], xh,
                    start=True, stop=True, tile_position=(hr, 0)),
                     reads=[Tb_b, xnT_buf], writes=[bank_b], tag="umm")
                p.op("vector", lambda e, tq=tq, bank=bank: e.tensor_copy(
                    uT[:, 512 * tq:512 * tq + 512], bank.ap()[0:64, :]),
                     reads=[bank_b], writes=[uT_b], tag="uev")
            for qi in range(NTILES):
                sb_banks = [ctx.take_bank() for _ in range(2)]
                for kj in range(4):
                    bank, bank_b = sb_banks[kj % 2]
                    p.op("tensor", lambda e, qi=qi, kj=kj, bank=bank: e.matmul(
                        bank.ap(), uT[:, 128 * qi:128 * qi + 128],
                        uT[:, 512 * kj:512 * kj + 512], start=True, stop=True),
                         reads=[uT_b], writes=[bank_b], tag="smm2")
                    p.op("vector", lambda e, kj=kj, bank=bank: e.tensor_copy(
                        srow[:, 512 * kj:512 * kj + 512], bank.ap()),
                         reads=[bank_b], writes=[srow_b], tag="sev2")
                p.op("vector", lambda e: e.tensor_reduce(
                    nmax[:], srow[:], axis=AX.X, op=ALU.max, negate=True),
                     reads=[srow_b], writes=[nmax_b], tag="smax2")
                p.op("vector", lambda e: e.tensor_scalar(
                    nmax[:], nmax[:], 0.125, None, op0=ALU.mult),
                     reads=[nmax_b], writes=[nmax_b], tag="smaxsc2")
                p.op("scalar", lambda e: e.activation(
                    pbf[:], srow[:], AF.Exp, bias=nmax[:], scale=0.125, accum_out=ssum[:]),
                     reads=[srow_b, nmax_b], writes=[pbf_b, ssum_b], tag="sexp2")
                p.op("vector", lambda e: e.reciprocal(rsum[:], ssum[:]),
                     reads=[ssum_b], writes=[rsum_b], tag="srec2")
                p.op("vector", lambda e: e.tensor_scalar(
                    pbf[:], pbf[:], rsum[:], None, op0=ALU.mult),
                     reads=[pbf_b, rsum_b], writes=[pbf_b], tag="pscale2")
                cbank, cbank_b, cbank_i = ctx.take_bank(hold=True)
                for kj in range(NTILES):
                    tbank, tbank_b = ctx.take_bank()
                    tb = tbank.ap().bitcast(BF16)
                    p.op("tensor", lambda e, kj=kj, tb=tb: e.transpose(
                        tb[:, 0:128], pbf[:, 128 * kj:128 * kj + 128], ctx.ident[:]),
                         reads=[pbf_b, ctx.ident_buf], writes=[tbank_b], tag="pxp2")
                    p.op("vector", lambda e, tb=tb: e.tensor_copy(pts[:], tb[:, 0:128]),
                         reads=[tbank_b], writes=[pts_b], tag="pxpe2")
                    p.op("tensor", lambda e, kj=kj, h=h, cbank=cbank: e.matmul(
                        cbank.ap()[0:64, 0:128], vtok[:, kj, 64 * h:64 * h + 64], pts[:],
                        start=(kj == 0), stop=(kj == NTILES - 1)),
                         reads=[pts_b, vtok_b], writes=[cbank_b], tag="pvmm2")
                p.op("vector", lambda e, qi=qi, hr=hr, hc=hc, cbank=cbank: e.tensor_copy(
                    ctxT[hr:hr + 64, hc, 128 * qi:128 * qi + 128], cbank.ap()[0:64, 0:128]),
                     reads=[cbank_b], writes=[ctxT_buf], tag="ctxev2")
                ctx.release_bank(cbank_i)
    woT, woT_b = ctx.sb("woT2", [128, 8, 1024], BF16)
    p.dma("gpsimd", lambda e: e.dma_start(out=woT[:], in_=din["l2a_woT"].rearrange("(c p) d -> p c d", p=128)), writes=[woT_b], tag="ldwo")
    if True:
        for ti in range(NTILES):
            for dh in range(2):
                bank, bank_b = ctx.take_bank()
                mm = [(ctxT[:, c, 128 * ti:128 * ti + 128], woT[:, c, 512 * dh:512 * dh + 512],
                       [ctxT_buf, woT_b]) for c in range(8)]
                mm.append((ctx.ones_row_bf[0:1, :], bvo[:, 1024 + 512 * dh:1024 + 512 * dh + 512],
                           [ctx.onesrbf_buf, bvo_b]))
                ctx.matmul_acc(bank.ap(), bank_b, mm, tag="omm2")
                p.op("vector", lambda e, ti=ti, dh=dh, bank=bank: e.tensor_tensor(
                    x_t[:, ti, 512 * dh:512 * dh + 512], x_t[:, ti, 512 * dh:512 * dh + 512],
                    bank.ap(), op=ALU.add),
                     reads=[bank_b, x_bufs[ti]], writes=[x_bufs[ti]], tag="oadd2")


def bcast_row(ctx, row_ap, row_buf, width, tag="bc"):
    p = ctx.prog
    bf = row_ap.dtype == BF16
    ones = ctx.ones_row_bf if bf else ctx.ones_row
    ones_b = ctx.onesrbf_buf if bf else ctx.onesr_buf
    out, out_b = ctx.sb(f"bc_{tag}", [128, width])
    for j in range(0, width, 512):
        w = min(512, width - j)
        bank, bank_b = ctx.take_bank()
        p.op("tensor", lambda e, j=j, w=w, bank=bank: e.matmul(
            bank.ap()[:, 0:w], ones[0:1, :], row_ap[:, j:j + w], start=True, stop=True),
             reads=[row_buf, ones_b], writes=[bank_b], tag=f"bcmm_{tag}")
        p.op("vector", lambda e, j=j, w=w, bank=bank: e.tensor_copy(out[:, j:j + w], bank.ap()[:, 0:w]),
             reads=[bank_b], writes=[out_b], tag=f"bcev_{tag}")
    return out, out_b


def emit_L3_ffn(ctx, din, x_t, x_bufs):
    p = ctx.prog
    xnT, xnT_buf = ctx.sb("xnT_3f", [128, 8, 2048], BF16)
    ln_to_featmajor(ctx, x_t, x_bufs, xnT, xnT_buf)
    with ctx.scope("l3f"):
        win, win_b = ctx.sb("win", [128, 8, 64], BF16)
        p.dma("gpsimd", lambda e: e.dma_start(out=win[:], in_=din["l3f_win"].rearrange("(c p) d -> p c d", p=128)), writes=[win_b], tag="ldwin")
        wout, wout_b = ctx.sb("wout", [64, 1024], BF16)
        p.dma("gpsimd", lambda e: e.dma_start(out=wout[:], in_=din["l3f_wout"]), writes=[wout_b], tag="ldwout")
        rows, rows_b = ctx.sb("l3f_rows", [1, 4224])
        p.dma("sync", lambda e: e.dma_start(out=rows[:, 0:1024], in_=din["l3f_w1"]), writes=[rows_b], tag="ldr")
        p.dma("sync", lambda e: e.dma_start(out=rows[:, 1024:2048], in_=din["l3f_b1"]), writes=[rows_b], tag="ldr")
        p.dma("sync", lambda e: e.dma_start(out=rows[:, 2048:3072], in_=din["l3f_w2"]), writes=[rows_b], tag="ldr")
        p.dma("sync", lambda e: e.dma_start(out=rows[:, 3072:3136], in_=din["l3f_b2"]), writes=[rows_b], tag="ldr")
        p.dma("sync", lambda e: e.dma_start(out=rows[:, 3136:3200], in_=din["l3f_bin"]), writes=[rows_b], tag="ldr")
        p.dma("sync", lambda e: e.dma_start(out=rows[:, 3200:4224], in_=din["l3f_bout"]), writes=[rows_b], tag="ldr")
        w1bc, w1bc_b = bcast_row(ctx, rows[:, 0:1024], rows_b, 1024, "w1")
        b1bc, b1bc_b = bcast_row(ctx, rows[:, 1024:2048], rows_b, 1024, "b1")
        w2bc, w2bc_b = bcast_row(ctx, rows[:, 2048:3072], rows_b, 1024, "w2")
        b2bc, b2bc_b = bcast_row(ctx, rows[:, 3072:3136], rows_b, 64, "b2")

        sin_, sin_b = ctx.sb("l3f_s", [128, 64])
        hh, hh_b = ctx.sb("l3f_h", [128, 1024])
        gg, gg_b = ctx.sb("l3f_g", [128, 64])
        ggbf, ggbf_b = ctx.sb("l3f_gbf", [128, 64], BF16)
        gT, gT_b = ctx.sb("l3f_gT", [64, 128], BF16)
        for ti in range(NTILES):
            bank, bank_b = ctx.take_bank()
            mm = [(xnT[:, c, 128 * ti:128 * ti + 128], win[:, c, :], [xnT_buf, win_b]) for c in range(8)]
            mm.append((ctx.ones_row[0:1, :], rows[:, 3136:3200], [ctx.onesr_buf, rows_b]))
            ctx.matmul_acc(bank.ap()[:, 0:64], bank_b, mm, tag="smm3f")
            p.op("vector", lambda e, bank=bank: e.tensor_copy(sin_[:], bank.ap()[:, 0:64]),
                 reads=[bank_b], writes=[sin_b], tag="sev3f")
            p.op("gpsimd", lambda e: e.tensor_tensor(
                hh[:].rearrange("p (m i) -> p m i", i=16),
                sin_[:].unsqueeze(2).broadcast_to((128, 64, 16)),
                w1bc[:].rearrange("p (m i) -> p m i", i=16), op=ALU.mult),
                 reads=[sin_b, w1bc_b], writes=[hh_b], tag="hmul3f")
            p.op("vector", lambda e: e.tensor_tensor(hh[:], hh[:], b1bc[:], op=ALU.add),
                 reads=[hh_b, b1bc_b], writes=[hh_b], tag="hadd3f")
            p.op("scalar", lambda e: e.activation(hh[:], hh[:], AF.Relu, bias=0.0, scale=1.0),
                 reads=[hh_b], writes=[hh_b], tag="hrelu3f")
            p.op("vector", lambda e: e.tensor_tensor(hh[:], hh[:], w2bc[:], op=ALU.mult),
                 reads=[hh_b, w2bc_b], writes=[hh_b], tag="gmul3f")
            p.op("vector", lambda e: e.tensor_reduce(
                gg[:], hh[:].rearrange("p (m i) -> p m i", i=16), axis=AX.X, op=ALU.add),
                 reads=[hh_b], writes=[gg_b], tag="gred3f")
            p.op("vector", lambda e: e.tensor_tensor(gg[:], gg[:], b2bc[:], op=ALU.add),
                 reads=[gg_b, b2bc_b], writes=[gg_b], tag="gb23f")
            p.op("vector", lambda e: e.tensor_copy(ggbf[:], gg[:]),
                 reads=[gg_b], writes=[ggbf_b], tag="gcast3f")
            bx, bx_b = ctx.take_bank()
            bxv = bx.ap().bitcast(BF16)
            p.op("tensor", lambda e, bxv=bxv: e.transpose(bxv[0:64, 0:128], ggbf[:], ctx.ident[:]),
                 reads=[ggbf_b, ctx.ident_buf], writes=[bx_b], tag="gxp3f")
            p.op("vector", lambda e, bxv=bxv: e.tensor_copy(gT[:], bxv[0:64, 0:128]),
                 reads=[bx_b], writes=[gT_b], tag="gxpe3f")
            for dh in range(2):
                bank2, bank2_b = ctx.take_bank()
                mm2 = [(gT[:], wout[:, 512 * dh:512 * dh + 512], [gT_b, wout_b])]
                mm2.append((ctx.ones_row[0:1, :], rows[:, 3200 + 512 * dh:3200 + 512 * dh + 512],
                            [ctx.onesr_buf, rows_b]))
                ctx.matmul_acc(bank2.ap(), bank2_b, mm2, tag="omm3f")
                p.op("vector", lambda e, ti=ti, dh=dh, bank2=bank2: e.tensor_tensor(
                    x_t[:, ti, 512 * dh:512 * dh + 512], x_t[:, ti, 512 * dh:512 * dh + 512],
                    bank2.ap(), op=ALU.add),
                     reads=[bank2_b, x_bufs[ti]], writes=[x_bufs[ti]], tag="oadd3f")


def emit_L2_ffn(ctx, din, x_t, x_bufs):
    p = ctx.prog
    attn_out, attn_out_b = ctx.sb("l2f_attn", [128, 16, 1024], BF16)

    with ctx.scope("l2f_attn"):
        lnr, lnr_b = ctx.sb("l2f_lnrows1", [1, 2048])
        p.dma("sync", lambda e: e.dma_start(out=lnr[:, 0:1024], in_=din["l2f_ln1w"]), writes=[lnr_b], tag="ldln")
        p.dma("sync", lambda e: e.dma_start(out=lnr[:, 1024:2048], in_=din["l2f_ln1b"]), writes=[lnr_b], tag="ldln")
        w1bc, w1bc_b = bcast_row(ctx, lnr[:, 0:1024], lnr_b, 1024, "ln1w")
        b1bc, b1bc_b = bcast_row(ctx, lnr[:, 1024:2048], lnr_b, 1024, "ln1b")
        ipb, ipb_b = ctx.sb("ipb", [1, 1024], BF16)
        wpart, wpart_b = ctx.sb("wpart", [128, 8, 512], BF16)
        hnTbt, hnTbt_b = ctx.sb("hnTbt", [128, 8, 1024], BF16)
        kk, kk_b = ctx.sb("l2f_k", [128, 8, 1024], BF16)
        vv, vv_b = ctx.sb("l2f_v", [128, 8, 1024], BF16)
        S, S_b = ctx.sb("l2f_S", [128, 16, 8, 8])
        P, P_b = S, S_b
        nmaxs, nmaxs_b = ctx.sb("l2f_nm", [128, 16])
        zs, zs_b = ctx.sb("l2f_z", [128, 16])
        rz, rz_b = ctx.sb("l2f_rz", [128, 16])
        hacc, hacc_b = ctx.sb("l2f_hacc", [128, 1024])
        htmp, htmp_b = ctx.sb("l2f_htmp", [128, 1024])
        for bt in range(2):
            ln_to_featmajor(ctx, x_t, x_bufs, hnTbt, hnTbt_b,
                            tile_list=list(range(8 * bt, 8 * bt + 8)),
                            affine=(w1bc, w1bc_b, b1bc, b1bc_b))
            for part, dst, dst_b in ((1, kk, kk_b), (2, vv, vv_b)):
                p.dma("gpsimd", lambda e, part=part: e.dma_start(
                    out=ipb[:], in_=din["l2f_inprojb"][:, 1024 * part:1024 * part + 1024]),
                      writes=[ipb_b], tag="ldipb")
                for dh in range(2):
                    p.dma("gpsimd", lambda e, part=part, dh=dh: e.dma_start(
                        out=wpart[:], in_=din["l2f_inprojT"][:, 1024 * part + 512 * dh:1024 * part + 512 * dh + 512].rearrange("(c p) d -> p c d", p=128)),
                          writes=[wpart_b], tag="ldwp")
                    for s in range(8):
                        bank, bank_b = ctx.take_bank()
                        mm = [(hnTbt[:, c, 128 * s:128 * s + 128], wpart[:, c, :],
                               [hnTbt_b, wpart_b]) for c in range(8)]
                        mm.append((ctx.ones_row_bf[0:1, :], ipb[:, 512 * dh:512 * dh + 512],
                                   [ctx.onesrbf_buf, ipb_b]))
                        ctx.matmul_acc(bank.ap(), bank_b, mm, tag="kvmm")
                        p.op("scalar", lambda e, s=s, dh=dh, bank=bank, dst=dst: e.activation(
                            dst[:, s, 512 * dh:512 * dh + 512], bank.ap(), AF.Copy, bias=0.0, scale=1.0),
                             reads=[bank_b], writes=[dst_b], tag="kvev")
            p.dma("gpsimd", lambda e: e.dma_start(
                out=ipb[:], in_=din["l2f_inprojb"][:, 0:1024]), writes=[ipb_b], tag="ldipq")
            for qh in range(2):
                p.dma("gpsimd", lambda e, qh=qh: e.dma_start(
                    out=wpart[:], in_=din["l2f_inprojT"][:, 512 * qh:512 * qh + 512].rearrange("(c p) d -> p c d", p=128)),
                      writes=[wpart_b], tag="ldwq")
                for s in range(8):
                    bank, bank_b, bank_i = ctx.take_bank(hold=True)
                    mm = [(hnTbt[:, c, 128 * s:128 * s + 128], wpart[:, c, :],
                           [hnTbt_b, wpart_b]) for c in range(8)]
                    mm.append((ctx.ones_row_bf[0:1, :], ipb[:, 512 * qh:512 * qh + 512],
                               [ctx.onesrbf_buf, ipb_b]))
                    ctx.matmul_acc(bank.ap(), bank_b, mm, tag="qmm2f")
                    for t in range(s + 1):
                        p.op("vector", lambda e, s=s, t=t, qh=qh, bank=bank: e.tensor_tensor(
                            htmp[:, 0:512], bank.ap(), kk[:, t, 512 * qh:512 * qh + 512], op=ALU.mult),
                             reads=[bank_b, kk_b], writes=[htmp_b], tag="qk2f")
                        p.op("vector", lambda e, s=s, t=t, qh=qh: e.tensor_reduce(
                            S[:, 8 * qh:8 * qh + 8, s, t].unsqueeze(2),
                            htmp[:, 0:512].rearrange("p (n d) -> p n d", d=64),
                            axis=AX.X, op=ALU.add),
                             reads=[htmp_b], writes=[S_b], tag="sred2f")
                    ctx.release_bank(bank_i)
            for s in range(8):
                p.op("vector", lambda e, s=s: e.tensor_scalar(
                    S[:, :, s, 0:s + 1], S[:, :, s, 0:s + 1], 0.125, None, op0=ALU.mult),
                     reads=[S_b], writes=[S_b], tag="ssc2f")
                p.op("vector", lambda e, s=s: e.tensor_reduce(
                    nmaxs[:], S[:, :, s, 0:s + 1], axis=AX.X, op=ALU.max, negate=True),
                     reads=[S_b], writes=[nmaxs_b], tag="smax2f")
                p.op("gpsimd", lambda e, s=s: e.tensor_tensor(
                    P[:, :, s, 0:s + 1], S[:, :, s, 0:s + 1],
                    nmaxs[:].unsqueeze(2).broadcast_to((128, 16, s + 1)), op=ALU.add),
                     reads=[S_b, nmaxs_b], writes=[P_b], tag="psub2f")
                p.op("scalar", lambda e, s=s: e.activation(
                    P[:, :, s, 0:s + 1], P[:, :, s, 0:s + 1], AF.Exp, bias=0.0, scale=1.0),
                     reads=[P_b], writes=[P_b], tag="pexp2f")
                p.op("vector", lambda e, s=s: e.tensor_reduce(
                    zs[:], P[:, :, s, 0:s + 1], axis=AX.X, op=ALU.add),
                     reads=[P_b], writes=[zs_b], tag="psum2f")
                p.op("vector", lambda e: e.reciprocal(rz[:], zs[:]),
                     reads=[zs_b], writes=[rz_b], tag="prec2f")
                p.op("gpsimd", lambda e, s=s: e.tensor_tensor(
                    P[:, :, s, 0:s + 1], P[:, :, s, 0:s + 1],
                    rz[:].unsqueeze(2).broadcast_to((128, 16, s + 1)), op=ALU.mult),
                     reads=[P_b, rz_b], writes=[P_b], tag="pnorm2f")
                for t in range(s + 1):
                    p.op("gpsimd", lambda e, s=s, t=t: e.tensor_tensor(
                        htmp[:].rearrange("p (n d) -> p n d", d=64),
                        vv[:, t, :].rearrange("p (n d) -> p n d", d=64),
                        P[:, :, s, t].unsqueeze(2).broadcast_to((128, 16, 64)), op=ALU.mult),
                         reads=[vv_b, P_b], writes=[htmp_b], tag="pv2f")
                    if t == 0:
                        p.op("vector", lambda e: e.tensor_copy(hacc[:], htmp[:]),
                             reads=[htmp_b], writes=[hacc_b], tag="pva2f")
                    else:
                        p.op("vector", lambda e: e.tensor_tensor(hacc[:], hacc[:], htmp[:], op=ALU.add),
                             reads=[htmp_b, hacc_b], writes=[hacc_b], tag="pvb2f")
                p.op("vector", lambda e, bt=bt, s=s: e.tensor_copy(attn_out[:, 8 * bt + s, :], hacc[:]),
                     reads=[hacc_b], writes=[attn_out_b], tag="hastash")

    with ctx.scope("l2f_oproj"):
        opw, opw_b = ctx.sb("opw", [128, 8, 1024], BF16)
        p.dma("gpsimd", lambda e: e.dma_start(out=opw[:], in_=din["l2f_outprojT"].rearrange("(c p) d -> p c d", p=128)), writes=[opw_b], tag="ldopw")
        opb, opb_b = ctx.sb("opb", [1, 1024], BF16)
        p.dma("gpsimd", lambda e: e.dma_start(out=opb[:], in_=din["l2f_outprojb"]), writes=[opb_b], tag="ldopb")
        atT, atT_b = ctx.sb("l2f_atT", [128, 8, 128], BF16)
        for ti in range(NTILES):
            for g in range(2):
                bank, bank_b = ctx.take_bank()
                bv = bank.ap().bitcast(BF16)
                for j in range(4):
                    p.op("tensor", lambda e, ti=ti, g=g, j=j, bv=bv: e.transpose(
                        bv[:, 128 * j:128 * j + 128], attn_out[:, ti, 512 * g + 128 * j:512 * g + 128 * j + 128],
                        ctx.ident[:]),
                         reads=[attn_out_b, ctx.ident_buf], writes=[bank_b], tag="haxp")
                p.op("vector", lambda e, g=g, bv=bv: e.tensor_copy(
                    atT[:, 4 * g:4 * g + 4, :], bv[:, 0:512].rearrange("p (a b) -> p a b", b=128)),
                     reads=[bank_b], writes=[atT_b], tag="haxpe")
            for dh in range(2):
                bank, bank_b = ctx.take_bank()
                mm = [(atT[:, c, :], opw[:, c, 512 * dh:512 * dh + 512], [atT_b, opw_b]) for c in range(8)]
                mm.append((ctx.ones_row_bf[0:1, :], opb[:, 512 * dh:512 * dh + 512], [ctx.onesrbf_buf, opb_b]))
                ctx.matmul_acc(bank.ap(), bank_b, mm, tag="opmm")
                p.op("scalar", lambda e, ti=ti, dh=dh, bank=bank: e.activation(
                    attn_out[:, ti, 512 * dh:512 * dh + 512], bank.ap(), AF.Copy, bias=0.0, scale=1.0),
                     reads=[bank_b], writes=[attn_out_b], tag="opev")

    hn2T, hn2T_buf = ctx.sb("hn2T", [128, 8, 2048], BF16)
    gates, gates_b = ctx.sb("l2f_gates", [128, 16, 8])
    with ctx.scope("l2f_mid"):
        lnr2, lnr2_b = ctx.sb("l2f_lnrows2", [1, 2048])
        p.dma("sync", lambda e: e.dma_start(out=lnr2[:, 0:1024], in_=din["l2f_ln2w"]), writes=[lnr2_b], tag="ldln")
        p.dma("sync", lambda e: e.dma_start(out=lnr2[:, 1024:2048], in_=din["l2f_ln2b"]), writes=[lnr2_b], tag="ldln")
        w2bc, w2bc_b = bcast_row(ctx, lnr2[:, 0:1024], lnr2_b, 1024, "ln2w")
        b2bc, b2bc_b = bcast_row(ctx, lnr2[:, 1024:2048], lnr2_b, 1024, "ln2b")
        rt, rt_b = ctx.sb("routerT", [128, 8, 8], BF16)
        p.dma("gpsimd", lambda e: e.dma_start(out=rt[:], in_=din["l2f_routerT"].rearrange("(c p) d -> p c d", p=128)), writes=[rt_b], tag="ldrt")
        x1t, x1t_b = ctx.sb("x1t", [128, 1, 1024])
        for ti in range(NTILES):
            _ln_to_featmajor(ctx, x_t, x_bufs, None, None, tile_list=[ti],
                             xn_out=(x1t, [x1t_b] * NTILES))
            p.op("vector", lambda e, ti=ti: e.tensor_tensor(
                x1t[:, 0, :], x1t[:, 0, :], attn_out[:, ti, :], op=ALU.add),
                 reads=[x1t_b, attn_out_b], writes=[x1t_b], tag="x1add")
            p.op("vector", lambda e, ti=ti: e.tensor_tensor(
                x_t[:, ti, :], x_t[:, ti, :], x1t[:, 0, :], op=ALU.add),
                 reads=[x1t_b, x_bufs[ti]], writes=[x_bufs[ti]], tag="strx1")
            _ln_to_featmajor(ctx, x1t, [x1t_b] * NTILES, hn2T, hn2T_buf, tile_list=[0],
                             dst_off=ti, affine=(w2bc, w2bc_b, b2bc, b2bc_b))
        lgt, lgt_b = ctx.sb("lgt2", [128, 8])
        m1, m1_b = ctx.sb("m1_2", [128, 1])
        m2, m2_b = ctx.sb("m2_2", [128, 1])
        eq1, eq1_b = ctx.sb("eq1_2", [128, 8])
        eq2, eq2_b = ctx.sb("eq2_2", [128, 8])
        zsum, zsum_b = ctx.sb("zsum2", [128, 1])
        for ti in range(NTILES):
            bank, bank_b = ctx.take_bank()
            mm = [(hn2T[:, c, 128 * ti:128 * ti + 128], rt[:, c, :], [hn2T_buf, rt_b]) for c in range(8)]
            ctx.matmul_acc(bank.ap()[:, 0:8], bank_b, mm, tag="rtmm")
            p.op("vector", lambda e, bank=bank: e.tensor_copy(lgt[:], bank.ap()[:, 0:8]),
                 reads=[bank_b], writes=[lgt_b], tag="rtev")
            p.op("vector", lambda e: e.tensor_reduce(m1[:], lgt[:], axis=AX.X, op=ALU.max, negate=True),
                 reads=[lgt_b], writes=[m1_b], tag="rtm1")
            p.op("scalar", lambda e: e.activation(lgt[:], lgt[:], AF.Exp, bias=m1[:], scale=1.0, accum_out=zsum[:]),
                 reads=[lgt_b, m1_b], writes=[lgt_b, zsum_b], tag="rtexp")
            p.op("vector", lambda e: e.reciprocal(m2[:], zsum[:]),
                 reads=[zsum_b], writes=[m2_b], tag="rtrec")
            p.op("vector", lambda e: e.tensor_scalar(lgt[:], lgt[:], m2[:], None, op0=ALU.mult),
                 reads=[lgt_b, m2_b], writes=[lgt_b], tag="rtnorm")
            p.op("vector", lambda e: e.tensor_reduce(m1[:], lgt[:], axis=AX.X, op=ALU.max),
                 reads=[lgt_b], writes=[m1_b], tag="g2m1")
            p.op("vector", lambda e: e.tensor_scalar(eq1[:], lgt[:], m1[:], None, op0=ALU.is_ge),
                 reads=[lgt_b, m1_b], writes=[eq1_b], tag="g2eq1")
            p.op("vector", lambda e: e.tensor_scalar(eq2[:], eq1[:], -1e30, None, op0=ALU.mult),
                 reads=[eq1_b], writes=[eq2_b], tag="g2msk")
            p.op("vector", lambda e: e.tensor_tensor(eq2[:], lgt[:], eq2[:], op=ALU.add),
                 reads=[lgt_b, eq2_b], writes=[eq2_b], tag="g2l2")
            p.op("vector", lambda e: e.tensor_reduce(m2[:], eq2[:], axis=AX.X, op=ALU.max),
                 reads=[eq2_b], writes=[m2_b], tag="g2m2")
            p.op("vector", lambda e: e.tensor_scalar(eq2[:], eq2[:], m2[:], None, op0=ALU.is_ge),
                 reads=[eq2_b, m2_b], writes=[eq2_b], tag="g2eq2")
            p.op("vector", lambda e: e.tensor_tensor(eq1[:], eq1[:], eq2[:], op=ALU.add),
                 reads=[eq1_b, eq2_b], writes=[eq1_b], tag="g2or")
            p.op("vector", lambda e, ti=ti: e.tensor_tensor(gates[:, ti, :], lgt[:], eq1[:], op=ALU.mult),
                 reads=[lgt_b, eq1_b], writes=[gates_b], tag="g2g")

    with ctx.scope("l2f_moe"):
        w1h, w1h_b = ctx.sb("w1h2", [128, 8, 1024], BF16)
        w2h, w2h_b = ctx.sb("w2h2", [128, 8, 1024], BF16)
        hT, hT_b = ctx.sb("hT2", [128, 8, 2048], BF16)
        tmp, tmp_b = ctx.sb("l2ftmp", [128, 512])
        for e_i in range(8):
            for fh in range(2):
                p.dma("gpsimd", lambda e, e_i=e_i, fh=fh: e.dma_start(
                    out=w1h[:], in_=din["l2f_w1"][e_i, :, 1024 * fh:1024 * fh + 1024].rearrange("(c p) f -> p c f", p=128)),
                      writes=[w1h_b], tag="ld_w1h2")
                p.dma("gpsimd", lambda e, e_i=e_i, fh=fh: e.dma_start(
                    out=w2h[:], in_=din["l2f_w2"][e_i, 1024 * fh:1024 * fh + 1024, :].rearrange("(c p) f -> p c f", p=128)),
                      writes=[w2h_b], tag="ld_w2h2")
                for fs in range(8):
                    bank, bank_b = ctx.take_bank()
                    for ts in range(4):
                        mm = [(w1h[:, c, 128 * fs:128 * fs + 128],
                               hn2T[:, c, 512 * ts:512 * ts + 512], [w1h_b, hn2T_buf]) for c in range(8)]
                        ctx.matmul_acc(bank.ap(), bank_b, mm, tag="moeh2")
                        p.op("scalar", lambda e, fs=fs, ts=ts, bank=bank: e.activation(
                            hT[:, fs, 512 * ts:512 * ts + 512], bank.ap(), AF.Gelu, bias=0.0, scale=1.0),
                             reads=[bank_b], writes=[hT_b], tag="moegelu")
                for ti in range(NTILES):
                    for dh in range(2):
                        bank, bank_b = ctx.take_bank()
                        mm = [(hT[:, fs, 128 * ti:128 * ti + 128], w2h[:, fs, 512 * dh:512 * dh + 512],
                               [hT_b, w2h_b]) for fs in range(8)]
                        ctx.matmul_acc(bank.ap(), bank_b, mm, tag="moey2")
                        p.op("vector", lambda e, e_i=e_i, ti=ti, bank=bank: e.tensor_scalar(
                            tmp[:], bank.ap(), gates[:, ti, e_i:e_i + 1], None, op0=ALU.mult),
                             reads=[bank_b, gates_b], writes=[tmp_b], tag="moesc2")
                        p.op("vector", lambda e, ti=ti, dh=dh: e.tensor_tensor(
                            x_t[:, ti, 512 * dh:512 * dh + 512], x_t[:, ti, 512 * dh:512 * dh + 512],
                            tmp[:], op=ALU.add),
                             reads=[tmp_b, x_bufs[ti]], writes=[x_bufs[ti]], tag="moeadd2")


def emit_L3_attn(ctx, din, x_t, x_bufs):
    p = ctx.prog
    RT2 = 128 ** -0.5
    lnr, lnr_b = ctx.sb("l3a_lnr", [1, 2048], BF16)
    p.dma("gpsimd", lambda e: e.dma_start(out=lnr[:, 0:1024], in_=din["l3a_lnw"]), writes=[lnr_b], tag="l3ln")
    p.dma("gpsimd", lambda e: e.dma_start(out=lnr[:, 1024:2048], in_=din["l3a_lnb"]), writes=[lnr_b], tag="l3ln")
    omr, omr_b = ctx.sb("l3a_omr", [1, 1024], BF16)
    p.dma("gpsimd", lambda e: e.dma_start(out=omr[:], in_=din["l3a_omega"]), writes=[omr_b], tag="l3om")
    gbr, gbr_b = ctx.sb("l3a_gbr", [1, 512])
    p.dma("sync", lambda e: e.dma_start(out=gbr[:], in_=din["l3a_gambet"]), writes=[gbr_b], tag="l3gb")
    g0bc, g0bc_b = bcast_row(ctx, gbr[:, 0:128], gbr_b, 128, "g0")
    g1bc, g1bc_b = bcast_row(ctx, gbr[:, 128:256], gbr_b, 128, "g1")
    b0bc, b0bc_b = bcast_row(ctx, gbr[:, 256:384], gbr_b, 128, "b0")
    b1bc, b1bc_b = bcast_row(ctx, gbr[:, 384:512], gbr_b, 128, "b1")

    biasbc, biasbc_b = ctx.sb("biasbc", [128, 64])
    with ctx.scope("l3a_rot"):
        rcs, rcs_b = ctx.sb("rcs", [8, 128])
        p.dma("sync", lambda e: e.dma_start(out=rcs[:, 0:64], in_=din["l3a_rcos"]), writes=[rcs_b], tag="rc")
        p.dma("sync", lambda e: e.dma_start(out=rcs[:, 64:128], in_=din["l3a_rsin"]), writes=[rcs_b], tag="rc")
        rbr, rbr_b = ctx.sb("rbr", [1, 256])
        p.dma("sync", lambda e: e.dma_start(out=rbr[:], in_=din["l3a_rb"]), writes=[rbr_b], tag="rb")
        rots = []
        for ri in range(2):
            we, we_b = ctx.sb("rwe", [8, 128])
            bank, bank_b = ctx.take_bank()
            p.op("tensor", lambda e, ri=ri, bank=bank: e.matmul(
                bank.ap()[0:8, 0:128], ctx.ones_row[0:1, 0:8], rbr[:, 128 * ri:128 * ri + 128],
                start=True, stop=True),
                 reads=[rbr_b, ctx.onesr_buf], writes=[bank_b], tag="rbc")
            p.op("vector", lambda e, bank=bank, we=we: e.tensor_copy(we[:], bank.ap()[0:8, 0:128]),
                 reads=[bank_b], writes=[we_b], tag="rbev")
            rot, rot_b = ctx.sb("rot", [8, 128])
            t4, t4_b = ctx.sb("rt4", [8, 64])
            p.op("vector", lambda e, we=we, rot=rot: e.tensor_tensor(rot[:, 0:64], we[:, 0:64], rcs[:, 0:64], op=ALU.mult),
                 reads=[we_b, rcs_b], writes=[rot_b], tag="r1")
            p.op("vector", lambda e, we=we, t4=t4: e.tensor_tensor(t4[:], we[:, 64:128], rcs[:, 64:128], op=ALU.mult),
                 reads=[we_b, rcs_b], writes=[t4_b], tag="r2")
            p.op("vector", lambda e, rot=rot, t4=t4: e.tensor_tensor(rot[:, 0:64], rot[:, 0:64], t4[:], op=ALU.subtract),
                 reads=[t4_b, rot_b], writes=[rot_b], tag="r3")
            p.op("vector", lambda e, we=we, rot=rot: e.tensor_tensor(rot[:, 64:128], we[:, 64:128], rcs[:, 0:64], op=ALU.mult),
                 reads=[we_b, rcs_b], writes=[rot_b], tag="r4")
            p.op("vector", lambda e, we=we, t4=t4: e.tensor_tensor(t4[:], we[:, 0:64], rcs[:, 64:128], op=ALU.mult),
                 reads=[we_b, rcs_b], writes=[t4_b], tag="r5")
            p.op("vector", lambda e, rot=rot, t4=t4: e.tensor_tensor(rot[:, 64:128], rot[:, 64:128], t4[:], op=ALU.add),
                 reads=[t4_b, rot_b], writes=[rot_b], tag="r6")
            bank2, bank2_b = ctx.take_bank()
            p.op("tensor", lambda e, rot=rot, bank2=bank2: e.transpose(
                bank2.ap()[:, 0:8], rot[:], ctx.ident32[0:8, 0:8]),
                 reads=[rot_b, ctx.ident32_buf], writes=[bank2_b], tag="rxp")
            rT, rT_b = ctx.sb("rotT", [128, 8])
            p.op("vector", lambda e, rT=rT, bank2=bank2: e.tensor_copy(rT[:], bank2.ap()[:, 0:8]),
                 reads=[bank2_b], writes=[rT_b], tag="rxpe")
            rots.append((rT, rT_b))
        bank3, bank3_b = ctx.take_bank()
        p.op("tensor", lambda e, bank3=bank3: e.matmul(
            bank3.ap()[0:8, 0:8], rots[0][0][:], rots[1][0][:], start=True, stop=True),
             reads=[rots[0][1], rots[1][1]], writes=[bank3_b], tag="rbmm")
        b88, b88_b = ctx.sb("b88", [8, 8])
        p.op("vector", lambda e, bank3=bank3: e.tensor_copy(b88[:], bank3.ap()[0:8, 0:8]),
             reads=[bank3_b], writes=[b88_b], tag="rbev2")
        brow64, brow64_b = ctx.sb("brow64", [1, 64])
        for s in range(8):
            p.dma("gpsimd", lambda e, s=s: e.dma_start(out=brow64[:, 8 * s:8 * s + 8], in_=b88[s:s + 1, :]),
                  reads=[b88_b], writes=[brow64_b], tag="rbgth")
        bank4, bank4_b = ctx.take_bank()
        p.op("tensor", lambda e, bank4=bank4: e.matmul(
            bank4.ap()[:, 0:64], ctx.ones_row[0:1, :], brow64[:], start=True, stop=True),
             reads=[brow64_b, ctx.onesr_buf], writes=[bank4_b], tag="rbbc")
        p.op("vector", lambda e, bank4=bank4: e.tensor_copy(biasbc[:], bank4.ap()[:, 0:64]),
             reads=[bank4_b], writes=[biasbc_b], tag="rbbce")

    hb_row, hb_row_b = ctx.sb("l3a_hbrow", [1, 1024], BF16)
    p.dma("gpsimd", lambda e: e.dma_start(out=hb_row[:], in_=din["l3a_hb"]), writes=[hb_row_b], tag="l3hb")
    vb_row, vb_row_b = ctx.sb("l3a_vbrow", [1, 2048], BF16)
    p.dma("gpsimd", lambda e: e.dma_start(out=vb_row[:], in_=din["l3a_vb"]), writes=[vb_row_b], tag="l3vb")
    mxb_row, mxb_row_b = ctx.sb("l3a_mxbrow", [1, 4224], BF16)
    p.dma("gpsimd", lambda e: e.dma_start(out=mxb_row[:], in_=din["l3a_mxb"]), writes=[mxb_row_b], tag="l3mxb")

    d_u = ctx.nc.dram_tensor("l3a_spill_u", [128, 8, 1024], BF16).ap()
    d_u_b = p.buf("d_u")
    d_hx = ctx.nc.dram_tensor("l3a_spill_hx", [128, 8, 1024], BF16).ap()
    d_hx_b = p.buf("d_hx")
    d_v = ctx.nc.dram_tensor("l3a_spill_v", [128, 8, 2048], BF16).ap()
    d_v_b = p.buf("d_v")
    d_r = ctx.nc.dram_tensor("l3a_spill_r", [128, 8, 2048], BF16).ap()
    d_r_b = p.buf("d_r")
    d_xnT = ctx.nc.dram_tensor("l3a_spill_xnT", [128, 8, 1024], BF16).ap()
    d_xnT_b = p.buf("d_xnT")
    d_mxT = ctx.nc.dram_tensor("l3a_spill_mxT", [128, 8, 1024], BF16).ap()
    d_mxT_b = p.buf("d_mxT")

    def _bt_body(bt):
        with ctx.scope("l3a_bt"):
            zz, zz_b = ctx.sb("l3a_z", [128, 8, 128])
            with ctx.scope("l3a_conv"):
                xnT, xnT_buf = ctx.sb("xnT3", [128, 8, 1024], BF16)
                mxT, mxT_b = ctx.sb("mxT", [128, 8, 1024], BF16)
                lnwbc, lnwbc_b = bcast_row(ctx, lnr[:, 0:1024], lnr_b, 1024, "l3w")
                lnbbc, lnbbc_b = bcast_row(ctx, lnr[:, 1024:2048], lnr_b, 1024, "l3b")
                ombc, ombc_b = bcast_row(ctx, omr[:], omr_b, 1024, "l3om")
                xn, xn_b = ctx.sb("l3a_xn", [128, 8, 1024], BF16)
                ln0t, ln0t_b = ctx.sb("l3a_ln0", [128, 1, 1024])
                for s in range(8):
                    _ln_to_featmajor(ctx, x_t, x_bufs, None, None, tile_list=[8 * bt + s],
                                     xn_out=(ln0t, [ln0t_b]))
                    _ln_to_featmajor(ctx, ln0t, [ln0t_b], xnT, xnT_buf, tile_list=[0], dst_off=s,
                                     xn_out=(xn[:, s:s + 1, :], [xn_b]),
                                     affine=(lnwbc, lnwbc_b, lnbbc, lnbbc_b))
                p.dma("sync", lambda e: e.dma_start(out=d_xnT[:], in_=xnT[:]),
                      reads=[xnT_buf], writes=[d_xnT_b], tag="sp_xnT")
                kbc, kbc_b = ctx.sb("kbc", [128, 8, 1024], BF16)
                with ctx.scope("l3a_ema"):
                    qqa, qqa_b = ctx.sb("ema_q", [16, 1024])
                    wwa, wwa_b = ctx.sb("ema_w", [16, 1024])
                    krow, krow_b = ctx.sb("ema_krow", [1, 1024])
                    emat, emat_b = ctx.sb("emaT", [16, 4, 512])
                    for ch in range(2):
                        sl = slice(512 * ch, 512 * ch + 512)
                        p.dma("sync", lambda e, ch=ch: e.dma_start(
                            out=emat[:], in_=din["l3a_emaT"].rearrange("n (k d) -> n k d", k=4)[:, :, 512 * ch:512 * ch + 512]),
                            writes=[emat_b], tag="ldema")
                        p.op("scalar", lambda e, sl=sl: e.activation(wwa[:, sl], emat[:, 0, :], AF.Sigmoid, bias=0.0, scale=1.0),
                             reads=[emat_b], writes=[wwa_b], tag="emp")
                        p.op("scalar", lambda e, sl=sl: e.activation(qqa[:, sl], emat[:, 1, :], AF.Sigmoid, bias=0.0, scale=1.0),
                             reads=[emat_b], writes=[qqa_b], tag="ema")
                        p.op("vector", lambda e, sl=sl: e.tensor_tensor(qqa[:, sl], wwa[:, sl], qqa[:, sl], op=ALU.mult),
                             reads=[wwa_b, qqa_b], writes=[qqa_b], tag="emq1")
                        p.op("vector", lambda e, sl=sl: e.tensor_scalar(qqa[:, sl], qqa[:, sl], -1.0, 1.0, op0=ALU.mult, op1=ALU.add),
                             reads=[qqa_b], writes=[qqa_b], tag="emq2")
                        p.op("vector", lambda e, sl=sl: e.tensor_tensor(wwa[:, sl], wwa[:, sl], emat[:, 2, :], op=ALU.mult),
                             reads=[wwa_b, emat_b], writes=[wwa_b], tag="emw1")
                        p.op("vector", lambda e, sl=sl: e.tensor_tensor(wwa[:, sl], wwa[:, sl], emat[:, 3, :], op=ALU.mult),
                             reads=[wwa_b, emat_b], writes=[wwa_b], tag="emw2")
                        p.op("vector", lambda e, sl=sl: e.tensor_scalar(wwa[:, sl], wwa[:, sl], 0.25, None, op0=ALU.mult),
                             reads=[wwa_b], writes=[wwa_b], tag="emw3")
                    for j in range(8):
                        bank, bank_b = ctx.take_bank()
                        for hh in range(2):
                            p.op("tensor", lambda e, hh=hh, bank=bank: e.matmul(
                                bank.ap()[0:1, :], ctx.ones_col[0:16, :], wwa[:, 512 * hh:512 * hh + 512],
                                start=True, stop=True),
                                 reads=[wwa_b, ctx.ones_buf], writes=[bank_b], tag="emsum")
                            p.op("vector", lambda e, hh=hh, bank=bank: e.tensor_copy(
                                krow[:, 512 * hh:512 * hh + 512], bank.ap()[0:1, :]),
                                 reads=[bank_b], writes=[krow_b], tag="emkev")
                        bank2, bank2_b = ctx.take_bank()
                        for hh in range(2):
                            p.op("tensor", lambda e, hh=hh, bank2=bank2: e.matmul(
                                bank2.ap()[:, 0:512], ctx.ones_row[0:1, :], krow[:, 512 * hh:512 * hh + 512],
                                start=True, stop=True),
                                 reads=[krow_b, ctx.onesr_buf], writes=[bank2_b], tag="emkbc")
                            p.op("vector", lambda e, j=j, hh=hh, bank2=bank2: e.tensor_copy(
                                kbc[:, j, 512 * hh:512 * hh + 512], bank2.ap()[:, 0:512]),
                                 reads=[bank2_b], writes=[kbc_b], tag="emkbce")
                        if j < 7:
                            p.op("vector", lambda e: e.tensor_tensor(wwa[:], wwa[:], qqa[:], op=ALU.mult),
                                 reads=[qqa_b, wwa_b], writes=[wwa_b], tag="emiter")
                mx, mx_b = ctx.sb("l3a_mx", [128, 8, 1024], BF16)
                cacc, cacc_b = ctx.sb("l3a_cacc", [128, 1024])
                ctmp, ctmp_b = ctx.sb("l3a_ctmp", [128, 1024])
                for s in range(8):
                    for j in range(s + 1):
                        if j == 0:
                            p.op("vector", lambda e, s=s: e.tensor_tensor(
                                cacc[:], xn[:, s, :], kbc[:, 0, :], op=ALU.mult),
                                 reads=[xn_b, kbc_b], writes=[cacc_b], tag="cv0")
                        else:
                            p.op("vector", lambda e, s=s, j=j: e.tensor_tensor(
                                ctmp[:], xn[:, s - j, :], kbc[:, j, :], op=ALU.mult),
                                 reads=[xn_b, kbc_b], writes=[ctmp_b], tag="cv1")
                            p.op("vector", lambda e: e.tensor_tensor(cacc[:], cacc[:], ctmp[:], op=ALU.add),
                                 reads=[ctmp_b, cacc_b], writes=[cacc_b], tag="cv2")
                    p.op("vector", lambda e, s=s: e.tensor_tensor(ctmp[:], xn[:, s, :], ombc[:], op=ALU.mult),
                         reads=[xn_b, ombc_b], writes=[ctmp_b], tag="cvom")
                    p.op("vector", lambda e: e.tensor_tensor(cacc[:], cacc[:], ctmp[:], op=ALU.add),
                         reads=[ctmp_b, cacc_b], writes=[cacc_b], tag="cvom2")
                    p.op("scalar", lambda e, s=s: e.activation(mx[:, s, :], cacc[:], AF.Silu, bias=0.0, scale=1.0),
                         reads=[cacc_b], writes=[mx_b], tag="cvsilu")
                for s in range(8):
                    for g in range(2):
                        bank, bank_b = ctx.take_bank()
                        bv = bank.ap().bitcast(BF16)
                        for j4 in range(4):
                            p.op("tensor", lambda e, s=s, g=g, j4=j4, bv=bv: e.transpose(
                                bv[:, 128 * j4:128 * j4 + 128], mx[:, s, 512 * g + 128 * j4:512 * g + 128 * j4 + 128],
                                ctx.ident[:]),
                                 reads=[mx_b, ctx.ident_buf], writes=[bank_b], tag="mxxp")
                        p.op("vector", lambda e, s=s, g=g, bv=bv: e.tensor_copy(
                            mxT[:, 4 * g:4 * g + 4, 128 * s:128 * s + 128],
                            bv[:, 0:512].rearrange("p (a b) -> p a b", b=128)),
                             reads=[bank_b], writes=[mxT_b], tag="mxxpe2")
                p.dma("sync", lambda e: e.dma_start(out=d_mxT[:], in_=mxT[:]),
                      reads=[mxT_b], writes=[d_mxT_b], tag="sp_mxT")
            with ctx.scope("l3a_vp"):
                xnT2, xnT2_b = ctx.sb("xnT3b", [128, 8, 1024], BF16)
                p.dma("sync", lambda e: e.dma_start(out=xnT2[:], in_=d_xnT[:]),
                      reads=[d_xnT_b], writes=[xnT2_b], tag="ld_xnT")
                wbuf, wbuf_b = ctx.sb("l3a_wbuf", [128, 8, 512], BF16)
                vstg, vstg_b = ctx.sb("l3a_vstg", [128, 512], BF16)
                for vh in range(4):
                    p.dma("gpsimd", lambda e, vh=vh: e.dma_start(
                        out=wbuf[:], in_=din["l3a_vwT"][:, 512 * vh:512 * vh + 512].rearrange("(c p) d -> p c d", p=128)),
                          writes=[wbuf_b], tag="ldvw")
                    for s in range(8):
                        bank, bank_b = ctx.take_bank()
                        mm = [(xnT2[:, c, 128 * s:128 * s + 128], wbuf[:, c, :],
                               [xnT2_b, wbuf_b]) for c in range(8)]
                        mm.append((ctx.ones_row_bf[0:1, :], vb_row[:, 512 * vh:512 * vh + 512],
                                   [ctx.onesrbf_buf, vb_row_b]))
                        ctx.matmul_acc(bank.ap(), bank_b, mm, tag="vmm3")
                        p.op("scalar", lambda e, bank=bank: e.activation(
                            vstg[:], bank.ap(), AF.Silu, bias=0.0, scale=1.0),
                             reads=[bank_b], writes=[vstg_b], tag="vev3")
                        p.dma("sync", lambda e, s=s, vh=vh: e.dma_start(
                            out=d_v[:, s, 512 * vh:512 * vh + 512], in_=vstg[:]),
                              reads=[vstg_b], writes=[d_v_b], tag="sp_v")
            with ctx.scope("l3a_base"):
                mxT2, mxT2_b = ctx.sb("mxT3b", [128, 8, 1024], BF16)
                p.dma("sync", lambda e: e.dma_start(out=mxT2[:], in_=d_mxT[:]),
                      reads=[d_mxT_b], writes=[mxT2_b], tag="ld_mxT")
                wbuf2, wbuf2_b = ctx.sb("l3a_wbuf2", [128, 8, 512], BF16)
                stg, stg_b = ctx.sb("l3a_stg", [128, 512], BF16)
                sections = [(0, 1024, d_u, d_u_b, AF.Sigmoid),
                            (1024, 1152, None, None, AF.Silu),
                            (1152, 3200, d_r, d_r_b, AF.Silu),
                            (3200, 4224, d_hx, d_hx_b, AF.Copy)]
                for (c0, c1, dsp, dsp_b, act) in sections:
                    for w0 in range(c0, c1, 512):
                        wd = min(512, c1 - w0)
                        p.dma("gpsimd", lambda e, w0=w0, wd=wd: e.dma_start(
                            out=wbuf2[:, :, 0:wd], in_=din["l3a_mxwT"][:, w0:w0 + wd].rearrange("(c p) d -> p c d", p=128)),
                              writes=[wbuf2_b], tag="ldmxw")
                        for s in range(8):
                            bank, bank_b = ctx.take_bank()
                            mm = [(mxT2[:, c, 128 * s:128 * s + 128], wbuf2[:, c, 0:wd],
                                   [mxT2_b, wbuf2_b]) for c in range(8)]
                            mm.append((ctx.ones_row_bf[0:1, :], mxb_row[:, w0:w0 + wd],
                                       [ctx.onesrbf_buf, mxb_row_b]))
                            ctx.matmul_acc(bank.ap()[:, 0:wd], bank_b, mm, tag="bsmm")
                            off = w0 - c0
                            if dsp is None:
                                p.op("scalar", lambda e, s=s, wd=wd, bank=bank: e.activation(
                                    zz[:, s, 0:wd], bank.ap()[:, 0:wd], AF.Silu, bias=0.0, scale=1.0),
                                     reads=[bank_b], writes=[zz_b], tag="bsevz")
                            else:
                                p.op("scalar", lambda e, wd=wd, bank=bank, act=act: e.activation(
                                    stg[:, 0:wd], bank.ap()[:, 0:wd], act, bias=0.0, scale=1.0),
                                     reads=[bank_b], writes=[stg_b], tag="bsevs")
                                p.dma("sync", lambda e, s=s, off=off, wd=wd, dsp=dsp: e.dma_start(
                                    out=dsp[:, s, off:off + wd], in_=stg[:, 0:wd]),
                                      reads=[stg_b], writes=[dsp_b], tag="spill")
            hpT, hpT_b = ctx.sb("hpT", [128, 16, 1024], BF16)
            with ctx.scope("l3a_attn"):
                rr_a, rr_a_b = ctx.sb("l3a_ra", [128, 8, 2048], BF16)
                p.dma("sync", lambda e: e.dma_start(out=rr_a[:], in_=d_r[:]),
                      reads=[d_r_b], writes=[rr_a_b], tag="ld_r")
                vv_a, vv_a_b = ctx.sb("l3a_va", [128, 8, 2048], BF16)
                p.dma("sync", lambda e: e.dma_start(out=vv_a[:], in_=d_v[:]),
                      reads=[d_v_b], writes=[vv_a_b], tag="ld_v")
                qq_, qq_b = ctx.sb("l3a_q", [128, 8, 128], BF16)
                kk_, kk_b = ctx.sb("l3a_k", [128, 8, 128], BF16)
                S64, S64_b = ctx.sb("l3a_S", [128, 8, 8])
                nm1, nm1_b = ctx.sb("l3a_nm", [128, 1])
                ss1, ss1_b = ctx.sb("l3a_ss", [128, 1])
                rs1, rs1_b = ctx.sb("l3a_rs", [128, 1])
                hacc, hacc_b = ctx.sb("l3a_hacc", [128, 2048])
                htmp, htmp_b = ctx.sb("l3a_htmp", [128, 2048], BF16)
                qtmp, qtmp_b = ctx.sb("l3a_qtmp", [128, 128])
                for s in range(8):
                    p.op("vector", lambda e, s=s: e.tensor_tensor(qq_[:, s, :], zz[:, s, :], g0bc[:], op=ALU.mult),
                         reads=[zz_b, g0bc_b], writes=[qq_b], tag="q3a")
                    p.op("vector", lambda e, s=s: e.tensor_tensor(qq_[:, s, :], qq_[:, s, :], b0bc[:], op=ALU.add),
                         reads=[qq_b, b0bc_b], writes=[qq_b], tag="q3b")
                    p.op("vector", lambda e, s=s: e.tensor_scalar(qq_[:, s, :], qq_[:, s, :], RT2, None, op0=ALU.mult),
                         reads=[qq_b], writes=[qq_b], tag="q3c")
                    p.op("vector", lambda e, s=s: e.tensor_tensor(kk_[:, s, :], zz[:, s, :], g1bc[:], op=ALU.mult),
                         reads=[zz_b, g1bc_b], writes=[kk_b], tag="k3a")
                    p.op("vector", lambda e, s=s: e.tensor_tensor(kk_[:, s, :], kk_[:, s, :], b1bc[:], op=ALU.add),
                         reads=[kk_b, b1bc_b], writes=[kk_b], tag="k3b")
                for s in range(8):
                    for t in range(8):
                        p.op("vector", lambda e, s=s, t=t: e.tensor_tensor(
                            qtmp[:], qq_[:, s, :], kk_[:, t, :], op=ALU.mult),
                             reads=[qq_b, kk_b], writes=[qtmp_b], tag="qk3")
                        p.op("vector", lambda e, s=s, t=t: e.tensor_reduce(
                            S64[:, s, t:t + 1], qtmp[:].rearrange("p (a b) -> p a b", a=1),
                            axis=AX.X, op=ALU.add),
                             reads=[qtmp_b], writes=[S64_b], tag="sred3")
                p.op("vector", lambda e: e.tensor_tensor(
                    S64[:].rearrange("p a b -> p (a b)"), S64[:].rearrange("p a b -> p (a b)"), biasbc[:], op=ALU.add),
                     reads=[S64_b, biasbc_b], writes=[S64_b], tag="sbias3")
                for s in range(8):
                    p.op("vector", lambda e, s=s: e.tensor_reduce(
                        nm1[:], S64[:, s, :], axis=AX.X, op=ALU.max, negate=True),
                         reads=[S64_b], writes=[nm1_b], tag="smax3")
                    p.op("scalar", lambda e, s=s: e.activation(
                        S64[:, s, :], S64[:, s, :], AF.Exp, bias=nm1[:], scale=1.0, accum_out=ss1[:]),
                         reads=[S64_b, nm1_b], writes=[S64_b, ss1_b], tag="sexp3")
                    p.op("vector", lambda e: e.reciprocal(rs1[:], ss1[:]),
                         reads=[ss1_b], writes=[rs1_b], tag="srec3")
                    p.op("vector", lambda e, s=s: e.tensor_scalar(
                        S64[:, s, :], S64[:, s, :], rs1[:], None, op0=ALU.mult),
                         reads=[S64_b, rs1_b], writes=[S64_b], tag="snorm3")
                    for t in range(8):
                        p.op("vector", lambda e, s=s, t=t: e.tensor_scalar(
                            htmp[:], vv_a[:, t, :], S64[:, s, t:t + 1], None, op0=ALU.mult),
                             reads=[vv_a_b, S64_b], writes=[htmp_b], tag="pv3")
                        if t == 0:
                            p.op("vector", lambda e: e.tensor_copy(hacc[:], htmp[:]),
                                 reads=[htmp_b], writes=[hacc_b], tag="pv3a")
                        else:
                            p.op("vector", lambda e: e.tensor_tensor(hacc[:], hacc[:], htmp[:], op=ALU.add),
                                 reads=[htmp_b, hacc_b], writes=[hacc_b], tag="pv3b")
                    p.op("vector", lambda e, s=s: e.tensor_tensor(rr_a[:, s, :], hacc[:], rr_a[:, s, :], op=ALU.mult),
                         reads=[hacc_b, rr_a_b], writes=[rr_a_b], tag="hr3")
                for s in range(8):
                    for g in range(4):
                        bank, bank_b = ctx.take_bank()
                        bv = bank.ap().bitcast(BF16)
                        for j4 in range(4):
                            p.op("tensor", lambda e, s=s, g=g, j4=j4, bv=bv: e.transpose(
                                bv[:, 128 * j4:128 * j4 + 128], rr_a[:, s, 512 * g + 128 * j4:512 * g + 128 * j4 + 128],
                                ctx.ident[:]),
                                 reads=[rr_a_b, ctx.ident_buf], writes=[bank_b], tag="hxp3")
                        p.op("vector", lambda e, s=s, g=g, bv=bv: e.tensor_copy(
                            hpT[:, 4 * g:4 * g + 4, 128 * s:128 * s + 128],
                            bv[:, 0:512].rearrange("p (a b) -> p a b", b=128)),
                             reads=[bank_b], writes=[hpT_b], tag="hxpe3")
            with ctx.scope("l3a_fin"):
                hw, hw_b = ctx.sb("l3a_hw", [128, 16, 512], BF16)
                h2s, h2s_b = ctx.sb("l3a_h2", [128, 1024])
                res, res_b = ctx.sb("l3a_res", [128, 1, 1024])
                dd, dd_b = ctx.sb("l3a_dd", [128, 1024])
                ustg, ustg_b = ctx.sb("l3a_ustg", [128, 1024], BF16)
                hxstg, hxstg_b = ctx.sb("l3a_hxstg", [128, 1024], BF16)
                h2h, h2h_b = ctx.sb("l3a_h2h", [128, 8, 1024])
                for dh in range(2):
                    p.dma("gpsimd", lambda e, dh=dh: e.dma_start(
                        out=hw[:], in_=din["l3a_hwT"][:, 512 * dh:512 * dh + 512].rearrange("(c p) d -> p c d", p=128)),
                          writes=[hw_b], tag="ldhw")
                    for s in range(8):
                        bank, bank_b = ctx.take_bank()
                        mm = [(hpT[:, fs, 128 * s:128 * s + 128], hw[:, fs, :],
                               [hpT_b, hw_b]) for fs in range(16)]
                        mm.append((ctx.ones_row_bf[0:1, :], hb_row[:, 512 * dh:512 * dh + 512],
                                   [ctx.onesrbf_buf, hb_row_b]))
                        ctx.matmul_acc(bank.ap(), bank_b, mm, tag="h2mm")
                        p.op("vector", lambda e, s=s, dh=dh, bank=bank: e.tensor_copy(
                            h2h[:, s, 512 * dh:512 * dh + 512], bank.ap()),
                             reads=[bank_b], writes=[h2h_b], tag="h2ev")
                for s in range(8):
                    p.dma("sync", lambda e, s=s: e.dma_start(out=hxstg[:], in_=d_hx[:, s, :]),
                          reads=[d_hx_b], writes=[hxstg_b], tag="unspill_hx")
                    p.dma("sync", lambda e, s=s: e.dma_start(out=ustg[:], in_=d_u[:, s, :]),
                          reads=[d_u_b], writes=[ustg_b], tag="unspill_u")
                    p.op("vector", lambda e, s=s: e.tensor_tensor(h2s[:], h2h[:, s, :], hxstg[:], op=ALU.add),
                         reads=[h2h_b, hxstg_b], writes=[h2s_b], tag="h2add")
                    p.op("scalar", lambda e: e.activation(h2s[:], h2s[:], AF.Silu, bias=0.0, scale=1.0),
                         reads=[h2s_b], writes=[h2s_b], tag="h2silu")
                    _ln_to_featmajor(ctx, x_t, x_bufs, None, None, tile_list=[8 * bt + s],
                                     xn_out=(res, [res_b]))
                    p.op("vector", lambda e, s=s: e.tensor_tensor(dd[:], h2s[:], res[:, 0, :], op=ALU.subtract),
                         reads=[h2s_b, res_b], writes=[dd_b], tag="fin1")
                    p.op("vector", lambda e, s=s: e.tensor_tensor(dd[:], dd[:], ustg[:], op=ALU.mult),
                         reads=[dd_b, ustg_b], writes=[dd_b], tag="fin2")
                    p.op("vector", lambda e, s=s: e.tensor_tensor(dd[:], dd[:], res[:, 0, :], op=ALU.add),
                         reads=[dd_b, res_b], writes=[dd_b], tag="fin3")
                    p.op("vector", lambda e, s=s, ti=8 * bt + s: e.tensor_tensor(
                        x_t[:, ti, :], x_t[:, ti, :], dd[:], op=ALU.add),
                         reads=[dd_b, x_bufs[8 * bt + s]], writes=[x_bufs[8 * bt + s]], tag="fin4")

    for bt in range(2):
        _bt_body(bt)



def _rope_tables():
    half = 32
    inv_freq = 1.0 / 10000.0 ** (np.arange(half, dtype=np.float64) * 2.0 / 64)
    ang = np.arange(2048, dtype=np.float64)[:, None] * inv_freq[None, :]
    return np.cos(ang).astype(np.float32), np.sin(ang).astype(np.float32)


def _rot_tables():
    import math
    half = 64
    freqs = np.exp(np.arange(half) * (-math.log(10000.0) / half))
    pos = np.arange(8)[:, None] * freqs[None, :]
    return np.cos(pos).astype(np.float32), np.sin(pos).astype(np.float32)


def _phase1_body(ctx, din, dout):
    ctx.causal_masks()
    x_t, x_bufs = load_stream(ctx, din["x"])
    with ctx.scope("B_l0a"):
        emit_L0_attn(ctx, din, x_t, x_bufs)
    with ctx.scope("B_l0f"):
        emit_L0_ffn(ctx, din, x_t, x_bufs)
    with ctx.scope("B_l1a"):
        emit_L1_attn(ctx, din, x_t, x_bufs)
    with ctx.scope("B_l1f"):
        emit_L1_ffn(ctx, din, x_t, x_bufs)
    with ctx.scope("B_l2a"):
        emit_L2_attn(ctx, din, x_t, x_bufs)
    store_stream(ctx, dout["out"], x_t, x_bufs)


def _phase2_body(ctx, din, dout):
    x_t, _ = ctx.sb("x", [128, 16, 1024])
    x_bufs = [ctx.prog.buf(f"x{i}") for i in range(16)]
    for bt in range(2):
        ctx.prog.dma("sync", lambda e, bt=bt: e.dma_start(
            out=x_t[:, 8 * bt:8 * bt + 8, :],
            in_=din["x2"][:, 128 * bt:128 * bt + 128, :].rearrange("s p d -> p s d")),
            writes=x_bufs[8 * bt:8 * bt + 8], tag="ldx")
    with ctx.scope("B_l2f"):
        emit_L2_ffn(ctx, din, x_t, x_bufs)
    with ctx.scope("B_l3a"):
        emit_L3_attn(ctx, din, x_t, x_bufs)
    with ctx.scope("B_l3f"):
        emit_L3_ffn(ctx, din, x_t, x_bufs)
    for bt in range(2):
        ctx.prog.dma("sync", lambda e, bt=bt: e.dma_start(
            out=dout["out"][:, 128 * bt:128 * bt + 128, :].rearrange("s p d -> p s d"),
            in_=x_t[:, 8 * bt:8 * bt + 8, :]),
            reads=x_bufs[8 * bt:8 * bt + 8], tag="stx")


_CACHE = {}


def _phase1_weights(I):
    C = np.ascontiguousarray
    sel8 = np.zeros((8, 128), np.float32)
    sel8[np.arange(128) // 16, np.arange(128)] = 1
    sel16 = np.zeros((16, 128), np.float32)
    sel16[np.arange(128) % 16, np.arange(128)] = 1
    cos, sin = _rope_tables()
    g1p = C(np.stack([I["L0_ffn_G1_R"], I["L0_ffn_G1_I"]], 0).reshape(2, 32, 4, 4, 8).transpose(4, 0, 2, 3, 1))
    g2p = C(np.stack([I["L0_ffn_G2_R"], I["L0_ffn_G2_I"]], 0).reshape(2, 4, 16, 16).transpose(3, 0, 1, 2))
    return {
        "l0a_wdqT": C(I["L0_attn_W_DQ"].T), "l0a_wuqT": C(I["L0_attn_W_UQ"].T),
        "l0a_wdkvT": C(I["L0_attn_W_DKV"].T), "l0a_wukvT": C(I["L0_attn_W_UKV"].T),
        "l0a_wqrT": C(I["L0_attn_W_QR"].T), "l0a_wkrT": C(I["L0_attn_W_KR"].T),
        "l0a_woT": C(I["L0_attn_W_O"].T),
        "l0a_normw": C(np.stack([I["L0_attn_q_norm_w"], I["L0_attn_k_norm_w"]], 1)),
        "l0f_g1p": g1p, "l0f_g2p": g2p,
        "l0f_br": I["L0_ffn_bias_real"].reshape(1, -1), "l0f_bi": I["L0_ffn_bias_imag"].reshape(1, -1),
        "l0f_sel8": sel8, "l0f_sel16": sel16,
        "l1a_wq": I["L1_attn_Wq"], "l1a_wk": I["L1_attn_Wk"],
        "l1a_wv": I["L1_attn_Wv"], "l1a_wp": I["L1_attn_Wp"],
        "l1a_bq": I["L1_attn_bq"].reshape(1, -1), "l1a_bk": I["L1_attn_bk"].reshape(1, -1),
        "l1a_bv": I["L1_attn_bv"].reshape(1, -1), "l1a_bp": I["L1_attn_bp"].reshape(1, -1),
        "ropecos": cos, "ropesin": sin,
        "l1f_gate_w": I["L1_ffn_gate_w"], "l1f_gate_b": I["L1_ffn_gate_b"].reshape(1, -1),
        "l1f_w1": I["L1_ffn_w1"],
        "l1f_b1c": C(I["L1_ffn_b1"].reshape(8, 32, 128).transpose(2, 0, 1)),
        "l1f_w2": I["L1_ffn_w2"], "l1f_b2": I["L1_ffn_b2"],
        "l2a_params": I["L2_attn_params"],
        "l2a_wvT": C(I["L2_attn_Wv"].T), "l2a_woT": C(I["L2_attn_Wo"].T),
        "l2a_bv": I["L2_attn_bv"].reshape(1, -1), "l2a_bo": I["L2_attn_bo"].reshape(1, -1),
    }


def _phase2_weights(I):
    C = np.ascontiguousarray
    rcos, rsin = _rot_tables()
    return {
        "l2f_ln1w": I["L2_ffn_ln1_w"].reshape(1, -1), "l2f_ln1b": I["L2_ffn_ln1_b"].reshape(1, -1),
        "l2f_ln2w": I["L2_ffn_ln2_w"].reshape(1, -1), "l2f_ln2b": I["L2_ffn_ln2_b"].reshape(1, -1),
        "l2f_inprojT": C(I["L2_ffn_in_proj_w"].T), "l2f_inprojb": I["L2_ffn_in_proj_b"].reshape(1, -1),
        "l2f_outprojT": C(I["L2_ffn_out_proj_w"].T), "l2f_outprojb": I["L2_ffn_out_proj_b"].reshape(1, -1),
        "l2f_routerT": C(I["L2_ffn_router_w"].T),
        "l2f_w1": I["L2_ffn_w1"], "l2f_w2": I["L2_ffn_w2"],
        "l3a_lnw": I["L3_attn_ln_w"].reshape(1, -1), "l3a_lnb": I["L3_attn_ln_b"].reshape(1, -1),
        "l3a_vwT": C(I["L3_attn_v_w"].T), "l3a_vb": I["L3_attn_v_b"].reshape(1, -1),
        "l3a_mxwT": C(I["L3_attn_mx_w"].T), "l3a_mxb": I["L3_attn_mx_b"].reshape(1, -1),
        "l3a_hwT": C(I["L3_attn_h_w"].T), "l3a_hb": I["L3_attn_h_b"].reshape(1, -1),
        "l3a_gambet": C(np.concatenate([I["L3_attn_qk_gamma"][0], I["L3_attn_qk_gamma"][1],
                                        I["L3_attn_qk_beta"][0], I["L3_attn_qk_beta"][1]]).reshape(1, -1)),
        "l3a_emaT": C(np.concatenate([I["L3_attn_ema_delta"][:, :, 0].T, I["L3_attn_ema_alpha"][:, :, 0].T,
                                      I["L3_attn_ema_beta"][:, :, 0].T, I["L3_attn_ema_gamma"].T], axis=1)),
        "l3a_omega": I["L3_attn_ema_omega"].reshape(1, -1),
        "l3a_rb": C(np.concatenate([I["L3_attn_rb_alpha"][0], I["L3_attn_rb_beta"][0]]).reshape(1, -1)),
        "l3a_rcos": rcos, "l3a_rsin": rsin,
        "l3f_win": I["L3_ffn_Win"], "l3f_bin": I["L3_ffn_bin_"].reshape(1, -1),
        "l3f_w1": I["L3_ffn_W1"].reshape(1, -1), "l3f_b1": I["L3_ffn_b1"].reshape(1, -1),
        "l3f_w2": I["L3_ffn_W2"].reshape(1, -1), "l3f_b2": I["L3_ffn_b2"].reshape(1, -1),
        "l3f_wout": I["L3_ffn_Wout"], "l3f_bout": I["L3_ffn_bout"].reshape(1, -1),
    }


def kernel(**inputs):
    I = inputs
    w1 = _phase1_weights(I)
    w2 = _phase2_weights(I)
    x = np.ascontiguousarray(I["L0_attn_x"]).astype(np.float32, copy=False)

    if "p1" not in _CACHE:
        L1_ = Launch("p1")
        specs1 = {k: v.shape for k, v in w1.items()}
        specs1["x"] = (2048, 1024)
        L1_.build(_phase1_body, specs1, {"out": (2048, 1024)})
        _CACHE["p1"] = L1_
    if "p2" not in _CACHE:
        L2_ = Launch("p2")
        specs2 = {k: v.shape for k, v in w2.items()}
        specs2["x2"] = (8, 256, 1024)
        L2_.build(_phase2_body, specs2, {"out": (8, 256, 1024)})
        _CACHE["p2"] = L2_

    in_maps1 = []
    for c in range(NCORES):
        m = dict(w1)
        m["x"] = np.ascontiguousarray(x[c])
        in_maps1.append(m)
    res1 = run_bass_kernel_spmd(_CACHE["p1"].nc, in_maps1, core_ids=list(range(NCORES)))
    mid = np.stack([r["out"] for r in res1.results], 0)

    in_maps2 = []
    for c in range(NCORES):
        m = dict(w2)
        m["x2"] = np.ascontiguousarray(mid[:, 256 * c:256 * c + 256, :])
        in_maps2.append(m)
    res2 = run_bass_kernel_spmd(_CACHE["p2"].nc, in_maps2, core_ids=list(range(NCORES)))
    out = np.concatenate([r["out"].reshape(8, 256, 1024) for r in res2.results], axis=1)
    return out.astype(np.float32)


# version 02
# speedup vs baseline: 1.2682x; 1.2682x over previous
import contextlib
import numpy as np
import ml_dtypes
import concourse.bass as bass
import concourse.mybir as mybir
from concourse.bass_utils import run_bass_kernel_spmd

F32 = mybir.dt.float32
BF16 = mybir.dt.bfloat16
AF = mybir.ActivationFunctionType
ALU = mybir.AluOpType
AX = mybir.AxisListType

ENGINES = ("tensor", "vector", "scalar", "gpsimd", "sync")
DRAIN_ENGINES = {"vector", "scalar", "gpsimd"}


class Buf:
    __slots__ = ("name", "last_writer", "readers", "is_psum")

    def __init__(self, name, is_psum=False):
        self.name = name
        self.last_writer = None
        self.readers = []
        self.is_psum = is_psum

    def __repr__(self):
        return f"Buf({self.name})"


class Step:
    __slots__ = ("eng", "emit", "reads", "writes", "tag", "kind", "idx",
                 "needs_inc", "inc_index", "waits", "drain_before")

    def __init__(self, eng, emit, reads, writes, tag, kind):
        self.eng = eng
        self.emit = emit
        self.reads = reads
        self.writes = writes
        self.tag = tag
        self.kind = kind
        self.idx = -1
        self.needs_inc = False
        self.inc_index = None
        self.waits = []
        self.drain_before = False


class Prog:
    def __init__(self, nc):
        self.nc = nc
        self.steps = []
        self.bufs = []
        self.n_barriers = 0

    def buf(self, name, is_psum=False):
        b = Buf(name, is_psum)
        self.bufs.append(b)
        return b

    def op(self, eng, emit, reads=(), writes=(), tag=""):
        s = Step(eng, emit, list(reads), list(writes), tag, "op")
        self.steps.append(s)
        return s

    def dma(self, queue, emit, reads=(), writes=(), tag=""):
        s = Step(queue, emit, list(reads), list(writes), tag, "dma")
        self.steps.append(s)
        return s

    def barrier(self, tag="barrier"):
        s = Step(None, None, [], [], tag, "barrier")
        self.steps.append(s)
        self.n_barriers += 1
        return s

    def finalize(self):
        drained_upto = {e: -1 for e in ENGINES}
        last_op_idx = {e: -1 for e in ENGINES}
        ensured = {e: {} for e in ENGINES}

        for i, s in enumerate(self.steps):
            s.idx = i

        for s in self.steps:
            if s.kind == "barrier":
                for b in self.bufs:
                    b.last_writer = None
                    b.readers = []
                for e in ENGINES:
                    drained_upto[e] = s.idx
                    ensured[e] = {"_barrier": s.idx}
                continue

            deps = {}
            for b in s.reads:
                if b.last_writer is not None:
                    deps[b.last_writer.idx] = (b.last_writer, True)
                if b.is_psum and s.eng in ("vector", "scalar"):
                    for r in b.readers:
                        if r.eng != s.eng and r.eng in ("vector", "scalar") and r.kind == "op":
                            deps.setdefault(r.idx, (r, False))
            for b in s.writes:
                if b.last_writer is not None:
                    deps.setdefault(b.last_writer.idx, (b.last_writer, False))
                for r in b.readers:
                    deps.setdefault(r.idx, (r, False))

            for d, is_raw in deps.values():
                if d is s:
                    continue
                if d.kind == "dma":
                    d.needs_inc = True
                    s.waits.append(d)
                elif d.eng == s.eng:
                    if s.eng == "tensor" or not is_raw:
                        pass
                    else:
                        if d.idx > drained_upto[s.eng]:
                            s.drain_before = True
                else:
                    d.needs_inc = True
                    s.waits.append(d)

            if s.drain_before:
                drained_upto[s.eng] = last_op_idx[s.eng]
            if s.kind == "op":
                last_op_idx[s.eng] = s.idx

            for b in s.reads:
                b.readers.append(s)
            for b in s.writes:
                b.last_writer = s
                b.readers = []

        DMA_POOL = 8
        self.DMA_POOL = DMA_POOL
        self.prog_counts = {e: 0 for e in ENGINES}
        self.dma_counts = {e: 0 for e in ENGINES}
        for s in self.steps:
            if s.kind == "dma":
                j = self.dma_counts[s.eng]
                self.dma_counts[s.eng] += 1
                slot = j % DMA_POOL
                rnd = j // DMA_POOL + 1
                s.inc_index = (slot, rnd)
                s.needs_inc = True
            elif s.kind == "op" and s.needs_inc:
                self.prog_counts[s.eng] += 1
                s.inc_index = self.prog_counts[s.eng]
            elif s.kind == "barrier":
                for e in ENGINES:
                    self.prog_counts[e] += 1
                s.inc_index = (dict(self.prog_counts), dict(self.dma_counts))

        ensured = {e: {} for e in ENGINES}

        def need_wait(eng, key, val):
            cur = ensured[eng].get(key, 0)
            if val <= cur:
                return False
            ensured[eng][key] = val
            return True

        def dma_keys_at(counts, eng):
            out = []
            n = counts[eng]
            for slot in range(min(n, DMA_POOL)):
                rounds = (n - slot - 1) // DMA_POOL + 1
                out.append((("d", eng, slot), 16 * rounds))
            return out

        self._dma_keys_at = dma_keys_at
        for s in self.steps:
            if s.kind == "barrier":
                pc, dc = s.inc_index
                for e in ENGINES:
                    for e2 in ENGINES:
                        ensured[e][("p", e2)] = pc[e2]
                        for key, val in dma_keys_at(dc, e2):
                            ensured[e][key] = val
                continue
            resolved = []
            if s.kind == "dma":
                slot, rnd = s.inc_index
                if rnd > 1 and need_wait(s.eng, ("d", s.eng, slot), 16 * (rnd - 1)):
                    resolved.append((("d", s.eng, slot), 16 * (rnd - 1)))
            for d in s.waits:
                if d.kind == "dma":
                    slot, rnd = d.inc_index
                    key = ("d", d.eng, slot)
                    val = 16 * rnd
                else:
                    key = ("p", d.eng)
                    val = d.inc_index
                if need_wait(s.eng, key, val):
                    resolved.append((key, val))
            s.waits = resolved

    def emit(self, block, prog_sems, dma_sems):

        def sem_for(key):
            if key[0] == "p":
                return prog_sems[key[1]]
            return dma_sems[key[1]][key[2]]

        def emit_for(eng_name, eng):
            for s in self.steps:
                if s.kind == "barrier":
                    pc, dc = s.inc_index
                    if eng_name != "sync":
                        eng.drain()
                    eng.nop().then_inc(prog_sems[eng_name], 1)
                    for e2 in ENGINES:
                        if e2 != eng_name:
                            eng.wait_ge(prog_sems[e2], pc[e2])
                        for key, val in self._dma_keys_at(dc, e2):
                            eng.wait_ge(sem_for(key), val)
                    continue
                if s.eng != eng_name:
                    continue
                if s.drain_before:
                    eng.drain()
                for key, val in s.waits:
                    eng.wait_ge(sem_for(key), val)
                inst = s.emit(eng)
                if s.kind == "dma":
                    slot, _ = s.inc_index
                    inst.then_inc(dma_sems[eng_name][slot], 16)
                elif s.needs_inc:
                    if inst is None:
                        raise RuntimeError(f"step {s.tag} needs inc but emitted no instruction")
                    inst.then_inc(prog_sems[eng_name], 1)

        @block.tensor
        def _(e):
            emit_for("tensor", e)

        @block.vector
        def _(e):
            emit_for("vector", e)

        @block.scalar
        def _(e):
            emit_for("scalar", e)

        @block.gpsimd
        def _(e):
            emit_for("gpsimd", e)

        @block.sync
        def _(e):
            emit_for("sync", e)


NCORES = 8
B, T, D = 8, 2048, 1024
TPC = T
NTILES = 16


class Ctx:

    def __init__(self, nc, prog, stack):
        self.nc = nc
        self.prog = prog
        self.stack = stack
        self._scopes = [stack]
        self.name_i = 0
        self.banks = []
        for i in range(8):
            t = stack.enter_context(nc.psum_tensor(f"bank{i}", [128, 512], F32))
            self.banks.append((t, prog.buf(f"bank{i}", is_psum=True)))
        self._bank_rr = 0
        self.ident = stack.enter_context(nc.sbuf_tensor("ident", [128, 128], BF16))
        self.ident_buf = prog.buf("ident")
        self.ident32 = stack.enter_context(nc.sbuf_tensor("ident32", [128, 128], F32))
        self.ident32_buf = prog.buf("ident32")
        self.ones_col = stack.enter_context(nc.sbuf_tensor("ones_col", [128, 1], F32))
        self.ones_buf = prog.buf("ones_col")
        self.ones_row = stack.enter_context(nc.sbuf_tensor("ones_row", [1, 128], F32))
        self.onesr_buf = prog.buf("ones_row")
        self.ones_row_bf = stack.enter_context(nc.sbuf_tensor("ones_row_bf", [1, 128], BF16))
        self.onesrbf_buf = prog.buf("ones_row_bf")
        self.consts = stack.enter_context(nc.sbuf_tensor("consts", [128, 8], F32))
        self.consts_buf = prog.buf("consts")
        self.const_vals = {}
        p = prog
        p.op("gpsimd", lambda e: e.memset(self.ident[:], 0.0), writes=[self.ident_buf], tag="ident0")
        p.op("gpsimd", lambda e: e.affine_select(
            out=self.ident[:], in_=self.ident[:], compare_op=ALU.not_equal,
            fill=1.0, base=0, pattern=[[-1, 128]], channel_multiplier=1),
            reads=[self.ident_buf], writes=[self.ident_buf], tag="ident1")
        p.op("gpsimd", lambda e: e.memset(self.ident32[:], 0.0), writes=[self.ident32_buf], tag="id32a")
        p.op("gpsimd", lambda e: e.affine_select(
            out=self.ident32[:], in_=self.ident32[:], compare_op=ALU.not_equal,
            fill=1.0, base=0, pattern=[[-1, 128]], channel_multiplier=1),
            reads=[self.ident32_buf], writes=[self.ident32_buf], tag="id32b")
        p.op("gpsimd", lambda e: e.memset(self.ones_col[:], 1.0), writes=[self.ones_buf], tag="ones")
        p.op("gpsimd", lambda e: e.memset(self.ones_row[:], 1.0), writes=[self.onesr_buf], tag="onesr")
        p.op("gpsimd", lambda e: e.memset(self.ones_row_bf[:], 1.0), writes=[self.onesrbf_buf], tag="onesrbf")

    def causal_masks(self):
        if hasattr(self, "_masks"):
            return self._masks
        masks, masks_b = self.sb("cmask", [128, 4, 512], BF16)
        for m in range(4):
            self.prog.op("gpsimd", lambda e, m=m: e.memset(masks[:, m, :], 0.0),
                         writes=[masks_b], tag="mask0")
            self.prog.op("gpsimd", lambda e, m=m: e.affine_select(
                out=masks[:, m, :], in_=masks[:, m, :], compare_op=ALU.is_ge,
                fill=-1e30, base=128 * m, pattern=[[-1, 512]], channel_multiplier=1),
                reads=[masks_b], writes=[masks_b], tag="mask1")
        self._masks = (masks, masks_b)
        return self._masks

    def const_col(self, val):
        if val in self.const_vals:
            return self.const_vals[val]
        i = len(self.const_vals)
        assert i < 8
        ap = self.consts[:, i:i + 1]
        self.prog.op("gpsimd", lambda e, ap=ap, v=val: e.memset(ap, v),
                     writes=[self.consts_buf], tag=f"const{val}")
        self.const_vals[val] = ap
        return ap

    def uname(self, base):
        self.name_i += 1
        return f"{base}_{self.name_i}"

    def sb(self, base, shape, dtype=F32):
        t = self._scopes[-1].enter_context(self.nc.sbuf_tensor(self.uname(base), shape, dtype))
        return t, self.prog.buf(base)

    def ln_scratch(self, d):
        if not hasattr(self, "_ln_cache"):
            self._ln_cache = {}
        key = (len(self._scopes), d)
        if key not in self._ln_cache:
            negmu = self.sb("negmu", [128, 1])
            ssq = self.sb("ssq", [128, 1])
            rstd = self.sb("rstd", [128, 1])
            sqscr = self.sb("sqscr", [128, d])
            xbf = self.sb("xbf", [128, d], BF16)
            self._ln_cache[key] = (negmu, ssq, rstd, sqscr, xbf)
        return self._ln_cache[key]

    @contextlib.contextmanager
    def scope(self, tag="scope"):
        st = contextlib.ExitStack()
        self._scopes.append(st)
        try:
            yield
        finally:
            depth = len(self._scopes)
            self._scopes.pop()
            if hasattr(self, "_ln_cache"):
                for k in [k for k in self._ln_cache if k[0] >= depth]:
                    del self._ln_cache[k]
            self.prog.barrier(f"scope_end_{tag}")
            st.close()

    def take_bank(self, hold=False):
        if not hasattr(self, "_held"):
            self._held = set()
        for _ in range(16):
            i = self._bank_rr % 8
            self._bank_rr += 1
            if i not in self._held:
                t, b = self.banks[i]
                if hold:
                    self._held.add(i)
                    return t, b, i
                return t, b
        raise RuntimeError("all banks held")

    def release_bank(self, i):
        self._held.discard(i)

    def matmul_acc(self, psum_ap, psum_buf, mms, tag="mm", extra_reads=()):
        n = len(mms)
        for j, (lhsT, rhs, rbufs) in enumerate(mms):
            st = (j == 0)
            sp = (j == n - 1)
            self.prog.op(
                "tensor",
                lambda e, lhsT=lhsT, rhs=rhs, st=st, sp=sp: e.matmul(psum_ap, lhsT, rhs, start=st, stop=sp),
                reads=list(rbufs) + list(extra_reads),
                writes=[psum_buf], tag=tag)

    def evac(self, dst_ap, dst_buf, src_ap, src_buf, eng="vector", tag="evac"):
        self.prog.op(eng, lambda e: e.tensor_copy(dst_ap, src_ap),
                     reads=[src_buf], writes=[dst_buf], tag=tag)

    def ln_tiles(self, x_t, x_bufs, xn_t, xn_bufs, ntiles=NTILES, eps=1e-5, d=1024, dbg=None):
        p = self.prog
        negmu, negmu_b = self.sb("negmu", [128, 1])
        ssq, ssq_b = self.sb("ssq", [128, 1])
        rstd, rstd_b = self.sb("rstd", [128, 1])
        sqscr, sqscr_b = self.sb("sqscr", [128, d])
        epsc = self.const_col(eps)
        for i in range(ntiles):
            p.op("vector", lambda e, i=i: e.tensor_reduce(
                negmu[:], x_t[:, i, :], axis=AX.X, op=ALU.add, negate=True),
                 reads=[x_bufs[i]], writes=[negmu_b], tag="lnsum")
            p.op("vector", lambda e: e.tensor_scalar(
                negmu[:], negmu[:], 1.0 / d, None, op0=ALU.mult),
                 reads=[negmu_b], writes=[negmu_b], tag="lnmu")
            p.op("scalar", lambda e, i=i: e.activation(
                sqscr[:], x_t[:, i, :], AF.Square, bias=negmu[:], scale=1.0, accum_out=ssq[:]),
                 reads=[x_bufs[i], negmu_b], writes=[sqscr_b, ssq_b], tag="lnsq")
            p.op("scalar", lambda e: e.activation(rstd[:], ssq[:], AF.Ln, bias=epsc, scale=1.0 / d),
                 reads=[ssq_b, self.consts_buf], writes=[rstd_b], tag="lnln")
            p.op("scalar", lambda e: e.activation(rstd[:], rstd[:], AF.Exp, bias=0.0, scale=-0.5),
                 reads=[rstd_b], writes=[rstd_b], tag="lnexp")
            p.op("vector", lambda e, i=i: e.tensor_scalar(
                xn_t[:, i, :], x_t[:, i, :], negmu[:], rstd[:], op0=ALU.add, op1=ALU.mult),
                 reads=[x_bufs[i], negmu_b, rstd_b], writes=[xn_bufs[i]], tag="lnapply")
            if dbg is not None:
                dt, db = dbg
                p.op("vector", lambda e, i=i: e.tensor_copy(dt[:, i, 0:1], negmu[:]),
                     reads=[negmu_b], writes=[db], tag="dbgmv")
                p.op("vector", lambda e, i=i: e.tensor_copy(dt[:, i, 2:3], rstd[:]),
                     reads=[rstd_b], writes=[db], tag="dbgrstd")

    def transpose_to_featmajor(self, xn_t, xn_bufs, xnT_t, xnT_buf, ntiles=NTILES, nchunk=8, cast_src=None):
        p = self.prog
        xbf, xbf_b = self.sb("xbf", [128, nchunk * 128], BF16)
        for i in range(ntiles):
            p.op("vector", lambda e, i=i: e.tensor_copy(xbf[:], xn_t[:, i, :]),
                 reads=[xn_bufs[i]], writes=[xbf_b], tag="castbf")
            for g in range(0, nchunk, 4):
                bank, bank_b = self.take_bank()
                bview = bank.ap().bitcast(BF16)
                for c in range(g, min(g + 4, nchunk)):
                    p.op("tensor", lambda e, c=c, g=g, bview=bview: e.transpose(
                        bview[:, 128 * (c - g):128 * (c - g) + 128],
                        xbf[:, 128 * c:128 * c + 128], self.ident[:]),
                         reads=[xbf_b, self.ident_buf], writes=[bank_b], tag="xpose")
                nc_ = min(g + 4, nchunk) - g
                p.op("vector", lambda e, i=i, g=g, nc_=nc_, bview=bview: e.tensor_copy(
                    xnT_t[:, g:g + nc_, 128 * i:128 * i + 128],
                    bview[:, 0:128 * nc_].rearrange("p (a b) -> p a b", b=128)),
                     reads=[bank_b], writes=[xnT_buf], tag="xpevac")


class Launch:

    def __init__(self, name="prog"):
        self.nc = bass.Bass("TRN2", target_bir_lowering=False, debug=False)
        self.stack = contextlib.ExitStack()
        self.prog = Prog(self.nc)
        self.name = name

    def build(self, body, in_specs, out_specs):
        nc = self.nc
        self.din = {n: nc.dram_tensor(n, list(s), F32, kind="ExternalInput").ap()
                    for n, s in in_specs.items()}
        self.dout = {n: nc.dram_tensor(n, list(s), F32, kind="ExternalOutput").ap()
                     for n, s in out_specs.items()}
        ctx = Ctx(nc, self.prog, self.stack)
        body(ctx, self.din, self.dout)
        self.prog.barrier("final")
        self.prog.finalize()
        prog_sems = {e: self.stack.enter_context(nc.semaphore(f"ps_{e}")) for e in ENGINES}
        dma_sems = {e: [self.stack.enter_context(nc.semaphore(f"ds_{e}_{i}"))
                        for i in range(self.prog.DMA_POOL)] for e in ENGINES}
        with nc.Block() as block:
            self.prog.emit(block, prog_sems, dma_sems)
        self.stack.close()
        return self.nc



def load_stream(ctx, dram_ap, ntiles=NTILES, d=1024, name="x"):
    t, _ = ctx.sb(name, [128, ntiles, d])
    bufs = [ctx.prog.buf(f"{name}{i}") for i in range(ntiles)]
    src = dram_ap.rearrange("(n p) d -> p n d", p=128)
    ctx.prog.dma("sync", lambda e: e.dma_start(out=t[:], in_=src),
                 writes=bufs, tag=f"ld_{name}")
    return t, bufs


def store_stream(ctx, dram_ap, t, bufs, ntiles=NTILES, d=1024):
    dst = dram_ap.rearrange("(n p) d -> p n d", p=128)
    ctx.prog.dma("sync", lambda e: e.dma_start(out=dst, in_=t[:]),
                 reads=bufs, tag="st_stream")


def ln_to_featmajor(ctx, x_t, x_bufs, xnT, xnT_buf, ntiles=NTILES, d=1024, eps=1e-5,
                    xn_out=None, affine=None, tile_list=None, dst_off=0):
    with ctx.scope("ln"):
        _ln_to_featmajor(ctx, x_t, x_bufs, xnT, xnT_buf, ntiles, d, eps, xn_out, affine, tile_list, dst_off)


def _ln_to_featmajor(ctx, x_t, x_bufs, xnT, xnT_buf, ntiles=NTILES, d=1024, eps=1e-5,
                    xn_out=None, affine=None, tile_list=None, dst_off=0):
    p = ctx.prog
    nchunk = d // 128
    (negmu, negmu_b), (ssq, ssq_b), (rstd, rstd_b), (sqscr, sqscr_b), (xbf, xbf_b) = ctx.ln_scratch(d)
    if xn_out is None:
        xn_t = sqscr.ap().rearrange("p (a b) -> p a b", a=1)
        xn_bufs = [sqscr_b]
        mode_scratch = True
    else:
        xn_t, xn_bufs = xn_out
        mode_scratch = False
    epsc = ctx.const_col(eps)
    if tile_list is None:
        tile_list = list(range(ntiles))
    for ii, i in enumerate(tile_list):
        j = 0 if mode_scratch else (ii if xn_t.shape[1] < ntiles else i)
        xnb = xn_bufs[j if j < len(xn_bufs) else 0]
        p.op("vector", lambda e, i=i: e.tensor_reduce(
            negmu[:], x_t[:, i, :], axis=AX.X, op=ALU.add, negate=True),
             reads=[x_bufs[i]], writes=[negmu_b], tag="lnsum")
        p.op("vector", lambda e: e.tensor_scalar(
            negmu[:], negmu[:], 1.0 / d, None, op0=ALU.mult),
             reads=[negmu_b], writes=[negmu_b], tag="lnmu")
        p.op("scalar", lambda e, i=i: e.activation(
            sqscr[:], x_t[:, i, :], AF.Square, bias=negmu[:], scale=1.0, accum_out=ssq[:]),
             reads=[x_bufs[i], negmu_b], writes=[sqscr_b, ssq_b], tag="lnsq")
        p.op("scalar", lambda e: e.activation(rstd[:], ssq[:], AF.Ln, bias=epsc, scale=1.0 / d),
             reads=[ssq_b, ctx.consts_buf], writes=[rstd_b], tag="lnln")
        p.op("scalar", lambda e: e.activation(rstd[:], rstd[:], AF.Exp, bias=0.0, scale=-0.5),
             reads=[rstd_b], writes=[rstd_b], tag="lnexp")
        p.op("vector", lambda e, i=i, j=j: e.tensor_scalar(
            xn_t[:, j, :], x_t[:, i, :], negmu[:], rstd[:], op0=ALU.add, op1=ALU.mult),
             reads=[x_bufs[i], negmu_b, rstd_b], writes=[xnb], tag="lnapply")
        if affine is not None:
            wbc, wbc_b, bbc, bbc_b = affine
            p.op("vector", lambda e, j=j: e.tensor_tensor(xn_t[:, j, :], xn_t[:, j, :], wbc[:], op=ALU.mult),
                 reads=[xnb, wbc_b], writes=[xnb], tag="lnaffw")
            p.op("vector", lambda e, j=j: e.tensor_tensor(xn_t[:, j, :], xn_t[:, j, :], bbc[:], op=ALU.add),
                 reads=[xnb, bbc_b], writes=[xnb], tag="lnaffb")
        if xnT is None:
            continue
        p.op("vector", lambda e, j=j: e.tensor_copy(xbf[:], xn_t[:, j, :]),
             reads=[xnb], writes=[xbf_b], tag="castbf")
        for g in range(0, nchunk, 4):
            bank, bank_b = ctx.take_bank()
            bview = bank.ap().bitcast(BF16)
            nc_ = min(g + 4, nchunk) - g
            for c in range(g, g + nc_):
                p.op("tensor", lambda e, c=c, g=g, bview=bview: e.transpose(
                    bview[:, 128 * (c - g):128 * (c - g) + 128],
                    xbf[:, 128 * c:128 * c + 128], ctx.ident[:]),
                     reads=[xbf_b, ctx.ident_buf], writes=[bank_b], tag="xpose")
            p.op("vector", lambda e, ii=ii, g=g, nc_=nc_, bview=bview: e.tensor_copy(
                xnT[:, g:g + nc_, 128 * (ii + dst_off):128 * (ii + dst_off) + 128],
                bview[:, 0:128 * nc_].rearrange("p (a b) -> p a b", b=128)),
                 reads=[bank_b], writes=[xnT_buf], tag="xpevac")


def emit_L1_ffn(ctx, din, x_t, x_bufs, want_dbg=False):
    p = ctx.prog
    NE, DFF = 8, 4096
    xnT, _x = ctx.sb("xnT", [128, 8, 2048], BF16)
    xnT_buf = ctx.prog.buf("xnT")
    ln_to_featmajor(ctx, x_t, x_bufs, xnT, xnT_buf)

    gw, gw_b = ctx.sb("gw", [128, 8, 8], BF16)
    p.dma("gpsimd", lambda e: e.dma_start(out=gw[:], in_=din["l1f_gate_w"].rearrange("(c p) n -> p c n", p=128)),
          writes=[gw_b], tag="ld_gw")
    gb, gb_b = ctx.sb("gb", [1, 8])
    p.dma("sync", lambda e: e.dma_start(out=gb[:], in_=din["l1f_gate_b"]), writes=[gb_b], tag="ld_gb")
    b2r, b2r_b = ctx.sb("b2r", [8, 1024])
    p.dma("sync", lambda e: e.dma_start(out=b2r[:], in_=din["l1f_b2"]), writes=[b2r_b], tag="ld_b2")
    b1c, b1c_b = ctx.sb("b1c", [128, 8, 32])
    p.dma("sync", lambda e: e.dma_start(out=b1c[:], in_=din["l1f_b1c"]), writes=[b1c_b], tag="ld_b1")

    gates, gates_b = ctx.sb("gates", [128, 16, 8])
    lgt, lgt_b = ctx.sb("lgt", [128, 8])
    m1, m1_b = ctx.sb("m1", [128, 1])
    m2, m2_b = ctx.sb("m2", [128, 1])
    eq1, eq1_b = ctx.sb("eq1", [128, 8])
    eq2, eq2_b = ctx.sb("eq2", [128, 8])
    wt1, wt1_b = ctx.sb("wt1", [128, 1])
    wt2, wt2_b = ctx.sb("wt2", [128, 1])
    negm, negm_b = ctx.sb("negm", [128, 2])
    gT, gT_b = ctx.sb("gT", [8, 128])
    for i in range(NTILES):
        bank, bank_b = ctx.take_bank()
        mm = [(xnT[:, c, 128 * i:128 * i + 128], gw[:, c, :], [xnT_buf, gw_b]) for c in range(8)]
        mm.append((ctx.ones_row[0:1, 0:128], gb[:], [ctx.onesr_buf, gb_b]))
        ctx.matmul_acc(bank.ap()[:, 0:8], bank_b, mm, tag="gate_mm")
        p.op("vector", lambda e, bank=bank: e.tensor_copy(lgt[:], bank.ap()[:, 0:8]),
             reads=[bank_b], writes=[lgt_b], tag="gate_evac")
        p.op("vector", lambda e: e.tensor_reduce(m1[:], lgt[:], axis=AX.X, op=ALU.max),
             reads=[lgt_b], writes=[m1_b], tag="g_m1")
        p.op("vector", lambda e: e.tensor_scalar(eq1[:], lgt[:], m1[:], None, op0=ALU.is_ge),
             reads=[lgt_b, m1_b], writes=[eq1_b], tag="g_eq1")
        p.op("vector", lambda e: e.tensor_scalar(eq2[:], eq1[:], -1e30, None, op0=ALU.mult),
             reads=[eq1_b], writes=[eq2_b], tag="g_mask")
        p.op("vector", lambda e: e.tensor_tensor(eq2[:], lgt[:], eq2[:], op=ALU.add),
             reads=[lgt_b, eq2_b], writes=[eq2_b], tag="g_lgt2")
        p.op("vector", lambda e: e.tensor_reduce(m2[:], eq2[:], axis=AX.X, op=ALU.max),
             reads=[eq2_b], writes=[m2_b], tag="g_m2")
        p.op("vector", lambda e: e.tensor_scalar(eq2[:], eq2[:], m2[:], None, op0=ALU.is_ge),
             reads=[eq2_b, m2_b], writes=[eq2_b], tag="g_eq2")
        p.op("scalar", lambda e: e.activation(negm[:, 0:1], m1[:], AF.Copy, bias=0.0, scale=-1.0),
             reads=[m1_b], writes=[negm_b], tag="g_neg1")
        p.op("scalar", lambda e: e.activation(negm[:, 1:2], m2[:], AF.Copy, bias=0.0, scale=-1.0),
             reads=[m2_b], writes=[negm_b], tag="g_neg2")
        p.op("scalar", lambda e: e.activation(wt1[:], m1[:], AF.Sigmoid, bias=negm[:, 1:2], scale=1.0),
             reads=[m1_b, negm_b], writes=[wt1_b], tag="g_w1")
        p.op("scalar", lambda e: e.activation(wt2[:], m2[:], AF.Sigmoid, bias=negm[:, 0:1], scale=1.0),
             reads=[m2_b, negm_b], writes=[wt2_b], tag="g_w2")
        p.op("vector", lambda e: e.tensor_scalar(eq1[:], eq1[:], wt1[:], None, op0=ALU.mult),
             reads=[eq1_b, wt1_b], writes=[eq1_b], tag="g_s1")
        p.op("vector", lambda e: e.tensor_scalar(eq2[:], eq2[:], wt2[:], None, op0=ALU.mult),
             reads=[eq2_b, wt2_b], writes=[eq2_b], tag="g_s2")
        p.op("vector", lambda e, i=i: e.tensor_tensor(gates[:, i, :], eq1[:], eq2[:], op=ALU.add),
             reads=[eq1_b, eq2_b], writes=[gates_b], tag="g_gates")
        bank2, bank2_b = ctx.take_bank()
        p.op("tensor", lambda e, i=i, bank2=bank2: e.transpose(
            bank2.ap()[0:8, 0:128], gates[:, i, :], ctx.ident32[:]),
             reads=[gates_b, ctx.ident32_buf], writes=[bank2_b], tag="gT")
        p.op("vector", lambda e, bank2=bank2: e.tensor_copy(gT[:], bank2.ap()[0:8, 0:128]),
             reads=[bank2_b], writes=[gT_b], tag="gT_evac")
        for dh in range(2):
            bank3, bank3_b = ctx.take_bank()
            ctx.matmul_acc(bank3.ap(), bank3_b,
                           [(gT[:], b2r[:, 512 * dh:512 * dh + 512], [gT_b, b2r_b])], tag="b2mm")
            p.op("vector", lambda e, i=i, dh=dh, bank3=bank3: e.tensor_tensor(
                x_t[:, i, 512 * dh:512 * dh + 512], x_t[:, i, 512 * dh:512 * dh + 512], bank3.ap(), op=ALU.add),
                 reads=[bank3_b, x_bufs[i]], writes=[x_bufs[i]], tag="b2add")

    w1h, w1h_b = ctx.sb("w1h", [128, 8, 2048], BF16)
    w2h, w2h_b = ctx.sb("w2h", [128, 16, 1024], BF16)
    hT, hT_b = ctx.sb("hT", [128, 16, 1024], BF16)
    tmp, tmp_b = ctx.sb("l1ftmp", [128, 512])
    for e_i in range(NE):
        for fh in range(2):
            p.dma("gpsimd", lambda e, e_i=e_i, fh=fh: e.dma_start(
                out=w1h[:], in_=din["l1f_w1"][e_i, :, 2048 * fh:2048 * fh + 2048].rearrange("(c p) f -> p c f", p=128)),
                  writes=[w1h_b], tag="ld_w1h")
            p.dma("gpsimd", lambda e, e_i=e_i, fh=fh: e.dma_start(
                out=w2h[:], in_=din["l1f_w2"][e_i, 2048 * fh:2048 * fh + 2048, :].rearrange("(c p) f -> p c f", p=128)),
                  writes=[w2h_b], tag="ld_w2h")
            for th in range(2):
                for fs in range(16):
                    bank, bank_b = ctx.take_bank()
                    for ts in range(2):
                        mm = [(w1h[:, c, 128 * fs:128 * fs + 128],
                               xnT[:, c, 1024 * th + 512 * ts: 1024 * th + 512 * ts + 512],
                               [w1h_b, xnT_buf]) for c in range(8)]
                        ctx.matmul_acc(bank.ap()[:, 0:512], bank_b, mm, tag="moe_h")
                        p.op("scalar", lambda e, fs=fs, ts=ts, e_i=e_i, fh=fh, bank=bank: e.activation(
                            hT[:, fs, 512 * ts:512 * ts + 512], bank.ap()[:, 0:512], AF.Silu,
                            bias=b1c[:, e_i, 16 * fh + fs:16 * fh + fs + 1], scale=1.0),
                             reads=[bank_b, b1c_b], writes=[hT_b], tag="moe_silu")
                for ts in range(8):
                    ti = 8 * th + ts
                    for dh in range(2):
                        bank, bank_b = ctx.take_bank()
                        mm = [(hT[:, fs, 128 * ts:128 * ts + 128],
                               w2h[:, fs, 512 * dh:512 * dh + 512],
                               [hT_b, w2h_b]) for fs in range(16)]
                        ctx.matmul_acc(bank.ap(), bank_b, mm, tag="moe_y")
                        p.op("vector", lambda e, e_i=e_i, ti=ti, bank=bank: e.tensor_scalar(
                            tmp[:], bank.ap(), gates[:, ti, e_i:e_i + 1], None, op0=ALU.mult),
                             reads=[bank_b, gates_b], writes=[tmp_b], tag="moe_scale")
                        p.op("vector", lambda e, ti=ti, dh=dh: e.tensor_tensor(
                            x_t[:, ti, 512 * dh:512 * dh + 512], x_t[:, ti, 512 * dh:512 * dh + 512],
                            tmp[:], op=ALU.add),
                             reads=[tmp_b, x_bufs[ti]], writes=[x_bufs[ti]], tag="moe_add")
    if want_dbg:
        return {"gates": (gates, gates_b), "xnT": (xnT, xnT_buf)}


def emit_L1_attn(ctx, din, x_t, x_bufs):
    p = ctx.prog
    xnT, xnT_buf = ctx.sb("xnT_a", [128, 8, 2048], BF16)
    ln_to_featmajor(ctx, x_t, x_bufs, xnT, xnT_buf)

    cost, cost_b = ctx.sb("ropecos", [128, 16, 32], BF16)
    sint, sint_b = ctx.sb("ropesin", [128, 16, 32], BF16)
    p.dma("gpsimd", lambda e: e.dma_start(out=cost[:], in_=din["ropecos"].rearrange("(n p) f -> p n f", p=128)),
          writes=[cost_b], tag="ld_cos")
    p.dma("gpsimd", lambda e: e.dma_start(out=sint[:], in_=din["ropesin"].rearrange("(n p) f -> p n f", p=128)),
          writes=[sint_b], tag="ld_sin")
    masks, masks_b = ctx.causal_masks()
    brow, brow_b = ctx.sb("l1a_b", [1, 3 * 1024], BF16)
    p.dma("gpsimd", lambda e: e.dma_start(out=brow[:, 0:1024], in_=din["l1a_bq"]), writes=[brow_b], tag="ld_bq")
    p.dma("gpsimd", lambda e: e.dma_start(out=brow[:, 1024:2048], in_=din["l1a_bk"]), writes=[brow_b], tag="ld_bk")
    p.dma("gpsimd", lambda e: e.dma_start(out=brow[:, 2048:3072], in_=din["l1a_bv"]), writes=[brow_b], tag="ld_bv")

    wp, wp_b = ctx.sb("wp", [128, 8, 512], BF16)

    ctxT, ctxT_buf = ctx.sb("ctxT", [128, 8, 2048], BF16)

    wqkv = [ctx.sb("wqkv", [128, 8, 384], BF16)]
    q32, q32_b = ctx.sb("q32", [128, 4, 128])
    k32, k32_b = ctx.sb("k32", [128, 4, 128])
    qbf, qbf_b = ctx.sb("qbf", [128, 16, 128], BF16)
    kbf, kbf_b = ctx.sb("kbf", [128, 16, 128], BF16)
    vtok, vtok_b = ctx.sb("vtok", [128, 16, 128], BF16)
    qT, qT_b = ctx.sb("qT", [128, 2048], BF16)
    kT, kT_b = ctx.sb("kT", [128, 2048], BF16)
    srow_p = [ctx.sb("srow", [128, 2048]) for _ in range(2)]
    pbf_p = [ctx.sb("pbf", [128, 2048], BF16) for _ in range(2)]
    pts4_p = [ctx.sb("pts4", [128, 512], BF16) for _ in range(2)]
    nmax_p = [ctx.sb("nmaxa", [128, 1]) for _ in range(2)]
    rsum_p = [ctx.sb("rsuma", [128, 1]) for _ in range(2)]
    ssum_p = [ctx.sb("ssuma", [128, 1]) for _ in range(2)]
    masks128, masks128_b = ctx.sb("m128", [128, 128], BF16)
    p.op("gpsimd", lambda e: e.memset(masks128[:], 0.0), writes=[masks128_b], tag="m128a")
    p.op("gpsimd", lambda e: e.affine_select(
        out=masks128[:], in_=masks128[:], compare_op=ALU.is_ge,
        fill=-1e30, base=0, pattern=[[-1, 128]], channel_multiplier=1),
        reads=[masks128_b], writes=[masks128_b], tag="m128b")
    pts_ctr = [0]

    t0, t0_b = ctx.sb("ropet0", [128, 4, 32])

    def rope_apply(src32, src_b, dstbf, dst_b, g):
        cs = cost[:, 4 * g:4 * g + 4, :]
        sn = sint[:, 4 * g:4 * g + 4, :]
        for h in range(2):
            v = src32[:].rearrange("p a (h d two) -> p a h d two", h=2, two=2)
            x0 = v[:, :, h, :, 0]
            x1 = v[:, :, h, :, 1]
            dv = dstbf[:, 4 * g:4 * g + 4, :].rearrange("p a (h r d) -> p a h r d", h=2, r=2)
            r0 = dv[:, :, h, 0, :]
            r1 = dv[:, :, h, 1, :]
            p.op("vector", lambda e, x0=x0, cs=cs: e.tensor_tensor(t0[:], x0, cs, op=ALU.mult),
                 reads=[src_b, cost_b], writes=[t0_b], tag="rp0")
            p.op("gpsimd", lambda e, x1=x1, sn=sn, r0=r0: e.tensor_tensor(r0, x1, sn, op=ALU.mult),
                 reads=[src_b, sint_b], writes=[dst_b], tag="rp1")
            p.op("vector", lambda e, r0=r0: e.tensor_tensor(r0, t0[:], r0, op=ALU.subtract),
                 reads=[t0_b, dst_b], writes=[dst_b], tag="rp2")
            p.op("vector", lambda e, x0=x0, sn=sn: e.tensor_tensor(t0[:], x0, sn, op=ALU.mult),
                 reads=[src_b, sint_b], writes=[t0_b], tag="rp3")
            p.op("gpsimd", lambda e, x1=x1, cs=cs, r1=r1: e.tensor_tensor(r1, x1, cs, op=ALU.mult),
                 reads=[src_b, cost_b], writes=[dst_b], tag="rp4")
            p.op("vector", lambda e, r1=r1: e.tensor_tensor(r1, t0[:], r1, op=ALU.add),
                 reads=[t0_b, dst_b], writes=[dst_b], tag="rp5")

    for pr in range(8):
        wt, wt_b = wqkv[0]
        p.dma("gpsimd", lambda e, pr=pr: e.dma_start(
            out=wt[:, :, 0:128], in_=din["l1a_wq"][:, 128 * pr:128 * pr + 128].rearrange("(c p) d -> p c d", p=128)),
              writes=[wt_b], tag="ld_wq")
        p.dma("gpsimd", lambda e, pr=pr: e.dma_start(
            out=wt[:, :, 128:256], in_=din["l1a_wk"][:, 128 * pr:128 * pr + 128].rearrange("(c p) d -> p c d", p=128)),
              writes=[wt_b], tag="ld_wk")
        p.dma("gpsimd", lambda e, pr=pr: e.dma_start(
            out=wt[:, :, 256:384], in_=din["l1a_wv"][:, 128 * pr:128 * pr + 128].rearrange("(c p) d -> p c d", p=128)),
              writes=[wt_b], tag="ld_wv")
        for g in range(4):
            bq_ = [ctx.take_bank() for _ in range(2)]
            bk_ = [ctx.take_bank() for _ in range(2)]
            for j in range(4):
                ti = 4 * g + j
                bnq, bnq_b = bq_[j % 2]
                bnk, bnk_b = bk_[j % 2]
                mmq = [(xnT[:, c, 128 * ti:128 * ti + 128], wt[:, c, 0:128], [xnT_buf, wt_b]) for c in range(8)]
                mmq.append((ctx.ones_row_bf[0:1, :], brow[:, 128 * pr:128 * pr + 128], [ctx.onesrbf_buf, brow_b]))
                ctx.matmul_acc(bnq.ap()[:, 0:128], bnq_b, mmq, tag="qmm")
                p.op("vector", lambda e, j=j, bnq=bnq: e.tensor_copy(q32[:, j, :], bnq.ap()[:, 0:128]),
                     reads=[bnq_b], writes=[q32_b], tag="qevac")
                mmk = [(xnT[:, c, 128 * ti:128 * ti + 128], wt[:, c, 128:256], [xnT_buf, wt_b]) for c in range(8)]
                mmk.append((ctx.ones_row_bf[0:1, :], brow[:, 1024 + 128 * pr:1024 + 128 * pr + 128], [ctx.onesrbf_buf, brow_b]))
                ctx.matmul_acc(bnk.ap()[:, 0:128], bnk_b, mmk, tag="kmm")
                p.op("vector", lambda e, j=j, bnk=bnk: e.tensor_copy(k32[:, j, :], bnk.ap()[:, 0:128]),
                     reads=[bnk_b], writes=[k32_b], tag="kevac")
                bnv, bnv_b = ctx.take_bank()
                mmv = [(xnT[:, c, 128 * ti:128 * ti + 128], wt[:, c, 256:384], [xnT_buf, wt_b]) for c in range(8)]
                mmv.append((ctx.ones_row_bf[0:1, :], brow[:, 2048 + 128 * pr:2048 + 128 * pr + 128], [ctx.onesrbf_buf, brow_b]))
                ctx.matmul_acc(bnv.ap()[:, 0:128], bnv_b, mmv, tag="vmm")
                p.op("scalar", lambda e, ti=ti, bnv=bnv: e.activation(
                    vtok[:, ti, :], bnv.ap()[:, 0:128], AF.Copy, bias=0.0, scale=1.0),
                     reads=[bnv_b], writes=[vtok_b], tag="vevac")
            rope_apply(q32, q32_b, qbf, qbf_b, g)
            rope_apply(k32, k32_b, kbf, kbf_b, g)
        for src, sb_, dstT, dT_b in ((qbf, qbf_b, qT, qT_b), (kbf, kbf_b, kT, kT_b)):
            for g in range(4):
                bank, bank_b = ctx.take_bank()
                bview = bank.ap().bitcast(BF16)
                for j in range(4):
                    p.op("tensor", lambda e, g=g, j=j, src=src, bview=bview: e.transpose(
                        bview[:, 128 * j:128 * j + 128], src[:, 4 * g + j, :], ctx.ident[:]),
                         reads=[sb_, ctx.ident_buf], writes=[bank_b], tag="qkxp")
                p.op("vector", lambda e, g=g, dstT=dstT, bview=bview: e.tensor_copy(
                    dstT[:, 512 * g:512 * g + 512].rearrange("p (a b) -> p a b", b=128),
                    bview[:, 0:512].rearrange("p (a b) -> p a b", b=128)),
                     reads=[bank_b], writes=[dT_b], tag="qkxpe")
        def stageA(h, qi):
            par = (h * NTILES + qi) % 2
            srow, srow_b = srow_p[par]
            pbf, pbf_b = pbf_p[par]
            nmax, nmax_b = nmax_p[par]
            rsum, rsum_b = rsum_p[par]
            ssum, ssum_b = ssum_p[par]
            hb = 64 * h
            nk5 = qi // 4 + 1
            sb_banks = [ctx.take_bank() for _ in range(2)]
            for kj in range(nk5):
                w = min(512, (qi + 1) * 128 - 512 * kj)
                bank, bank_b = sb_banks[kj % 2]
                p.op("tensor", lambda e, qi=qi, kj=kj, w=w, bank=bank, hb=hb: e.matmul(
                    bank.ap()[:, 0:w], qT[hb:hb + 64, 128 * qi:128 * qi + 128],
                    kT[hb:hb + 64, 512 * kj:512 * kj + w], start=True, stop=True),
                     reads=[qT_b, kT_b], writes=[bank_b], tag="smm")
                if kj == 0:
                    p.op("vector", lambda e, kj=kj, w=w, bank=bank, srow=srow, nmax=nmax: e.tensor_scalar(
                        srow[:, 512 * kj:512 * kj + w], bank.ap()[:, 0:w], 0.125, -1e30,
                        op0=ALU.mult, op1=ALU.max, accum_out=nmax[:]),
                         reads=[bank_b], writes=[srow_b, nmax_b], tag="sevac")
                else:
                    p.op("vector", lambda e, kj=kj, w=w, bank=bank, srow=srow, nmax=nmax: e.tensor_scalar(
                        srow[:, 512 * kj:512 * kj + w], bank.ap()[:, 0:w], 0.125, nmax[:],
                        op0=ALU.mult, op1=ALU.max, accum_out=nmax[:]),
                         reads=[bank_b, nmax_b], writes=[srow_b, nmax_b], tag="sevac")
            kj = qi // 4
            w0 = 128 * qi - 512 * kj
            p.op("vector", lambda e, kj=kj, w0=w0, srow=srow: e.tensor_tensor(
                srow[:, 512 * kj + w0:512 * kj + w0 + 128], srow[:, 512 * kj + w0:512 * kj + w0 + 128],
                masks128[:], op=ALU.add),
                 reads=[srow_b, masks128_b], writes=[srow_b], tag="smask")
            p.op("vector", lambda e, nmax=nmax: e.tensor_scalar(
                nmax[:], nmax[:], -1.0, None, op0=ALU.mult),
                 reads=[nmax_b], writes=[nmax_b], tag="snneg")
            W = (qi + 1) * 128
            p.op("scalar", lambda e, W=W, pbf=pbf, srow=srow, nmax=nmax, ssum=ssum: e.activation(
                pbf[:, 0:W], srow[:, 0:W], AF.Exp, bias=nmax[:], scale=1.0, accum_out=ssum[:]),
                 reads=[srow_b, nmax_b], writes=[pbf_b, ssum_b], tag="sexp")
            p.op("vector", lambda e, rsum=rsum, ssum=ssum: e.reciprocal(rsum[:], ssum[:]),
                 reads=[ssum_b], writes=[rsum_b], tag="srec")
            p.op("vector", lambda e, W=W, pbf=pbf, rsum=rsum: e.tensor_scalar(
                pbf[:, 0:W], pbf[:, 0:W], rsum[:], None, op0=ALU.mult),
                 reads=[pbf_b, rsum_b], writes=[pbf_b], tag="pscale")

        def stageB(h, qi):
            par = (h * NTILES + qi) % 2
            pbf, pbf_b = pbf_p[par]
            hb = 64 * h
            ch = pr
            cbank, cbank_b, cbank_i = ctx.take_bank(hold=True)
            nkj = qi + 1
            for g0 in range(0, nkj, 4):
                ng = min(4, nkj - g0)
                pt4, pt4_b = pts4_p[pts_ctr[0] % 2]
                pts_ctr[0] += 1
                tbank, tbank_b = ctx.take_bank()
                tb = tbank.ap().bitcast(BF16)
                for j in range(ng):
                    p.op("tensor", lambda e, g0=g0, j=j, tb=tb, pbf=pbf: e.transpose(
                        tb[:, 128 * j:128 * j + 128], pbf[:, 128 * (g0 + j):128 * (g0 + j) + 128],
                        ctx.ident[:]),
                         reads=[pbf_b, ctx.ident_buf], writes=[tbank_b], tag="pxp")
                p.op("vector", lambda e, ng=ng, tb=tb, pt4=pt4: e.tensor_copy(
                    pt4[:, 0:128 * ng], tb[:, 0:128 * ng]),
                     reads=[tbank_b], writes=[pt4_b], tag="pxpe")
                for j in range(ng):
                    kj = g0 + j
                    p.op("tensor", lambda e, kj=kj, j=j, qi=qi, hb=hb, cbank=cbank, pt4=pt4: e.matmul(
                        cbank.ap()[0:64, 0:128], vtok[:, kj, hb:hb + 64], pt4[:, 128 * j:128 * j + 128],
                        start=(kj == 0), stop=(kj == qi)),
                         reads=[pt4_b, vtok_b], writes=[cbank_b], tag="pvmm")
            p.op("vector", lambda e, qi=qi, hb=hb, ch=ch, cbank=cbank: e.tensor_copy(
                ctxT[hb:hb + 64, ch, 128 * qi:128 * qi + 128], cbank.ap()[0:64, 0:128]),
                 reads=[cbank_b], writes=[ctxT_buf], tag="ctxevac")
            ctx.release_bank(cbank_i)

        hq = [(h, qi) for h in range(2) for qi in range(NTILES)]
        stageA(*hq[0])
        for i in range(len(hq)):
            if i + 1 < len(hq):
                stageA(*hq[i + 1])
            stageB(*hq[i])

    p.dma("gpsimd", lambda e: e.dma_start(out=brow[:, 0:1024], in_=din["l1a_bp"]), writes=[brow_b], tag="ld_bp")
    for dh in range(2):
        p.dma("gpsimd", lambda e, dh=dh: e.dma_start(
            out=wp[:], in_=din["l1a_wp"][:, 512 * dh:512 * dh + 512].rearrange("(c p) d -> p c d", p=128)),
              writes=[wp_b], tag="ld_wp")
        for ti in range(NTILES):
            bank, bank_b = ctx.take_bank()
            mm = [(ctxT[:, c, 128 * ti:128 * ti + 128], wp[:, c, :],
                   [ctxT_buf, wp_b]) for c in range(8)]
            mm.append((ctx.ones_row_bf[0:1, :], brow[:, 512 * dh:512 * dh + 512],
                       [ctx.onesrbf_buf, brow_b]))
            ctx.matmul_acc(bank.ap(), bank_b, mm, tag="wpmm")
            p.op("vector", lambda e, ti=ti, dh=dh, bank=bank: e.tensor_tensor(
                x_t[:, ti, 512 * dh:512 * dh + 512], x_t[:, ti, 512 * dh:512 * dh + 512],
                bank.ap(), op=ALU.add),
                 reads=[bank_b, x_bufs[ti]], writes=[x_bufs[ti]], tag="wpadd")


def emit_L0_attn(ctx, din, x_t, x_bufs, dbg_head=None, dout=None, nheads=16, do_attn=True, do_kv=True, do_q=True, do_rms=True):
    p = ctx.prog
    EPS0 = 1.1920929e-07
    xnT, xnT_buf = ctx.sb("xnT_m", [128, 8, 2048], BF16)
    ln_to_featmajor(ctx, x_t, x_bufs, xnT, xnT_buf)
    masks, masks_b = ctx.causal_masks()
    epsc = ctx.const_col(EPS0)

    wdqT, wdqT_b = ctx.sb("wdqT", [128, 512], BF16)
    wuqT, wuqT_b = ctx.sb("wuqT", [128, 4, 64], BF16)
    wdkvT, wdkvT_b = ctx.sb("wdkvT", [128, 1024], BF16)
    wukvT, wukvT_b = ctx.sb("wukvT", [128, 8, 128], BF16)
    wqrT, wqrT_b = ctx.sb("wqrT", [64, 64], BF16)
    wkrT, wkrT_b = ctx.sb("wkrT", [128, 64], BF16)
    woT, woT_b = ctx.sb("woT", [64, 64], BF16)
    nw, nw_b = ctx.sb("normw", [64, 2])
    p.dma("gpsimd", lambda e: e.dma_start(out=wdqT[0:64, :], in_=din["l0a_wdqT"]), writes=[wdqT_b], tag="ldw")
    p.dma("gpsimd", lambda e: e.dma_start(out=wdqT[64:128, :], in_=din["l0a_wdqT"]), writes=[wdqT_b], tag="ldw")
    p.dma("gpsimd", lambda e: e.dma_start(out=wuqT[:], in_=din["l0a_wuqT"].rearrange("(c p) d -> p c d", p=128)), writes=[wuqT_b], tag="ldw")
    p.dma("gpsimd", lambda e: e.dma_start(out=wdkvT[0:64, :], in_=din["l0a_wdkvT"]), writes=[wdkvT_b], tag="ldw")
    p.dma("gpsimd", lambda e: e.dma_start(out=wdkvT[64:128, :], in_=din["l0a_wdkvT"]), writes=[wdkvT_b], tag="ldw")
    p.dma("gpsimd", lambda e: e.dma_start(out=wukvT[:], in_=din["l0a_wukvT"].rearrange("(c p) d -> p c d", p=128)), writes=[wukvT_b], tag="ldw")
    p.dma("gpsimd", lambda e: e.dma_start(out=wqrT[:], in_=din["l0a_wqrT"]), writes=[wqrT_b], tag="ldw")
    p.dma("gpsimd", lambda e: e.dma_start(out=wkrT[0:64, :], in_=din["l0a_wkrT"]), writes=[wkrT_b], tag="ldw")
    p.dma("gpsimd", lambda e: e.dma_start(out=wkrT[64:128, :], in_=din["l0a_wkrT"]), writes=[wkrT_b], tag="ldw")
    p.dma("gpsimd", lambda e: e.dma_start(out=woT[:], in_=din["l0a_woT"]), writes=[woT_b], tag="ldw")
    p.dma("sync", lambda e: e.dma_start(out=nw[:], in_=din["l0a_normw"]), writes=[nw_b], tag="ldnw")

    with ctx.scope("l0a"):
        cqT, cqT_b = ctx.sb("cqT", [128, 4, 2048], BF16)
        ckvT, ckvT_b = ctx.sb("ckvT", [128, 8, 2048], BF16)
        QT, QT_b = ctx.sb("QT0", [128, 2048], BF16)
        KT, KT_b = ctx.sb("KT0", [128, 2048], BF16)
        vtok_p = [ctx.sb("vtok0", [128, 16, 64], BF16) for _ in range(2)]
        vfs, vfs_b = ctx.sb("vfs", [64, 512], BF16)
        sqs, sqs_b = ctx.sb("sqs", [64, 512])
        qcs, qcs_b = ctx.sb("qcs", [64, 512])
        qcn32, qcn32_b = ctx.sb("qcn32", [64, 512])
        rrow, rrow_b = ctx.sb("rrow", [1, 512])
        srow_p = [ctx.sb("srow0", [128, 2048]) for _ in range(2)]
        pbf_p = [ctx.sb("pbf0", [128, 2048], BF16) for _ in range(2)]
        pts4_p = [ctx.sb("pts40", [128, 512], BF16) for _ in range(2)]
        ctxbf_p = [ctx.sb("ctxbf", [64, 128], BF16) for _ in range(2)]
        nmax_p = [ctx.sb("nmax0", [128, 1]) for _ in range(2)]
        rsum_p = [ctx.sb("rsum0", [128, 1]) for _ in range(2)]
        ssum_p = [ctx.sb("ssum0", [128, 1]) for _ in range(2)]
        otmp_p = [ctx.sb("otmp", [128, 64]) for _ in range(2)]
        m128, m128_b = ctx.sb("m128_0", [128, 128], BF16)
        p.op("gpsimd", lambda e: e.memset(m128[:], 0.0), writes=[m128_b], tag="m1280")
        p.op("gpsimd", lambda e: e.affine_select(
            out=m128[:], in_=m128[:], compare_op=ALU.is_ge,
            fill=-1e30, base=0, pattern=[[-1, 128]], channel_multiplier=1),
            reads=[m128_b], writes=[m128_b], tag="m1281")
        pts_ctr = [0]

        import os as _os
        RMSLVL = int(_os.environ.get("RMSLVL", "7"))

        def rms_chain(psum_ap, psum_b, rows, out_ap, out_b, w_col, tq):
            p.op("vector", lambda e: e.tensor_copy(qcs[:], psum_ap),
                 reads=[psum_b], writes=[qcs_b], tag="rms_cp")
            p.op("scalar", lambda e: e.activation(sqs[:], qcs[:], AF.Square, bias=0.0, scale=1.0),
                 reads=[qcs_b], writes=[sqs_b], tag="rms_sq")
            if RMSLVL < 2:
                p.op("vector", lambda e, out_ap=out_ap: e.tensor_copy(out_ap, qcs[:]),
                     reads=[qcs_b], writes=[out_b], tag="rms_skip")
                return
            bsum, bsum_b = ctx.take_bank()
            p.op("tensor", lambda e, bsum=bsum: e.matmul(
                bsum.ap()[0:1, :], ctx.ones_col[0:64, :], sqs[:], start=True, stop=True),
                 reads=[sqs_b, ctx.ones_buf], writes=[bsum_b], tag="rms_sum")
            if RMSLVL < 3:
                p.op("vector", lambda e, out_ap=out_ap: e.tensor_copy(out_ap, qcs[:]),
                     reads=[qcs_b], writes=[out_b], tag="rms_skip")
                return
            p.op("scalar", lambda e, bsum=bsum: e.activation(
                rrow[:], bsum.ap()[0:1, :], AF.Ln, bias=epsc[0:1, :], scale=1.0 / 64),
                 reads=[bsum_b, ctx.consts_buf], writes=[rrow_b], tag="rms_ln")
            if RMSLVL < 4:
                p.op("vector", lambda e, out_ap=out_ap: e.tensor_copy(out_ap, qcs[:]),
                     reads=[qcs_b], writes=[out_b], tag="rms_skip")
                return
            p.op("scalar", lambda e: e.activation(rrow[:], rrow[:], AF.Exp, bias=0.0, scale=-0.5),
                 reads=[rrow_b], writes=[rrow_b], tag="rms_exp")
            bbc, bbc_b = ctx.take_bank()
            if RMSLVL < 5:
                p.op("vector", lambda e, out_ap=out_ap: e.tensor_copy(out_ap, qcs[:]),
                     reads=[qcs_b], writes=[out_b], tag="rms_skip")
                return
            p.op("tensor", lambda e, bbc=bbc: e.matmul(
                bbc.ap()[0:64, :], ctx.ones_row[0:1, 0:64], rrow[:], start=True, stop=True),
                 reads=[rrow_b, ctx.onesr_buf], writes=[bbc_b], tag="rms_bc")
            if RMSLVL < 6:
                p.op("vector", lambda e, out_ap=out_ap: e.tensor_copy(out_ap, qcs[:]),
                     reads=[qcs_b], writes=[out_b], tag="rms_skip")
                return
            p.op("vector", lambda e, bbc=bbc: e.tensor_tensor(qcn32[:], qcs[:], bbc.ap()[0:64, :], op=ALU.mult),
                 reads=[qcs_b, bbc_b], writes=[qcn32_b], tag="rms_mul")
            p.op("vector", lambda e, w_col=w_col, out_ap=out_ap: e.tensor_scalar(
                out_ap, qcn32[:], w_col, None, op0=ALU.mult),
                 reads=[qcn32_b, nw_b], writes=[out_b], tag="rms_w")

        for h in range(nheads):
            hc = h // 2
            hr = (h % 2) * 64
            xhT = lambda tq, w=512: xnT[hr:hr + 64, hc, 512 * tq:512 * tq + w]
            for lc in range(4):
                for tq in range(4):
                    bank, bank_b = ctx.take_bank()
                    p.op("tensor", lambda e, lc=lc, bank=bank, hr=hr, xh=xhT(tq): e.matmul(
                        bank.ap(), wdqT[hr:hr + 64, 128 * lc:128 * lc + 128], xh,
                        start=True, stop=True, tile_position=(hr, 0)),
                         reads=[wdqT_b, xnT_buf], writes=[bank_b], tag="cq_mm")
                    p.op("vector", lambda e, lc=lc, tq=tq, bank=bank: e.tensor_copy(
                        cqT[:, lc, 512 * tq:512 * tq + 512], bank.ap()),
                         reads=[bank_b], writes=[cqT_b], tag="cq_ev")
            for tq in range(4 if do_q else 0):
                bank, bank_b = ctx.take_bank()
                mm = [(wuqT[:, lc, :], cqT[:, lc, 512 * tq:512 * tq + 512], [wuqT_b, cqT_b]) for lc in range(4)]
                ctx.matmul_acc(bank.ap()[0:64, :], bank_b, mm, tag="qc_mm")
                if do_rms:
                    rms_chain(bank.ap()[0:64, :], bank_b, 64, QT[0:64, 512 * tq:512 * tq + 512], QT_b, nw[:, 0:1], tq)
                else:
                    p.op("vector", lambda e, tq=tq, bank=bank: e.tensor_copy(
                        QT[0:64, 512 * tq:512 * tq + 512], bank.ap()[0:64, :]),
                         reads=[bank_b], writes=[QT_b], tag="qc_ev")
                bqr, bqr_b = ctx.take_bank()
                p.op("tensor", lambda e, tq=tq, bqr=bqr: e.matmul(
                    bqr.ap()[0:64, :], wqrT[:], QT[0:64, 512 * tq:512 * tq + 512], start=True, stop=True),
                     reads=[wqrT_b, QT_b], writes=[bqr_b], tag="qr_mm")
                p.op("vector", lambda e, tq=tq, bqr=bqr: e.tensor_copy(
                    QT[64:128, 512 * tq:512 * tq + 512], bqr.ap()[0:64, :]),
                     reads=[bqr_b], writes=[QT_b], tag="qr_ev")
            for lc in range(8 if do_kv else 0):
                for tq in range(4):
                    bank, bank_b = ctx.take_bank()
                    p.op("tensor", lambda e, lc=lc, bank=bank, hr=hr, xh=xhT(tq): e.matmul(
                        bank.ap(), wdkvT[hr:hr + 64, 128 * lc:128 * lc + 128], xh,
                        start=True, stop=True, tile_position=(hr, 0)),
                         reads=[wdkvT_b, xnT_buf], writes=[bank_b], tag="ckv_mm")
                    p.op("vector", lambda e, lc=lc, tq=tq, bank=bank: e.tensor_copy(
                        ckvT[:, lc, 512 * tq:512 * tq + 512], bank.ap()),
                         reads=[bank_b], writes=[ckvT_b], tag="ckv_ev")
            for tq in range(4 if do_kv else 0):
                bank, bank_b, bank_i = ctx.take_bank(hold=True)
                mm = [(wukvT[:, lc, :], ckvT[:, lc, 512 * tq:512 * tq + 512], [wukvT_b, ckvT_b]) for lc in range(8)]
                ctx.matmul_acc(bank.ap(), bank_b, mm, tag="kv_mm")
                rms_chain(bank.ap()[0:64, :], bank_b, 64, KT[0:64, 512 * tq:512 * tq + 512], KT_b, nw[:, 1:2], tq)
                p.op("scalar", lambda e, tq=tq, bank=bank: e.activation(
                    vfs[:], bank.ap()[64:128, :], AF.Copy, bias=0.0, scale=1.0),
                     reads=[bank_b], writes=[vfs_b], tag="v_ev")
                vtok, vtok_b = vtok_p[h % 2]
                for j in range(4):
                    bx, bx_b = ctx.take_bank()
                    bxv = bx.ap().bitcast(BF16)
                    p.op("tensor", lambda e, j=j, bxv=bxv: e.transpose(
                        bxv[:, 0:64], vfs[:, 128 * j:128 * j + 128], ctx.ident[0:64, 0:64]),
                         reads=[vfs_b, ctx.ident_buf], writes=[bx_b], tag="v_xp")
                    p.op("vector", lambda e, tq=tq, j=j, bxv=bxv, vtok=vtok: e.tensor_copy(
                        vtok[:, 4 * tq + j, :], bxv[:, 0:64]),
                         reads=[bx_b], writes=[vtok_b], tag="v_xpe")
                bkr, bkr_b = ctx.take_bank()
                p.op("tensor", lambda e, bkr=bkr, hr=hr, xh=xhT(tq): e.matmul(
                    bkr.ap()[0:64, :], wkrT[hr:hr + 64, :], xh,
                    start=True, stop=True, tile_position=(hr, 0)),
                     reads=[wkrT_b, xnT_buf], writes=[bkr_b], tag="kr_mm")
                p.op("vector", lambda e, tq=tq, bkr=bkr: e.tensor_copy(
                    KT[64:128, 512 * tq:512 * tq + 512], bkr.ap()[0:64, :]),
                     reads=[bkr_b], writes=[KT_b], tag="kr_ev")
                ctx.release_bank(bank_i)
            if dbg_head is not None and h == dbg_head:
                p.dma("gpsimd", lambda e: e.dma_start(out=dout["dbgQT"], in_=QT[:]), reads=[QT_b], tag="dbgQT")
                p.dma("gpsimd", lambda e: e.dma_start(out=dout["dbgKT"], in_=KT[:]), reads=[KT_b], tag="dbgKT")
                p.dma("gpsimd", lambda e: e.dma_start(out=dout["dbgV"], in_=vtok[:].rearrange("p a b -> p (a b)")), reads=[vtok_b], tag="dbgV")
                p.dma("gpsimd", lambda e: e.dma_start(out=dout["dbgcq"], in_=cqT[:, 0, :]), reads=[cqT_b], tag="dbgcq")
            def stageA0(qi):
                par = qi % 2
                srow, srow_b = srow_p[par]
                pbf, pbf_b = pbf_p[par]
                nmax, nmax_b = nmax_p[par]
                ssum, ssum_b = ssum_p[par]
                rsum, rsum_b = rsum_p[par]
                nk5 = qi // 4 + 1
                sb_banks = [ctx.take_bank() for _ in range(2)]
                for kj in range(nk5):
                    w = min(512, (qi + 1) * 128 - 512 * kj)
                    bank, bank_b = sb_banks[kj % 2]
                    p.op("tensor", lambda e, qi=qi, kj=kj, w=w, bank=bank: e.matmul(
                        bank.ap()[:, 0:w], QT[:, 128 * qi:128 * qi + 128],
                        KT[:, 512 * kj:512 * kj + w], start=True, stop=True),
                         reads=[QT_b, KT_b], writes=[bank_b], tag="smm0")
                    if kj == 0:
                        p.op("vector", lambda e, kj=kj, w=w, bank=bank, srow=srow, nmax=nmax: e.tensor_scalar(
                            srow[:, 512 * kj:512 * kj + w], bank.ap()[:, 0:w], 0.125, -1e30,
                            op0=ALU.mult, op1=ALU.max, accum_out=nmax[:]),
                             reads=[bank_b], writes=[srow_b, nmax_b], tag="sev0")
                    else:
                        p.op("vector", lambda e, kj=kj, w=w, bank=bank, srow=srow, nmax=nmax: e.tensor_scalar(
                            srow[:, 512 * kj:512 * kj + w], bank.ap()[:, 0:w], 0.125, nmax[:],
                            op0=ALU.mult, op1=ALU.max, accum_out=nmax[:]),
                             reads=[bank_b, nmax_b], writes=[srow_b, nmax_b], tag="sev0")
                kj = qi // 4
                w0 = 128 * qi - 512 * kj
                p.op("vector", lambda e, kj=kj, w0=w0, srow=srow: e.tensor_tensor(
                    srow[:, 512 * kj + w0:512 * kj + w0 + 128], srow[:, 512 * kj + w0:512 * kj + w0 + 128],
                    m128[:], op=ALU.add),
                     reads=[srow_b, m128_b], writes=[srow_b], tag="smask0")
                p.op("vector", lambda e, nmax=nmax: e.tensor_scalar(
                    nmax[:], nmax[:], -1.0, None, op0=ALU.mult),
                     reads=[nmax_b], writes=[nmax_b], tag="snneg0")
                W = (qi + 1) * 128
                p.op("scalar", lambda e, W=W, pbf=pbf, srow=srow, nmax=nmax, ssum=ssum: e.activation(
                    pbf[:, 0:W], srow[:, 0:W], AF.Exp, bias=nmax[:], scale=1.0, accum_out=ssum[:]),
                     reads=[srow_b, nmax_b], writes=[pbf_b, ssum_b], tag="sexp0")
                p.op("vector", lambda e, rsum=rsum, ssum=ssum: e.reciprocal(rsum[:], ssum[:]),
                     reads=[ssum_b], writes=[rsum_b], tag="srec0")

            def stageB0(h, qi):
                par = qi % 2
                pbf, pbf_b = pbf_p[par]
                rsum, rsum_b = rsum_p[par]
                ctxbf, ctxbf_b = ctxbf_p[par]
                otmp, otmp_b = otmp_p[par]
                vtok, vtok_b = vtok_p[h % 2]
                cbank, cbank_b, cbank_i = ctx.take_bank(hold=True)
                nkj = qi + 1
                for g0 in range(0, nkj, 4):
                    ng = min(4, nkj - g0)
                    pt4, pt4_b = pts4_p[pts_ctr[0] % 2]
                    pts_ctr[0] += 1
                    tbank, tbank_b = ctx.take_bank()
                    tb = tbank.ap().bitcast(BF16)
                    for j in range(ng):
                        p.op("tensor", lambda e, g0=g0, j=j, tb=tb, pbf=pbf: e.transpose(
                            tb[:, 128 * j:128 * j + 128], pbf[:, 128 * (g0 + j):128 * (g0 + j) + 128],
                            ctx.ident[:]),
                             reads=[pbf_b, ctx.ident_buf], writes=[tbank_b], tag="pxp0")
                    p.op("vector", lambda e, ng=ng, tb=tb, pt4=pt4: e.tensor_copy(
                        pt4[:, 0:128 * ng], tb[:, 0:128 * ng]),
                         reads=[tbank_b], writes=[pt4_b], tag="pxpe0")
                    for j in range(ng):
                        kj = g0 + j
                        p.op("tensor", lambda e, kj=kj, j=j, qi=qi, cbank=cbank, pt4=pt4, vtok=vtok: e.matmul(
                            cbank.ap()[0:64, 0:128], vtok[:, kj, :], pt4[:, 128 * j:128 * j + 128],
                            start=(kj == 0), stop=(kj == qi)),
                             reads=[pt4_b, vtok_b], writes=[cbank_b], tag="pvmm0")
                p.op("vector", lambda e, cbank=cbank, ctxbf=ctxbf: e.tensor_copy(ctxbf[:], cbank.ap()[0:64, 0:128]),
                     reads=[cbank_b], writes=[ctxbf_b], tag="ctxev0")
                ctx.release_bank(cbank_i)
                obank, obank_b = ctx.take_bank()
                p.op("tensor", lambda e, obank=obank, ctxbf=ctxbf: e.matmul(
                    obank.ap()[:, 0:64], ctxbf[:], woT[:], start=True, stop=True),
                     reads=[ctxbf_b, woT_b], writes=[obank_b], tag="owmm")
                p.op("vector", lambda e, obank=obank, otmp=otmp, rsum=rsum: e.tensor_scalar(
                    otmp[:], obank.ap()[:, 0:64], rsum[:], None, op0=ALU.mult),
                     reads=[obank_b, rsum_b], writes=[otmp_b], tag="oscale")
                p.op("vector", lambda e, qi=qi, h=h, otmp=otmp: e.tensor_tensor(
                    x_t[:, qi, 64 * h:64 * h + 64], x_t[:, qi, 64 * h:64 * h + 64], otmp[:], op=ALU.add),
                     reads=[otmp_b, x_bufs[qi]], writes=[x_bufs[qi]], tag="oadd")

            if do_attn:
                stageA0(0)
                for qi in range(NTILES):
                    if qi + 1 < NTILES:
                        stageA0(qi + 1)
                    stageB0(h, qi)


def emit_L0_ffn(ctx, din, x_t, x_bufs):
    p = ctx.prog
    xnT, xnT_buf = ctx.sb("xnT_f0", [128, 8, 2048], BF16)
    ln_to_featmajor(ctx, x_t, x_bufs, xnT, xnT_buf)

    g1p, g1p_b = ctx.sb("g1p", [8, 2, 4, 4, 32])
    g2p, g2p_b = ctx.sb("g2p", [16, 2, 4, 16])
    br, br_b = ctx.sb("l0f_b", [1, 1024])
    sel8, sel8_b = ctx.sb("sel8", [8, 128])
    sel16, sel16_b = ctx.sb("sel16", [16, 128])
    p.dma("sync", lambda e: e.dma_start(out=g1p[:], in_=din["l0f_g1p"]), writes=[g1p_b], tag="ldg")
    p.dma("sync", lambda e: e.dma_start(out=g2p[:], in_=din["l0f_g2p"]), writes=[g2p_b], tag="ldg")
    p.dma("sync", lambda e: e.dma_start(out=br[:, 0:512], in_=din["l0f_br"]), writes=[br_b], tag="ldb")
    p.dma("sync", lambda e: e.dma_start(out=br[:, 512:1024], in_=din["l0f_bi"]), writes=[br_b], tag="ldb")
    p.dma("sync", lambda e: e.dma_start(out=sel8[:], in_=din["l0f_sel8"]), writes=[sel8_b], tag="lds")
    p.dma("sync", lambda e: e.dma_start(out=sel16[:], in_=din["l0f_sel16"]), writes=[sel16_b], tag="lds")

    with ctx.scope("l0f"):
        WT = {}
        rows8, rows8_b = ctx.sb("rows8", [8, 32, 16])
        rows16, rows16_b = ctx.sb("rows16", [16, 32, 16])
        asb, asb_b = ctx.sb("asb", [128, 512])
        prod, prod_b = ctx.sb("prodf0", [128, 512])
        prod2, prod2_b = ctx.sb("prodf02", [128, 512])
        for ri, nm in ((0, "R"), (1, "I")):
            wt, wt_b = ctx.sb(f"WT{nm}", [128, 4, 512], BF16)
            WT[nm] = (wt, wt_b)
            for k in range(4):
                for c in range(4):
                    p.op("gpsimd", lambda e, ri=ri, c=c, k=k: e.tensor_copy(
                        rows8[:], g1p[:, ri, c, k, :].unsqueeze(2).broadcast_to((8, 32, 16))),
                         reads=[g1p_b], writes=[rows8_b], tag="rows8")
                    p.op("gpsimd", lambda e, ri=ri, c=c: e.tensor_copy(
                        rows16[:], g2p[:, ri, c, :].unsqueeze(1).broadcast_to((16, 32, 16))),
                         reads=[g2p_b], writes=[rows16_b], tag="rows16")
                    ba, ba_b = ctx.take_bank()
                    p.op("tensor", lambda e, ba=ba: e.matmul(
                        ba.ap(), sel8[:], rows8[:].rearrange("p a b -> p (a b)"), start=True, stop=True),
                         reads=[sel8_b, rows8_b], writes=[ba_b], tag="amm")
                    bb, bb_b = ctx.take_bank()
                    p.op("tensor", lambda e, bb=bb: e.matmul(
                        bb.ap(), sel16[:], rows16[:].rearrange("p a b -> p (a b)"), start=True, stop=True),
                         reads=[sel16_b, rows16_b], writes=[bb_b], tag="bmm")
                    p.op("vector", lambda e, ba=ba: e.tensor_copy(asb[:], ba.ap()),
                         reads=[ba_b], writes=[asb_b], tag="aev")
                    if c == 0:
                        p.op("vector", lambda e, bb=bb: e.tensor_tensor(prod[:], asb[:], bb.ap(), op=ALU.mult),
                             reads=[asb_b, bb_b], writes=[prod_b], tag="wmul")
                    else:
                        p.op("vector", lambda e, bb=bb: e.tensor_tensor(prod2[:], asb[:], bb.ap(), op=ALU.mult),
                             reads=[asb_b, bb_b], writes=[prod2_b], tag="wmul2")
                        p.op("vector", lambda e: e.tensor_tensor(prod[:], prod[:], prod2[:], op=ALU.add),
                             reads=[prod2_b, prod_b], writes=[prod_b], tag="wacc")
                p.op("vector", lambda e, wt=wt, k=k: e.tensor_copy(wt[:, k, :], prod[:]),
                     reads=[prod_b], writes=[wt_b], tag="wstore")
        wtn, wtn_b = ctx.sb("WTIn", [128, 4, 512], BF16)
        for k in range(4):
            p.op("vector", lambda e, k=k: e.tensor_scalar(
                wtn[:, k, :], WT["I"][0][:, k, :], -1.0, None, op0=ALU.mult),
                 reads=[WT["I"][1]], writes=[wtn_b], tag="wneg")

        wr, wr_b = WT["R"]
        wi, wi_b = WT["I"]
        for ti in range(NTILES):
            for half in range(2):
                bank, bank_b = ctx.take_bank()
                if half == 0:
                    mm = [(xnT[:, c, 128 * ti:128 * ti + 128], wr[:, c, :], [xnT_buf, wr_b]) for c in range(4)]
                    mm += [(xnT[:, 4 + c, 128 * ti:128 * ti + 128], wtn[:, c, :], [xnT_buf, wtn_b]) for c in range(4)]
                else:
                    mm = [(xnT[:, c, 128 * ti:128 * ti + 128], wi[:, c, :], [xnT_buf, wi_b]) for c in range(4)]
                    mm += [(xnT[:, 4 + c, 128 * ti:128 * ti + 128], wr[:, c, :], [xnT_buf, wr_b]) for c in range(4)]
                mm.append((ctx.ones_row[0:1, :], br[:, 512 * half:512 * half + 512], [ctx.onesr_buf, br_b]))
                ctx.matmul_acc(bank.ap(), bank_b, mm, tag="f0mm")
                p.op("vector", lambda e, ti=ti, half=half, bank=bank: e.tensor_tensor(
                    x_t[:, ti, 512 * half:512 * half + 512], x_t[:, ti, 512 * half:512 * half + 512],
                    bank.ap(), op=ALU.add),
                     reads=[bank_b, x_bufs[ti]], writes=[x_bufs[ti]], tag="f0add")


def emit_L2_attn(ctx, din, x_t, x_bufs):
    p = ctx.prog
    xnT, xnT_buf = ctx.sb("xnT_2a", [128, 8, 2048], BF16)
    ln_to_featmajor(ctx, x_t, x_bufs, xnT, xnT_buf)

    bvo, bvo_b = ctx.sb("bvo", [1, 2048], BF16)
    p.dma("gpsimd", lambda e: e.dma_start(out=bvo[:, 0:1024], in_=din["l2a_bv"]), writes=[bvo_b], tag="ldbv")
    p.dma("gpsimd", lambda e: e.dma_start(out=bvo[:, 1024:2048], in_=din["l2a_bo"]), writes=[bvo_b], tag="ldbo")
    ctxT, ctxT_buf = ctx.sb("ctxT2", [128, 8, 2048], BF16)

    with ctx.scope("l2a"):
        Tb, Tb_b = ctx.sb("trilT", [128, 16, 64], BF16)
        p.op("gpsimd", lambda e: e.memset(Tb[:], 0.0), writes=[Tb_b], tag="T0")
        for i in range(64):
            a = i * (i + 1) // 2
            p.dma("gpsimd", lambda e, i=i, a=a: e.dma_start(
                out=Tb[i:i + 1, :, 0:i + 1], in_=din["l2a_params"][:, a:a + i + 1].unsqueeze(0),
                allow_slow_non_contiguous=True),
                  writes=[Tb_b], tag="Trow")
            p.dma("gpsimd", lambda e, i=i, a=a: e.dma_start(
                out=Tb[64 + i:64 + i + 1, :, 0:i + 1], in_=din["l2a_params"][:, a:a + i + 1].unsqueeze(0),
                allow_slow_non_contiguous=True),
                  writes=[Tb_b], tag="Trow")
        vtok, vtok_b = ctx.sb("vtok2", [128, 16, 1024], BF16)
        vw_scope = ctx.scope("l2a_vw")
        vw_scope.__enter__()
        wvT, wvT_b = ctx.sb("wvT2", [128, 8, 1024], BF16)
        p.dma("gpsimd", lambda e: e.dma_start(out=wvT[:], in_=din["l2a_wvT"].rearrange("(c p) d -> p c d", p=128)), writes=[wvT_b], tag="ldwv")
        for ti in range(NTILES):
            for dh in range(2):
                bank, bank_b = ctx.take_bank()
                mm = [(xnT[:, c, 128 * ti:128 * ti + 128], wvT[:, c, 512 * dh:512 * dh + 512],
                       [xnT_buf, wvT_b]) for c in range(8)]
                mm.append((ctx.ones_row_bf[0:1, :], bvo[:, 512 * dh:512 * dh + 512], [ctx.onesrbf_buf, bvo_b]))
                ctx.matmul_acc(bank.ap(), bank_b, mm, tag="vmm2")
                p.op("scalar", lambda e, ti=ti, dh=dh, bank=bank: e.activation(
                    vtok[:, ti, 512 * dh:512 * dh + 512], bank.ap(), AF.Copy, bias=0.0, scale=1.0),
                     reads=[bank_b], writes=[vtok_b], tag="vev2")
        vw_scope.__exit__(None, None, None)
        uT_p = [ctx.sb("uT2", [64, 2048], BF16) for _ in range(2)]
        srow_p = [ctx.sb("srow2", [128, 2048]) for _ in range(2)]
        pbf_p = [ctx.sb("pbf2", [128, 2048], BF16) for _ in range(2)]
        pts4_p = [ctx.sb("pts42", [128, 512], BF16) for _ in range(2)]
        nmax_p = [ctx.sb("nmax2", [128, 1]) for _ in range(2)]
        rsum_p = [ctx.sb("rsum2", [128, 1]) for _ in range(2)]
        ssum_p = [ctx.sb("ssum2", [128, 1]) for _ in range(2)]
        pts_ctr = [0]

        def stageA(h, qi):
            hc = h // 2
            hr = (h % 2) * 64
            par = (h * NTILES + qi) % 2
            uT, uT_b = uT_p[h % 2]
            srow, srow_b = srow_p[par]
            pbf, pbf_b = pbf_p[par]
            nmax, nmax_b = nmax_p[par]
            rsum, rsum_b = rsum_p[par]
            ssum, ssum_b = ssum_p[par]
            if qi == 0:
                for tq in range(4):
                    bank, bank_b = ctx.take_bank()
                    p.op("tensor", lambda e, bank=bank, hr=hr, h=h, uT=uT,
                         xh=xnT[hr:hr + 64, hc, :].rearrange("p (a b) -> p a b", b=512)[:, tq, :]: e.matmul(
                        bank.ap()[0:64, :], Tb[hr:hr + 64, h, :], xh,
                        start=True, stop=True, tile_position=(hr, 0)),
                         reads=[Tb_b, xnT_buf], writes=[bank_b], tag="umm")
                    p.op("vector", lambda e, tq=tq, bank=bank, uT=uT: e.tensor_copy(
                        uT[:, 512 * tq:512 * tq + 512], bank.ap()[0:64, :]),
                         reads=[bank_b], writes=[uT_b], tag="uev")
            sb_banks = [ctx.take_bank() for _ in range(2)]
            for kj in range(4):
                bank, bank_b = sb_banks[kj % 2]
                p.op("tensor", lambda e, qi=qi, kj=kj, bank=bank, uT=uT: e.matmul(
                    bank.ap(), uT[:, 128 * qi:128 * qi + 128],
                    uT[:, 512 * kj:512 * kj + 512], start=True, stop=True),
                     reads=[uT_b], writes=[bank_b], tag="smm2")
                if kj == 0:
                    p.op("vector", lambda e, kj=kj, bank=bank, srow=srow, nmax=nmax: e.tensor_scalar(
                        srow[:, 512 * kj:512 * kj + 512], bank.ap(), 0.125, -1e30,
                        op0=ALU.mult, op1=ALU.max, accum_out=nmax[:]),
                         reads=[bank_b], writes=[srow_b, nmax_b], tag="sev2")
                else:
                    p.op("vector", lambda e, kj=kj, bank=bank, srow=srow, nmax=nmax: e.tensor_scalar(
                        srow[:, 512 * kj:512 * kj + 512], bank.ap(), 0.125, nmax[:],
                        op0=ALU.mult, op1=ALU.max, accum_out=nmax[:]),
                         reads=[bank_b, nmax_b], writes=[srow_b, nmax_b], tag="sev2")
            p.op("vector", lambda e, nmax=nmax: e.tensor_scalar(
                nmax[:], nmax[:], -1.0, None, op0=ALU.mult),
                 reads=[nmax_b], writes=[nmax_b], tag="snneg2")
            p.op("scalar", lambda e, pbf=pbf, srow=srow, nmax=nmax, ssum=ssum: e.activation(
                pbf[:], srow[:], AF.Exp, bias=nmax[:], scale=1.0, accum_out=ssum[:]),
                 reads=[srow_b, nmax_b], writes=[pbf_b, ssum_b], tag="sexp2")
            p.op("vector", lambda e, rsum=rsum, ssum=ssum: e.reciprocal(rsum[:], ssum[:]),
                 reads=[ssum_b], writes=[rsum_b], tag="srec2")
            p.op("vector", lambda e, pbf=pbf, rsum=rsum: e.tensor_scalar(
                pbf[:], pbf[:], rsum[:], None, op0=ALU.mult),
                 reads=[pbf_b, rsum_b], writes=[pbf_b], tag="pscale2")

        def stageB(h, qi):
            hc = h // 2
            hr = (h % 2) * 64
            par = (h * NTILES + qi) % 2
            pbf, pbf_b = pbf_p[par]
            cbank, cbank_b, cbank_i = ctx.take_bank(hold=True)
            for g0 in range(0, NTILES, 4):
                pt4, pt4_b = pts4_p[pts_ctr[0] % 2]
                pts_ctr[0] += 1
                tbank, tbank_b = ctx.take_bank()
                tb = tbank.ap().bitcast(BF16)
                for j in range(4):
                    p.op("tensor", lambda e, g0=g0, j=j, tb=tb, pbf=pbf: e.transpose(
                        tb[:, 128 * j:128 * j + 128], pbf[:, 128 * (g0 + j):128 * (g0 + j) + 128],
                        ctx.ident[:]),
                         reads=[pbf_b, ctx.ident_buf], writes=[tbank_b], tag="pxp2")
                p.op("vector", lambda e, tb=tb, pt4=pt4: e.tensor_copy(pt4[:], tb[:, 0:512]),
                     reads=[tbank_b], writes=[pt4_b], tag="pxpe2")
                for j in range(4):
                    kj = g0 + j
                    p.op("tensor", lambda e, kj=kj, j=j, h=h, cbank=cbank, pt4=pt4: e.matmul(
                        cbank.ap()[0:64, 0:128], vtok[:, kj, 64 * h:64 * h + 64], pt4[:, 128 * j:128 * j + 128],
                        start=(kj == 0), stop=(kj == NTILES - 1)),
                         reads=[pt4_b, vtok_b], writes=[cbank_b], tag="pvmm2")
            p.op("vector", lambda e, qi=qi, hr=hr, hc=hc, cbank=cbank: e.tensor_copy(
                ctxT[hr:hr + 64, hc, 128 * qi:128 * qi + 128], cbank.ap()[0:64, 0:128]),
                 reads=[cbank_b], writes=[ctxT_buf], tag="ctxev2")
            ctx.release_bank(cbank_i)

        hq = [(h, qi) for h in range(16) for qi in range(NTILES)]
        stageA(*hq[0])
        for i in range(len(hq)):
            if i + 1 < len(hq):
                stageA(*hq[i + 1])
            stageB(*hq[i])
    woT, woT_b = ctx.sb("woT2", [128, 8, 1024], BF16)
    p.dma("gpsimd", lambda e: e.dma_start(out=woT[:], in_=din["l2a_woT"].rearrange("(c p) d -> p c d", p=128)), writes=[woT_b], tag="ldwo")
    if True:
        for ti in range(NTILES):
            for dh in range(2):
                bank, bank_b = ctx.take_bank()
                mm = [(ctxT[:, c, 128 * ti:128 * ti + 128], woT[:, c, 512 * dh:512 * dh + 512],
                       [ctxT_buf, woT_b]) for c in range(8)]
                mm.append((ctx.ones_row_bf[0:1, :], bvo[:, 1024 + 512 * dh:1024 + 512 * dh + 512],
                           [ctx.onesrbf_buf, bvo_b]))
                ctx.matmul_acc(bank.ap(), bank_b, mm, tag="omm2")
                p.op("vector", lambda e, ti=ti, dh=dh, bank=bank: e.tensor_tensor(
                    x_t[:, ti, 512 * dh:512 * dh + 512], x_t[:, ti, 512 * dh:512 * dh + 512],
                    bank.ap(), op=ALU.add),
                     reads=[bank_b, x_bufs[ti]], writes=[x_bufs[ti]], tag="oadd2")


def bcast_row(ctx, row_ap, row_buf, width, tag="bc"):
    p = ctx.prog
    bf = row_ap.dtype == BF16
    ones = ctx.ones_row_bf if bf else ctx.ones_row
    ones_b = ctx.onesrbf_buf if bf else ctx.onesr_buf
    out, out_b = ctx.sb(f"bc_{tag}", [128, width])
    for j in range(0, width, 512):
        w = min(512, width - j)
        bank, bank_b = ctx.take_bank()
        p.op("tensor", lambda e, j=j, w=w, bank=bank: e.matmul(
            bank.ap()[:, 0:w], ones[0:1, :], row_ap[:, j:j + w], start=True, stop=True),
             reads=[row_buf, ones_b], writes=[bank_b], tag=f"bcmm_{tag}")
        p.op("vector", lambda e, j=j, w=w, bank=bank: e.tensor_copy(out[:, j:j + w], bank.ap()[:, 0:w]),
             reads=[bank_b], writes=[out_b], tag=f"bcev_{tag}")
    return out, out_b


def emit_L3_ffn(ctx, din, x_t, x_bufs):
    p = ctx.prog
    xnT, xnT_buf = ctx.sb("xnT_3f", [128, 8, 2048], BF16)
    ln_to_featmajor(ctx, x_t, x_bufs, xnT, xnT_buf)
    with ctx.scope("l3f"):
        win, win_b = ctx.sb("win", [128, 8, 64], BF16)
        p.dma("gpsimd", lambda e: e.dma_start(out=win[:], in_=din["l3f_win"].rearrange("(c p) d -> p c d", p=128)), writes=[win_b], tag="ldwin")
        wout, wout_b = ctx.sb("wout", [64, 1024], BF16)
        p.dma("gpsimd", lambda e: e.dma_start(out=wout[:], in_=din["l3f_wout"]), writes=[wout_b], tag="ldwout")
        rows, rows_b = ctx.sb("l3f_rows", [1, 4224])
        p.dma("sync", lambda e: e.dma_start(out=rows[:, 0:1024], in_=din["l3f_w1"]), writes=[rows_b], tag="ldr")
        p.dma("sync", lambda e: e.dma_start(out=rows[:, 1024:2048], in_=din["l3f_b1"]), writes=[rows_b], tag="ldr")
        p.dma("sync", lambda e: e.dma_start(out=rows[:, 2048:3072], in_=din["l3f_w2"]), writes=[rows_b], tag="ldr")
        p.dma("sync", lambda e: e.dma_start(out=rows[:, 3072:3136], in_=din["l3f_b2"]), writes=[rows_b], tag="ldr")
        p.dma("sync", lambda e: e.dma_start(out=rows[:, 3136:3200], in_=din["l3f_bin"]), writes=[rows_b], tag="ldr")
        p.dma("sync", lambda e: e.dma_start(out=rows[:, 3200:4224], in_=din["l3f_bout"]), writes=[rows_b], tag="ldr")
        w1bc, w1bc_b = bcast_row(ctx, rows[:, 0:1024], rows_b, 1024, "w1")
        b1bc, b1bc_b = bcast_row(ctx, rows[:, 1024:2048], rows_b, 1024, "b1")
        w2bc, w2bc_b = bcast_row(ctx, rows[:, 2048:3072], rows_b, 1024, "w2")
        b2bc, b2bc_b = bcast_row(ctx, rows[:, 3072:3136], rows_b, 64, "b2")

        sin_, sin_b = ctx.sb("l3f_s", [128, 64])
        hh, hh_b = ctx.sb("l3f_h", [128, 1024])
        gg, gg_b = ctx.sb("l3f_g", [128, 64])
        ggbf, ggbf_b = ctx.sb("l3f_gbf", [128, 64], BF16)
        gT, gT_b = ctx.sb("l3f_gT", [64, 128], BF16)
        for ti in range(NTILES):
            bank, bank_b = ctx.take_bank()
            mm = [(xnT[:, c, 128 * ti:128 * ti + 128], win[:, c, :], [xnT_buf, win_b]) for c in range(8)]
            mm.append((ctx.ones_row[0:1, :], rows[:, 3136:3200], [ctx.onesr_buf, rows_b]))
            ctx.matmul_acc(bank.ap()[:, 0:64], bank_b, mm, tag="smm3f")
            p.op("vector", lambda e, bank=bank: e.tensor_copy(sin_[:], bank.ap()[:, 0:64]),
                 reads=[bank_b], writes=[sin_b], tag="sev3f")
            p.op("gpsimd", lambda e: e.tensor_tensor(
                hh[:].rearrange("p (m i) -> p m i", i=16),
                sin_[:].unsqueeze(2).broadcast_to((128, 64, 16)),
                w1bc[:].rearrange("p (m i) -> p m i", i=16), op=ALU.mult),
                 reads=[sin_b, w1bc_b], writes=[hh_b], tag="hmul3f")
            p.op("vector", lambda e: e.tensor_tensor(hh[:], hh[:], b1bc[:], op=ALU.add),
                 reads=[hh_b, b1bc_b], writes=[hh_b], tag="hadd3f")
            p.op("scalar", lambda e: e.activation(hh[:], hh[:], AF.Relu, bias=0.0, scale=1.0),
                 reads=[hh_b], writes=[hh_b], tag="hrelu3f")
            p.op("vector", lambda e: e.tensor_tensor(hh[:], hh[:], w2bc[:], op=ALU.mult),
                 reads=[hh_b, w2bc_b], writes=[hh_b], tag="gmul3f")
            p.op("vector", lambda e: e.tensor_reduce(
                gg[:], hh[:].rearrange("p (m i) -> p m i", i=16), axis=AX.X, op=ALU.add),
                 reads=[hh_b], writes=[gg_b], tag="gred3f")
            p.op("vector", lambda e: e.tensor_tensor(gg[:], gg[:], b2bc[:], op=ALU.add),
                 reads=[gg_b, b2bc_b], writes=[gg_b], tag="gb23f")
            p.op("vector", lambda e: e.tensor_copy(ggbf[:], gg[:]),
                 reads=[gg_b], writes=[ggbf_b], tag="gcast3f")
            bx, bx_b = ctx.take_bank()
            bxv = bx.ap().bitcast(BF16)
            p.op("tensor", lambda e, bxv=bxv: e.transpose(bxv[0:64, 0:128], ggbf[:], ctx.ident[:]),
                 reads=[ggbf_b, ctx.ident_buf], writes=[bx_b], tag="gxp3f")
            p.op("vector", lambda e, bxv=bxv: e.tensor_copy(gT[:], bxv[0:64, 0:128]),
                 reads=[bx_b], writes=[gT_b], tag="gxpe3f")
            for dh in range(2):
                bank2, bank2_b = ctx.take_bank()
                mm2 = [(gT[:], wout[:, 512 * dh:512 * dh + 512], [gT_b, wout_b])]
                mm2.append((ctx.ones_row[0:1, :], rows[:, 3200 + 512 * dh:3200 + 512 * dh + 512],
                            [ctx.onesr_buf, rows_b]))
                ctx.matmul_acc(bank2.ap(), bank2_b, mm2, tag="omm3f")
                p.op("vector", lambda e, ti=ti, dh=dh, bank2=bank2: e.tensor_tensor(
                    x_t[:, ti, 512 * dh:512 * dh + 512], x_t[:, ti, 512 * dh:512 * dh + 512],
                    bank2.ap(), op=ALU.add),
                     reads=[bank2_b, x_bufs[ti]], writes=[x_bufs[ti]], tag="oadd3f")


def emit_L2_ffn(ctx, din, x_t, x_bufs):
    p = ctx.prog
    attn_out, attn_out_b = ctx.sb("l2f_attn", [128, 16, 1024], BF16)

    with ctx.scope("l2f_attn"):
        lnr, lnr_b = ctx.sb("l2f_lnrows1", [1, 2048])
        p.dma("sync", lambda e: e.dma_start(out=lnr[:, 0:1024], in_=din["l2f_ln1w"]), writes=[lnr_b], tag="ldln")
        p.dma("sync", lambda e: e.dma_start(out=lnr[:, 1024:2048], in_=din["l2f_ln1b"]), writes=[lnr_b], tag="ldln")
        w1bc, w1bc_b = bcast_row(ctx, lnr[:, 0:1024], lnr_b, 1024, "ln1w")
        b1bc, b1bc_b = bcast_row(ctx, lnr[:, 1024:2048], lnr_b, 1024, "ln1b")
        ipb, ipb_b = ctx.sb("ipb", [1, 1024], BF16)
        wpart, wpart_b = ctx.sb("wpart", [128, 8, 512], BF16)
        hnTbt, hnTbt_b = ctx.sb("hnTbt", [128, 8, 1024], BF16)
        kk, kk_b = ctx.sb("l2f_k", [128, 8, 1024], BF16)
        vv, vv_b = ctx.sb("l2f_v", [128, 8, 1024], BF16)
        S, S_b = ctx.sb("l2f_S", [128, 16, 8, 8])
        P, P_b = S, S_b
        nmaxs, nmaxs_b = ctx.sb("l2f_nm", [128, 16])
        zs, zs_b = ctx.sb("l2f_z", [128, 16])
        rz, rz_b = ctx.sb("l2f_rz", [128, 16])
        hacc, hacc_b = ctx.sb("l2f_hacc", [128, 1024])
        htmp, htmp_b = ctx.sb("l2f_htmp", [128, 1024])
        for bt in range(2):
            ln_to_featmajor(ctx, x_t, x_bufs, hnTbt, hnTbt_b,
                            tile_list=list(range(8 * bt, 8 * bt + 8)),
                            affine=(w1bc, w1bc_b, b1bc, b1bc_b))
            for part, dst, dst_b in ((1, kk, kk_b), (2, vv, vv_b)):
                p.dma("gpsimd", lambda e, part=part: e.dma_start(
                    out=ipb[:], in_=din["l2f_inprojb"][:, 1024 * part:1024 * part + 1024]),
                      writes=[ipb_b], tag="ldipb")
                for dh in range(2):
                    p.dma("gpsimd", lambda e, part=part, dh=dh: e.dma_start(
                        out=wpart[:], in_=din["l2f_inprojT"][:, 1024 * part + 512 * dh:1024 * part + 512 * dh + 512].rearrange("(c p) d -> p c d", p=128)),
                          writes=[wpart_b], tag="ldwp")
                    for s in range(8):
                        bank, bank_b = ctx.take_bank()
                        mm = [(hnTbt[:, c, 128 * s:128 * s + 128], wpart[:, c, :],
                               [hnTbt_b, wpart_b]) for c in range(8)]
                        mm.append((ctx.ones_row_bf[0:1, :], ipb[:, 512 * dh:512 * dh + 512],
                                   [ctx.onesrbf_buf, ipb_b]))
                        ctx.matmul_acc(bank.ap(), bank_b, mm, tag="kvmm")
                        p.op("scalar", lambda e, s=s, dh=dh, bank=bank, dst=dst: e.activation(
                            dst[:, s, 512 * dh:512 * dh + 512], bank.ap(), AF.Copy, bias=0.0, scale=1.0),
                             reads=[bank_b], writes=[dst_b], tag="kvev")
            p.dma("gpsimd", lambda e: e.dma_start(
                out=ipb[:], in_=din["l2f_inprojb"][:, 0:1024]), writes=[ipb_b], tag="ldipq")
            for qh in range(2):
                p.dma("gpsimd", lambda e, qh=qh: e.dma_start(
                    out=wpart[:], in_=din["l2f_inprojT"][:, 512 * qh:512 * qh + 512].rearrange("(c p) d -> p c d", p=128)),
                      writes=[wpart_b], tag="ldwq")
                for s in range(8):
                    bank, bank_b, bank_i = ctx.take_bank(hold=True)
                    mm = [(hnTbt[:, c, 128 * s:128 * s + 128], wpart[:, c, :],
                           [hnTbt_b, wpart_b]) for c in range(8)]
                    mm.append((ctx.ones_row_bf[0:1, :], ipb[:, 512 * qh:512 * qh + 512],
                               [ctx.onesrbf_buf, ipb_b]))
                    ctx.matmul_acc(bank.ap(), bank_b, mm, tag="qmm2f")
                    for t in range(s + 1):
                        p.op("vector", lambda e, s=s, t=t, qh=qh, bank=bank: e.tensor_tensor(
                            htmp[:, 0:512], bank.ap(), kk[:, t, 512 * qh:512 * qh + 512], op=ALU.mult),
                             reads=[bank_b, kk_b], writes=[htmp_b], tag="qk2f")
                        p.op("vector", lambda e, s=s, t=t, qh=qh: e.tensor_reduce(
                            S[:, 8 * qh:8 * qh + 8, s, t].unsqueeze(2),
                            htmp[:, 0:512].rearrange("p (n d) -> p n d", d=64),
                            axis=AX.X, op=ALU.add),
                             reads=[htmp_b], writes=[S_b], tag="sred2f")
                    ctx.release_bank(bank_i)
            for s in range(8):
                p.op("vector", lambda e, s=s: e.tensor_scalar(
                    S[:, :, s, 0:s + 1], S[:, :, s, 0:s + 1], 0.125, None, op0=ALU.mult),
                     reads=[S_b], writes=[S_b], tag="ssc2f")
                p.op("vector", lambda e, s=s: e.tensor_reduce(
                    nmaxs[:], S[:, :, s, 0:s + 1], axis=AX.X, op=ALU.max, negate=True),
                     reads=[S_b], writes=[nmaxs_b], tag="smax2f")
                p.op("gpsimd", lambda e, s=s: e.tensor_tensor(
                    P[:, :, s, 0:s + 1], S[:, :, s, 0:s + 1],
                    nmaxs[:].unsqueeze(2).broadcast_to((128, 16, s + 1)), op=ALU.add),
                     reads=[S_b, nmaxs_b], writes=[P_b], tag="psub2f")
                p.op("scalar", lambda e, s=s: e.activation(
                    P[:, :, s, 0:s + 1], P[:, :, s, 0:s + 1], AF.Exp, bias=0.0, scale=1.0),
                     reads=[P_b], writes=[P_b], tag="pexp2f")
                p.op("vector", lambda e, s=s: e.tensor_reduce(
                    zs[:], P[:, :, s, 0:s + 1], axis=AX.X, op=ALU.add),
                     reads=[P_b], writes=[zs_b], tag="psum2f")
                p.op("vector", lambda e: e.reciprocal(rz[:], zs[:]),
                     reads=[zs_b], writes=[rz_b], tag="prec2f")
                p.op("gpsimd", lambda e, s=s: e.tensor_tensor(
                    P[:, :, s, 0:s + 1], P[:, :, s, 0:s + 1],
                    rz[:].unsqueeze(2).broadcast_to((128, 16, s + 1)), op=ALU.mult),
                     reads=[P_b, rz_b], writes=[P_b], tag="pnorm2f")
                for t in range(s + 1):
                    p.op("gpsimd", lambda e, s=s, t=t: e.tensor_tensor(
                        htmp[:].rearrange("p (n d) -> p n d", d=64),
                        vv[:, t, :].rearrange("p (n d) -> p n d", d=64),
                        P[:, :, s, t].unsqueeze(2).broadcast_to((128, 16, 64)), op=ALU.mult),
                         reads=[vv_b, P_b], writes=[htmp_b], tag="pv2f")
                    if t == 0:
                        p.op("vector", lambda e: e.tensor_copy(hacc[:], htmp[:]),
                             reads=[htmp_b], writes=[hacc_b], tag="pva2f")
                    else:
                        p.op("vector", lambda e: e.tensor_tensor(hacc[:], hacc[:], htmp[:], op=ALU.add),
                             reads=[htmp_b, hacc_b], writes=[hacc_b], tag="pvb2f")
                p.op("vector", lambda e, bt=bt, s=s: e.tensor_copy(attn_out[:, 8 * bt + s, :], hacc[:]),
                     reads=[hacc_b], writes=[attn_out_b], tag="hastash")

    with ctx.scope("l2f_oproj"):
        opw, opw_b = ctx.sb("opw", [128, 8, 1024], BF16)
        p.dma("gpsimd", lambda e: e.dma_start(out=opw[:], in_=din["l2f_outprojT"].rearrange("(c p) d -> p c d", p=128)), writes=[opw_b], tag="ldopw")
        opb, opb_b = ctx.sb("opb", [1, 1024], BF16)
        p.dma("gpsimd", lambda e: e.dma_start(out=opb[:], in_=din["l2f_outprojb"]), writes=[opb_b], tag="ldopb")
        atT, atT_b = ctx.sb("l2f_atT", [128, 8, 128], BF16)
        for ti in range(NTILES):
            for g in range(2):
                bank, bank_b = ctx.take_bank()
                bv = bank.ap().bitcast(BF16)
                for j in range(4):
                    p.op("tensor", lambda e, ti=ti, g=g, j=j, bv=bv: e.transpose(
                        bv[:, 128 * j:128 * j + 128], attn_out[:, ti, 512 * g + 128 * j:512 * g + 128 * j + 128],
                        ctx.ident[:]),
                         reads=[attn_out_b, ctx.ident_buf], writes=[bank_b], tag="haxp")
                p.op("vector", lambda e, g=g, bv=bv: e.tensor_copy(
                    atT[:, 4 * g:4 * g + 4, :], bv[:, 0:512].rearrange("p (a b) -> p a b", b=128)),
                     reads=[bank_b], writes=[atT_b], tag="haxpe")
            for dh in range(2):
                bank, bank_b = ctx.take_bank()
                mm = [(atT[:, c, :], opw[:, c, 512 * dh:512 * dh + 512], [atT_b, opw_b]) for c in range(8)]
                mm.append((ctx.ones_row_bf[0:1, :], opb[:, 512 * dh:512 * dh + 512], [ctx.onesrbf_buf, opb_b]))
                ctx.matmul_acc(bank.ap(), bank_b, mm, tag="opmm")
                p.op("scalar", lambda e, ti=ti, dh=dh, bank=bank: e.activation(
                    attn_out[:, ti, 512 * dh:512 * dh + 512], bank.ap(), AF.Copy, bias=0.0, scale=1.0),
                     reads=[bank_b], writes=[attn_out_b], tag="opev")

    hn2T, hn2T_buf = ctx.sb("hn2T", [128, 8, 2048], BF16)
    gates, gates_b = ctx.sb("l2f_gates", [128, 16, 8])
    with ctx.scope("l2f_mid"):
        lnr2, lnr2_b = ctx.sb("l2f_lnrows2", [1, 2048])
        p.dma("sync", lambda e: e.dma_start(out=lnr2[:, 0:1024], in_=din["l2f_ln2w"]), writes=[lnr2_b], tag="ldln")
        p.dma("sync", lambda e: e.dma_start(out=lnr2[:, 1024:2048], in_=din["l2f_ln2b"]), writes=[lnr2_b], tag="ldln")
        w2bc, w2bc_b = bcast_row(ctx, lnr2[:, 0:1024], lnr2_b, 1024, "ln2w")
        b2bc, b2bc_b = bcast_row(ctx, lnr2[:, 1024:2048], lnr2_b, 1024, "ln2b")
        rt, rt_b = ctx.sb("routerT", [128, 8, 8], BF16)
        p.dma("gpsimd", lambda e: e.dma_start(out=rt[:], in_=din["l2f_routerT"].rearrange("(c p) d -> p c d", p=128)), writes=[rt_b], tag="ldrt")
        x1t, x1t_b = ctx.sb("x1t", [128, 1, 1024])
        for ti in range(NTILES):
            _ln_to_featmajor(ctx, x_t, x_bufs, None, None, tile_list=[ti],
                             xn_out=(x1t, [x1t_b] * NTILES))
            p.op("vector", lambda e, ti=ti: e.tensor_tensor(
                x1t[:, 0, :], x1t[:, 0, :], attn_out[:, ti, :], op=ALU.add),
                 reads=[x1t_b, attn_out_b], writes=[x1t_b], tag="x1add")
            p.op("vector", lambda e, ti=ti: e.tensor_tensor(
                x_t[:, ti, :], x_t[:, ti, :], x1t[:, 0, :], op=ALU.add),
                 reads=[x1t_b, x_bufs[ti]], writes=[x_bufs[ti]], tag="strx1")
            _ln_to_featmajor(ctx, x1t, [x1t_b] * NTILES, hn2T, hn2T_buf, tile_list=[0],
                             dst_off=ti, affine=(w2bc, w2bc_b, b2bc, b2bc_b))
        lgt, lgt_b = ctx.sb("lgt2", [128, 8])
        m1, m1_b = ctx.sb("m1_2", [128, 1])
        m2, m2_b = ctx.sb("m2_2", [128, 1])
        eq1, eq1_b = ctx.sb("eq1_2", [128, 8])
        eq2, eq2_b = ctx.sb("eq2_2", [128, 8])
        zsum, zsum_b = ctx.sb("zsum2", [128, 1])
        for ti in range(NTILES):
            bank, bank_b = ctx.take_bank()
            mm = [(hn2T[:, c, 128 * ti:128 * ti + 128], rt[:, c, :], [hn2T_buf, rt_b]) for c in range(8)]
            ctx.matmul_acc(bank.ap()[:, 0:8], bank_b, mm, tag="rtmm")
            p.op("vector", lambda e, bank=bank: e.tensor_copy(lgt[:], bank.ap()[:, 0:8]),
                 reads=[bank_b], writes=[lgt_b], tag="rtev")
            p.op("vector", lambda e: e.tensor_reduce(m1[:], lgt[:], axis=AX.X, op=ALU.max, negate=True),
                 reads=[lgt_b], writes=[m1_b], tag="rtm1")
            p.op("scalar", lambda e: e.activation(lgt[:], lgt[:], AF.Exp, bias=m1[:], scale=1.0, accum_out=zsum[:]),
                 reads=[lgt_b, m1_b], writes=[lgt_b, zsum_b], tag="rtexp")
            p.op("vector", lambda e: e.reciprocal(m2[:], zsum[:]),
                 reads=[zsum_b], writes=[m2_b], tag="rtrec")
            p.op("vector", lambda e: e.tensor_scalar(lgt[:], lgt[:], m2[:], None, op0=ALU.mult),
                 reads=[lgt_b, m2_b], writes=[lgt_b], tag="rtnorm")
            p.op("vector", lambda e: e.tensor_reduce(m1[:], lgt[:], axis=AX.X, op=ALU.max),
                 reads=[lgt_b], writes=[m1_b], tag="g2m1")
            p.op("vector", lambda e: e.tensor_scalar(eq1[:], lgt[:], m1[:], None, op0=ALU.is_ge),
                 reads=[lgt_b, m1_b], writes=[eq1_b], tag="g2eq1")
            p.op("vector", lambda e: e.tensor_scalar(eq2[:], eq1[:], -1e30, None, op0=ALU.mult),
                 reads=[eq1_b], writes=[eq2_b], tag="g2msk")
            p.op("vector", lambda e: e.tensor_tensor(eq2[:], lgt[:], eq2[:], op=ALU.add),
                 reads=[lgt_b, eq2_b], writes=[eq2_b], tag="g2l2")
            p.op("vector", lambda e: e.tensor_reduce(m2[:], eq2[:], axis=AX.X, op=ALU.max),
                 reads=[eq2_b], writes=[m2_b], tag="g2m2")
            p.op("vector", lambda e: e.tensor_scalar(eq2[:], eq2[:], m2[:], None, op0=ALU.is_ge),
                 reads=[eq2_b, m2_b], writes=[eq2_b], tag="g2eq2")
            p.op("vector", lambda e: e.tensor_tensor(eq1[:], eq1[:], eq2[:], op=ALU.add),
                 reads=[eq1_b, eq2_b], writes=[eq1_b], tag="g2or")
            p.op("vector", lambda e, ti=ti: e.tensor_tensor(gates[:, ti, :], lgt[:], eq1[:], op=ALU.mult),
                 reads=[lgt_b, eq1_b], writes=[gates_b], tag="g2g")

    with ctx.scope("l2f_moe"):
        w1h, w1h_b = ctx.sb("w1h2", [128, 8, 1024], BF16)
        w2h, w2h_b = ctx.sb("w2h2", [128, 8, 1024], BF16)
        hT, hT_b = ctx.sb("hT2", [128, 8, 2048], BF16)
        tmp, tmp_b = ctx.sb("l2ftmp", [128, 512])
        for e_i in range(8):
            for fh in range(2):
                p.dma("gpsimd", lambda e, e_i=e_i, fh=fh: e.dma_start(
                    out=w1h[:], in_=din["l2f_w1"][e_i, :, 1024 * fh:1024 * fh + 1024].rearrange("(c p) f -> p c f", p=128)),
                      writes=[w1h_b], tag="ld_w1h2")
                p.dma("gpsimd", lambda e, e_i=e_i, fh=fh: e.dma_start(
                    out=w2h[:], in_=din["l2f_w2"][e_i, 1024 * fh:1024 * fh + 1024, :].rearrange("(c p) f -> p c f", p=128)),
                      writes=[w2h_b], tag="ld_w2h2")
                for fs in range(8):
                    bank, bank_b = ctx.take_bank()
                    for ts in range(4):
                        mm = [(w1h[:, c, 128 * fs:128 * fs + 128],
                               hn2T[:, c, 512 * ts:512 * ts + 512], [w1h_b, hn2T_buf]) for c in range(8)]
                        ctx.matmul_acc(bank.ap(), bank_b, mm, tag="moeh2")
                        p.op("scalar", lambda e, fs=fs, ts=ts, bank=bank: e.activation(
                            hT[:, fs, 512 * ts:512 * ts + 512], bank.ap(), AF.Gelu, bias=0.0, scale=1.0),
                             reads=[bank_b], writes=[hT_b], tag="moegelu")
                for ti in range(NTILES):
                    for dh in range(2):
                        bank, bank_b = ctx.take_bank()
                        mm = [(hT[:, fs, 128 * ti:128 * ti + 128], w2h[:, fs, 512 * dh:512 * dh + 512],
                               [hT_b, w2h_b]) for fs in range(8)]
                        ctx.matmul_acc(bank.ap(), bank_b, mm, tag="moey2")
                        p.op("vector", lambda e, e_i=e_i, ti=ti, bank=bank: e.tensor_scalar(
                            tmp[:], bank.ap(), gates[:, ti, e_i:e_i + 1], None, op0=ALU.mult),
                             reads=[bank_b, gates_b], writes=[tmp_b], tag="moesc2")
                        p.op("vector", lambda e, ti=ti, dh=dh: e.tensor_tensor(
                            x_t[:, ti, 512 * dh:512 * dh + 512], x_t[:, ti, 512 * dh:512 * dh + 512],
                            tmp[:], op=ALU.add),
                             reads=[tmp_b, x_bufs[ti]], writes=[x_bufs[ti]], tag="moeadd2")


def emit_L3_attn(ctx, din, x_t, x_bufs):
    p = ctx.prog
    RT2 = 128 ** -0.5
    lnr, lnr_b = ctx.sb("l3a_lnr", [1, 2048], BF16)
    p.dma("gpsimd", lambda e: e.dma_start(out=lnr[:, 0:1024], in_=din["l3a_lnw"]), writes=[lnr_b], tag="l3ln")
    p.dma("gpsimd", lambda e: e.dma_start(out=lnr[:, 1024:2048], in_=din["l3a_lnb"]), writes=[lnr_b], tag="l3ln")
    omr, omr_b = ctx.sb("l3a_omr", [1, 1024], BF16)
    p.dma("gpsimd", lambda e: e.dma_start(out=omr[:], in_=din["l3a_omega"]), writes=[omr_b], tag="l3om")
    gbr, gbr_b = ctx.sb("l3a_gbr", [1, 512])
    p.dma("sync", lambda e: e.dma_start(out=gbr[:], in_=din["l3a_gambet"]), writes=[gbr_b], tag="l3gb")
    g0bc, g0bc_b = bcast_row(ctx, gbr[:, 0:128], gbr_b, 128, "g0")
    g1bc, g1bc_b = bcast_row(ctx, gbr[:, 128:256], gbr_b, 128, "g1")
    b0bc, b0bc_b = bcast_row(ctx, gbr[:, 256:384], gbr_b, 128, "b0")
    b1bc, b1bc_b = bcast_row(ctx, gbr[:, 384:512], gbr_b, 128, "b1")

    biasbc, biasbc_b = ctx.sb("biasbc", [128, 64])
    with ctx.scope("l3a_rot"):
        rcs, rcs_b = ctx.sb("rcs", [8, 128])
        p.dma("sync", lambda e: e.dma_start(out=rcs[:, 0:64], in_=din["l3a_rcos"]), writes=[rcs_b], tag="rc")
        p.dma("sync", lambda e: e.dma_start(out=rcs[:, 64:128], in_=din["l3a_rsin"]), writes=[rcs_b], tag="rc")
        rbr, rbr_b = ctx.sb("rbr", [1, 256])
        p.dma("sync", lambda e: e.dma_start(out=rbr[:], in_=din["l3a_rb"]), writes=[rbr_b], tag="rb")
        rots = []
        for ri in range(2):
            we, we_b = ctx.sb("rwe", [8, 128])
            bank, bank_b = ctx.take_bank()
            p.op("tensor", lambda e, ri=ri, bank=bank: e.matmul(
                bank.ap()[0:8, 0:128], ctx.ones_row[0:1, 0:8], rbr[:, 128 * ri:128 * ri + 128],
                start=True, stop=True),
                 reads=[rbr_b, ctx.onesr_buf], writes=[bank_b], tag="rbc")
            p.op("vector", lambda e, bank=bank, we=we: e.tensor_copy(we[:], bank.ap()[0:8, 0:128]),
                 reads=[bank_b], writes=[we_b], tag="rbev")
            rot, rot_b = ctx.sb("rot", [8, 128])
            t4, t4_b = ctx.sb("rt4", [8, 64])
            p.op("vector", lambda e, we=we, rot=rot: e.tensor_tensor(rot[:, 0:64], we[:, 0:64], rcs[:, 0:64], op=ALU.mult),
                 reads=[we_b, rcs_b], writes=[rot_b], tag="r1")
            p.op("vector", lambda e, we=we, t4=t4: e.tensor_tensor(t4[:], we[:, 64:128], rcs[:, 64:128], op=ALU.mult),
                 reads=[we_b, rcs_b], writes=[t4_b], tag="r2")
            p.op("vector", lambda e, rot=rot, t4=t4: e.tensor_tensor(rot[:, 0:64], rot[:, 0:64], t4[:], op=ALU.subtract),
                 reads=[t4_b, rot_b], writes=[rot_b], tag="r3")
            p.op("vector", lambda e, we=we, rot=rot: e.tensor_tensor(rot[:, 64:128], we[:, 64:128], rcs[:, 0:64], op=ALU.mult),
                 reads=[we_b, rcs_b], writes=[rot_b], tag="r4")
            p.op("vector", lambda e, we=we, t4=t4: e.tensor_tensor(t4[:], we[:, 0:64], rcs[:, 64:128], op=ALU.mult),
                 reads=[we_b, rcs_b], writes=[t4_b], tag="r5")
            p.op("vector", lambda e, rot=rot, t4=t4: e.tensor_tensor(rot[:, 64:128], rot[:, 64:128], t4[:], op=ALU.add),
                 reads=[t4_b, rot_b], writes=[rot_b], tag="r6")
            bank2, bank2_b = ctx.take_bank()
            p.op("tensor", lambda e, rot=rot, bank2=bank2: e.transpose(
                bank2.ap()[:, 0:8], rot[:], ctx.ident32[0:8, 0:8]),
                 reads=[rot_b, ctx.ident32_buf], writes=[bank2_b], tag="rxp")
            rT, rT_b = ctx.sb("rotT", [128, 8])
            p.op("vector", lambda e, rT=rT, bank2=bank2: e.tensor_copy(rT[:], bank2.ap()[:, 0:8]),
                 reads=[bank2_b], writes=[rT_b], tag="rxpe")
            rots.append((rT, rT_b))
        bank3, bank3_b = ctx.take_bank()
        p.op("tensor", lambda e, bank3=bank3: e.matmul(
            bank3.ap()[0:8, 0:8], rots[0][0][:], rots[1][0][:], start=True, stop=True),
             reads=[rots[0][1], rots[1][1]], writes=[bank3_b], tag="rbmm")
        b88, b88_b = ctx.sb("b88", [8, 8])
        p.op("vector", lambda e, bank3=bank3: e.tensor_copy(b88[:], bank3.ap()[0:8, 0:8]),
             reads=[bank3_b], writes=[b88_b], tag="rbev2")
        brow64, brow64_b = ctx.sb("brow64", [1, 64])
        for s in range(8):
            p.dma("gpsimd", lambda e, s=s: e.dma_start(out=brow64[:, 8 * s:8 * s + 8], in_=b88[s:s + 1, :]),
                  reads=[b88_b], writes=[brow64_b], tag="rbgth")
        bank4, bank4_b = ctx.take_bank()
        p.op("tensor", lambda e, bank4=bank4: e.matmul(
            bank4.ap()[:, 0:64], ctx.ones_row[0:1, :], brow64[:], start=True, stop=True),
             reads=[brow64_b, ctx.onesr_buf], writes=[bank4_b], tag="rbbc")
        p.op("vector", lambda e, bank4=bank4: e.tensor_copy(biasbc[:], bank4.ap()[:, 0:64]),
             reads=[bank4_b], writes=[biasbc_b], tag="rbbce")

    hb_row, hb_row_b = ctx.sb("l3a_hbrow", [1, 1024], BF16)
    p.dma("gpsimd", lambda e: e.dma_start(out=hb_row[:], in_=din["l3a_hb"]), writes=[hb_row_b], tag="l3hb")
    vb_row, vb_row_b = ctx.sb("l3a_vbrow", [1, 2048], BF16)
    p.dma("gpsimd", lambda e: e.dma_start(out=vb_row[:], in_=din["l3a_vb"]), writes=[vb_row_b], tag="l3vb")
    mxb_row, mxb_row_b = ctx.sb("l3a_mxbrow", [1, 4224], BF16)
    p.dma("gpsimd", lambda e: e.dma_start(out=mxb_row[:], in_=din["l3a_mxb"]), writes=[mxb_row_b], tag="l3mxb")

    d_u = ctx.nc.dram_tensor("l3a_spill_u", [128, 8, 1024], BF16).ap()
    d_u_b = p.buf("d_u")
    d_hx = ctx.nc.dram_tensor("l3a_spill_hx", [128, 8, 1024], BF16).ap()
    d_hx_b = p.buf("d_hx")
    d_v = ctx.nc.dram_tensor("l3a_spill_v", [128, 8, 2048], BF16).ap()
    d_v_b = p.buf("d_v")
    d_r = ctx.nc.dram_tensor("l3a_spill_r", [128, 8, 2048], BF16).ap()
    d_r_b = p.buf("d_r")
    d_xnT = ctx.nc.dram_tensor("l3a_spill_xnT", [128, 8, 1024], BF16).ap()
    d_xnT_b = p.buf("d_xnT")
    d_mxT = ctx.nc.dram_tensor("l3a_spill_mxT", [128, 8, 1024], BF16).ap()
    d_mxT_b = p.buf("d_mxT")

    def _bt_body(bt):
        with ctx.scope("l3a_bt"):
            zz, zz_b = ctx.sb("l3a_z", [128, 8, 128])
            with ctx.scope("l3a_conv"):
                xnT, xnT_buf = ctx.sb("xnT3", [128, 8, 1024], BF16)
                mxT, mxT_b = ctx.sb("mxT", [128, 8, 1024], BF16)
                lnwbc, lnwbc_b = bcast_row(ctx, lnr[:, 0:1024], lnr_b, 1024, "l3w")
                lnbbc, lnbbc_b = bcast_row(ctx, lnr[:, 1024:2048], lnr_b, 1024, "l3b")
                ombc, ombc_b = bcast_row(ctx, omr[:], omr_b, 1024, "l3om")
                xn, xn_b = ctx.sb("l3a_xn", [128, 8, 1024], BF16)
                ln0t, ln0t_b = ctx.sb("l3a_ln0", [128, 1, 1024])
                for s in range(8):
                    _ln_to_featmajor(ctx, x_t, x_bufs, None, None, tile_list=[8 * bt + s],
                                     xn_out=(ln0t, [ln0t_b]))
                    _ln_to_featmajor(ctx, ln0t, [ln0t_b], xnT, xnT_buf, tile_list=[0], dst_off=s,
                                     xn_out=(xn[:, s:s + 1, :], [xn_b]),
                                     affine=(lnwbc, lnwbc_b, lnbbc, lnbbc_b))
                p.dma("sync", lambda e: e.dma_start(out=d_xnT[:], in_=xnT[:]),
                      reads=[xnT_buf], writes=[d_xnT_b], tag="sp_xnT")
                kbc, kbc_b = ctx.sb("kbc", [128, 8, 1024], BF16)
                with ctx.scope("l3a_ema"):
                    qqa, qqa_b = ctx.sb("ema_q", [16, 1024])
                    wwa, wwa_b = ctx.sb("ema_w", [16, 1024])
                    krow, krow_b = ctx.sb("ema_krow", [1, 1024])
                    emat, emat_b = ctx.sb("emaT", [16, 4, 512])
                    for ch in range(2):
                        sl = slice(512 * ch, 512 * ch + 512)
                        p.dma("sync", lambda e, ch=ch: e.dma_start(
                            out=emat[:], in_=din["l3a_emaT"].rearrange("n (k d) -> n k d", k=4)[:, :, 512 * ch:512 * ch + 512]),
                            writes=[emat_b], tag="ldema")
                        p.op("scalar", lambda e, sl=sl: e.activation(wwa[:, sl], emat[:, 0, :], AF.Sigmoid, bias=0.0, scale=1.0),
                             reads=[emat_b], writes=[wwa_b], tag="emp")
                        p.op("scalar", lambda e, sl=sl: e.activation(qqa[:, sl], emat[:, 1, :], AF.Sigmoid, bias=0.0, scale=1.0),
                             reads=[emat_b], writes=[qqa_b], tag="ema")
                        p.op("vector", lambda e, sl=sl: e.tensor_tensor(qqa[:, sl], wwa[:, sl], qqa[:, sl], op=ALU.mult),
                             reads=[wwa_b, qqa_b], writes=[qqa_b], tag="emq1")
                        p.op("vector", lambda e, sl=sl: e.tensor_scalar(qqa[:, sl], qqa[:, sl], -1.0, 1.0, op0=ALU.mult, op1=ALU.add),
                             reads=[qqa_b], writes=[qqa_b], tag="emq2")
                        p.op("vector", lambda e, sl=sl: e.tensor_tensor(wwa[:, sl], wwa[:, sl], emat[:, 2, :], op=ALU.mult),
                             reads=[wwa_b, emat_b], writes=[wwa_b], tag="emw1")
                        p.op("vector", lambda e, sl=sl: e.tensor_tensor(wwa[:, sl], wwa[:, sl], emat[:, 3, :], op=ALU.mult),
                             reads=[wwa_b, emat_b], writes=[wwa_b], tag="emw2")
                        p.op("vector", lambda e, sl=sl: e.tensor_scalar(wwa[:, sl], wwa[:, sl], 0.25, None, op0=ALU.mult),
                             reads=[wwa_b], writes=[wwa_b], tag="emw3")
                    for j in range(8):
                        bank, bank_b = ctx.take_bank()
                        for hh in range(2):
                            p.op("tensor", lambda e, hh=hh, bank=bank: e.matmul(
                                bank.ap()[0:1, :], ctx.ones_col[0:16, :], wwa[:, 512 * hh:512 * hh + 512],
                                start=True, stop=True),
                                 reads=[wwa_b, ctx.ones_buf], writes=[bank_b], tag="emsum")
                            p.op("vector", lambda e, hh=hh, bank=bank: e.tensor_copy(
                                krow[:, 512 * hh:512 * hh + 512], bank.ap()[0:1, :]),
                                 reads=[bank_b], writes=[krow_b], tag="emkev")
                        bank2, bank2_b = ctx.take_bank()
                        for hh in range(2):
                            p.op("tensor", lambda e, hh=hh, bank2=bank2: e.matmul(
                                bank2.ap()[:, 0:512], ctx.ones_row[0:1, :], krow[:, 512 * hh:512 * hh + 512],
                                start=True, stop=True),
                                 reads=[krow_b, ctx.onesr_buf], writes=[bank2_b], tag="emkbc")
                            p.op("vector", lambda e, j=j, hh=hh, bank2=bank2: e.tensor_copy(
                                kbc[:, j, 512 * hh:512 * hh + 512], bank2.ap()[:, 0:512]),
                                 reads=[bank2_b], writes=[kbc_b], tag="emkbce")
                        if j < 7:
                            p.op("vector", lambda e: e.tensor_tensor(wwa[:], wwa[:], qqa[:], op=ALU.mult),
                                 reads=[qqa_b, wwa_b], writes=[wwa_b], tag="emiter")
                mx, mx_b = ctx.sb("l3a_mx", [128, 8, 1024], BF16)
                cacc, cacc_b = ctx.sb("l3a_cacc", [128, 1024])
                ctmp, ctmp_b = ctx.sb("l3a_ctmp", [128, 1024])
                for s in range(8):
                    for j in range(s + 1):
                        if j == 0:
                            p.op("vector", lambda e, s=s: e.tensor_tensor(
                                cacc[:], xn[:, s, :], kbc[:, 0, :], op=ALU.mult),
                                 reads=[xn_b, kbc_b], writes=[cacc_b], tag="cv0")
                        else:
                            p.op("vector", lambda e, s=s, j=j: e.tensor_tensor(
                                ctmp[:], xn[:, s - j, :], kbc[:, j, :], op=ALU.mult),
                                 reads=[xn_b, kbc_b], writes=[ctmp_b], tag="cv1")
                            p.op("vector", lambda e: e.tensor_tensor(cacc[:], cacc[:], ctmp[:], op=ALU.add),
                                 reads=[ctmp_b, cacc_b], writes=[cacc_b], tag="cv2")
                    p.op("vector", lambda e, s=s: e.tensor_tensor(ctmp[:], xn[:, s, :], ombc[:], op=ALU.mult),
                         reads=[xn_b, ombc_b], writes=[ctmp_b], tag="cvom")
                    p.op("vector", lambda e: e.tensor_tensor(cacc[:], cacc[:], ctmp[:], op=ALU.add),
                         reads=[ctmp_b, cacc_b], writes=[cacc_b], tag="cvom2")
                    p.op("scalar", lambda e, s=s: e.activation(mx[:, s, :], cacc[:], AF.Silu, bias=0.0, scale=1.0),
                         reads=[cacc_b], writes=[mx_b], tag="cvsilu")
                for s in range(8):
                    for g in range(2):
                        bank, bank_b = ctx.take_bank()
                        bv = bank.ap().bitcast(BF16)
                        for j4 in range(4):
                            p.op("tensor", lambda e, s=s, g=g, j4=j4, bv=bv: e.transpose(
                                bv[:, 128 * j4:128 * j4 + 128], mx[:, s, 512 * g + 128 * j4:512 * g + 128 * j4 + 128],
                                ctx.ident[:]),
                                 reads=[mx_b, ctx.ident_buf], writes=[bank_b], tag="mxxp")
                        p.op("vector", lambda e, s=s, g=g, bv=bv: e.tensor_copy(
                            mxT[:, 4 * g:4 * g + 4, 128 * s:128 * s + 128],
                            bv[:, 0:512].rearrange("p (a b) -> p a b", b=128)),
                             reads=[bank_b], writes=[mxT_b], tag="mxxpe2")
                p.dma("sync", lambda e: e.dma_start(out=d_mxT[:], in_=mxT[:]),
                      reads=[mxT_b], writes=[d_mxT_b], tag="sp_mxT")
            with ctx.scope("l3a_vp"):
                xnT2, xnT2_b = ctx.sb("xnT3b", [128, 8, 1024], BF16)
                p.dma("sync", lambda e: e.dma_start(out=xnT2[:], in_=d_xnT[:]),
                      reads=[d_xnT_b], writes=[xnT2_b], tag="ld_xnT")
                wbuf, wbuf_b = ctx.sb("l3a_wbuf", [128, 8, 512], BF16)
                vstg, vstg_b = ctx.sb("l3a_vstg", [128, 512], BF16)
                for vh in range(4):
                    p.dma("gpsimd", lambda e, vh=vh: e.dma_start(
                        out=wbuf[:], in_=din["l3a_vwT"][:, 512 * vh:512 * vh + 512].rearrange("(c p) d -> p c d", p=128)),
                          writes=[wbuf_b], tag="ldvw")
                    for s in range(8):
                        bank, bank_b = ctx.take_bank()
                        mm = [(xnT2[:, c, 128 * s:128 * s + 128], wbuf[:, c, :],
                               [xnT2_b, wbuf_b]) for c in range(8)]
                        mm.append((ctx.ones_row_bf[0:1, :], vb_row[:, 512 * vh:512 * vh + 512],
                                   [ctx.onesrbf_buf, vb_row_b]))
                        ctx.matmul_acc(bank.ap(), bank_b, mm, tag="vmm3")
                        p.op("scalar", lambda e, bank=bank: e.activation(
                            vstg[:], bank.ap(), AF.Silu, bias=0.0, scale=1.0),
                             reads=[bank_b], writes=[vstg_b], tag="vev3")
                        p.dma("sync", lambda e, s=s, vh=vh: e.dma_start(
                            out=d_v[:, s, 512 * vh:512 * vh + 512], in_=vstg[:]),
                              reads=[vstg_b], writes=[d_v_b], tag="sp_v")
            with ctx.scope("l3a_base"):
                mxT2, mxT2_b = ctx.sb("mxT3b", [128, 8, 1024], BF16)
                p.dma("sync", lambda e: e.dma_start(out=mxT2[:], in_=d_mxT[:]),
                      reads=[d_mxT_b], writes=[mxT2_b], tag="ld_mxT")
                wbuf2, wbuf2_b = ctx.sb("l3a_wbuf2", [128, 8, 512], BF16)
                stg, stg_b = ctx.sb("l3a_stg", [128, 512], BF16)
                sections = [(0, 1024, d_u, d_u_b, AF.Sigmoid),
                            (1024, 1152, None, None, AF.Silu),
                            (1152, 3200, d_r, d_r_b, AF.Silu),
                            (3200, 4224, d_hx, d_hx_b, AF.Copy)]
                for (c0, c1, dsp, dsp_b, act) in sections:
                    for w0 in range(c0, c1, 512):
                        wd = min(512, c1 - w0)
                        p.dma("gpsimd", lambda e, w0=w0, wd=wd: e.dma_start(
                            out=wbuf2[:, :, 0:wd], in_=din["l3a_mxwT"][:, w0:w0 + wd].rearrange("(c p) d -> p c d", p=128)),
                              writes=[wbuf2_b], tag="ldmxw")
                        for s in range(8):
                            bank, bank_b = ctx.take_bank()
                            mm = [(mxT2[:, c, 128 * s:128 * s + 128], wbuf2[:, c, 0:wd],
                                   [mxT2_b, wbuf2_b]) for c in range(8)]
                            mm.append((ctx.ones_row_bf[0:1, :], mxb_row[:, w0:w0 + wd],
                                       [ctx.onesrbf_buf, mxb_row_b]))
                            ctx.matmul_acc(bank.ap()[:, 0:wd], bank_b, mm, tag="bsmm")
                            off = w0 - c0
                            if dsp is None:
                                p.op("scalar", lambda e, s=s, wd=wd, bank=bank: e.activation(
                                    zz[:, s, 0:wd], bank.ap()[:, 0:wd], AF.Silu, bias=0.0, scale=1.0),
                                     reads=[bank_b], writes=[zz_b], tag="bsevz")
                            else:
                                p.op("scalar", lambda e, wd=wd, bank=bank, act=act: e.activation(
                                    stg[:, 0:wd], bank.ap()[:, 0:wd], act, bias=0.0, scale=1.0),
                                     reads=[bank_b], writes=[stg_b], tag="bsevs")
                                p.dma("sync", lambda e, s=s, off=off, wd=wd, dsp=dsp: e.dma_start(
                                    out=dsp[:, s, off:off + wd], in_=stg[:, 0:wd]),
                                      reads=[stg_b], writes=[dsp_b], tag="spill")
            hpT, hpT_b = ctx.sb("hpT", [128, 16, 1024], BF16)
            with ctx.scope("l3a_attn"):
                rr_a, rr_a_b = ctx.sb("l3a_ra", [128, 8, 2048], BF16)
                p.dma("sync", lambda e: e.dma_start(out=rr_a[:], in_=d_r[:]),
                      reads=[d_r_b], writes=[rr_a_b], tag="ld_r")
                vv_a, vv_a_b = ctx.sb("l3a_va", [128, 8, 2048], BF16)
                p.dma("sync", lambda e: e.dma_start(out=vv_a[:], in_=d_v[:]),
                      reads=[d_v_b], writes=[vv_a_b], tag="ld_v")
                qq_, qq_b = ctx.sb("l3a_q", [128, 8, 128], BF16)
                kk_, kk_b = ctx.sb("l3a_k", [128, 8, 128], BF16)
                S64, S64_b = ctx.sb("l3a_S", [128, 8, 8])
                nm1, nm1_b = ctx.sb("l3a_nm", [128, 1])
                ss1, ss1_b = ctx.sb("l3a_ss", [128, 1])
                rs1, rs1_b = ctx.sb("l3a_rs", [128, 1])
                hacc, hacc_b = ctx.sb("l3a_hacc", [128, 2048])
                htmp, htmp_b = ctx.sb("l3a_htmp", [128, 2048], BF16)
                qtmp, qtmp_b = ctx.sb("l3a_qtmp", [128, 128])
                for s in range(8):
                    p.op("vector", lambda e, s=s: e.tensor_tensor(qq_[:, s, :], zz[:, s, :], g0bc[:], op=ALU.mult),
                         reads=[zz_b, g0bc_b], writes=[qq_b], tag="q3a")
                    p.op("vector", lambda e, s=s: e.tensor_tensor(qq_[:, s, :], qq_[:, s, :], b0bc[:], op=ALU.add),
                         reads=[qq_b, b0bc_b], writes=[qq_b], tag="q3b")
                    p.op("vector", lambda e, s=s: e.tensor_scalar(qq_[:, s, :], qq_[:, s, :], RT2, None, op0=ALU.mult),
                         reads=[qq_b], writes=[qq_b], tag="q3c")
                    p.op("vector", lambda e, s=s: e.tensor_tensor(kk_[:, s, :], zz[:, s, :], g1bc[:], op=ALU.mult),
                         reads=[zz_b, g1bc_b], writes=[kk_b], tag="k3a")
                    p.op("vector", lambda e, s=s: e.tensor_tensor(kk_[:, s, :], kk_[:, s, :], b1bc[:], op=ALU.add),
                         reads=[kk_b, b1bc_b], writes=[kk_b], tag="k3b")
                for s in range(8):
                    for t in range(8):
                        p.op("vector", lambda e, s=s, t=t: e.tensor_tensor(
                            qtmp[:], qq_[:, s, :], kk_[:, t, :], op=ALU.mult),
                             reads=[qq_b, kk_b], writes=[qtmp_b], tag="qk3")
                        p.op("vector", lambda e, s=s, t=t: e.tensor_reduce(
                            S64[:, s, t:t + 1], qtmp[:].rearrange("p (a b) -> p a b", a=1),
                            axis=AX.X, op=ALU.add),
                             reads=[qtmp_b], writes=[S64_b], tag="sred3")
                p.op("vector", lambda e: e.tensor_tensor(
                    S64[:].rearrange("p a b -> p (a b)"), S64[:].rearrange("p a b -> p (a b)"), biasbc[:], op=ALU.add),
                     reads=[S64_b, biasbc_b], writes=[S64_b], tag="sbias3")
                for s in range(8):
                    p.op("vector", lambda e, s=s: e.tensor_reduce(
                        nm1[:], S64[:, s, :], axis=AX.X, op=ALU.max, negate=True),
                         reads=[S64_b], writes=[nm1_b], tag="smax3")
                    p.op("scalar", lambda e, s=s: e.activation(
                        S64[:, s, :], S64[:, s, :], AF.Exp, bias=nm1[:], scale=1.0, accum_out=ss1[:]),
                         reads=[S64_b, nm1_b], writes=[S64_b, ss1_b], tag="sexp3")
                    p.op("vector", lambda e: e.reciprocal(rs1[:], ss1[:]),
                         reads=[ss1_b], writes=[rs1_b], tag="srec3")
                    p.op("vector", lambda e, s=s: e.tensor_scalar(
                        S64[:, s, :], S64[:, s, :], rs1[:], None, op0=ALU.mult),
                         reads=[S64_b, rs1_b], writes=[S64_b], tag="snorm3")
                    for t in range(8):
                        p.op("vector", lambda e, s=s, t=t: e.tensor_scalar(
                            htmp[:], vv_a[:, t, :], S64[:, s, t:t + 1], None, op0=ALU.mult),
                             reads=[vv_a_b, S64_b], writes=[htmp_b], tag="pv3")
                        if t == 0:
                            p.op("vector", lambda e: e.tensor_copy(hacc[:], htmp[:]),
                                 reads=[htmp_b], writes=[hacc_b], tag="pv3a")
                        else:
                            p.op("vector", lambda e: e.tensor_tensor(hacc[:], hacc[:], htmp[:], op=ALU.add),
                                 reads=[htmp_b, hacc_b], writes=[hacc_b], tag="pv3b")
                    p.op("vector", lambda e, s=s: e.tensor_tensor(rr_a[:, s, :], hacc[:], rr_a[:, s, :], op=ALU.mult),
                         reads=[hacc_b, rr_a_b], writes=[rr_a_b], tag="hr3")
                for s in range(8):
                    for g in range(4):
                        bank, bank_b = ctx.take_bank()
                        bv = bank.ap().bitcast(BF16)
                        for j4 in range(4):
                            p.op("tensor", lambda e, s=s, g=g, j4=j4, bv=bv: e.transpose(
                                bv[:, 128 * j4:128 * j4 + 128], rr_a[:, s, 512 * g + 128 * j4:512 * g + 128 * j4 + 128],
                                ctx.ident[:]),
                                 reads=[rr_a_b, ctx.ident_buf], writes=[bank_b], tag="hxp3")
                        p.op("vector", lambda e, s=s, g=g, bv=bv: e.tensor_copy(
                            hpT[:, 4 * g:4 * g + 4, 128 * s:128 * s + 128],
                            bv[:, 0:512].rearrange("p (a b) -> p a b", b=128)),
                             reads=[bank_b], writes=[hpT_b], tag="hxpe3")
            with ctx.scope("l3a_fin"):
                hw, hw_b = ctx.sb("l3a_hw", [128, 16, 512], BF16)
                h2s, h2s_b = ctx.sb("l3a_h2", [128, 1024])
                res, res_b = ctx.sb("l3a_res", [128, 1, 1024])
                dd, dd_b = ctx.sb("l3a_dd", [128, 1024])
                ustg, ustg_b = ctx.sb("l3a_ustg", [128, 1024], BF16)
                hxstg, hxstg_b = ctx.sb("l3a_hxstg", [128, 1024], BF16)
                h2h, h2h_b = ctx.sb("l3a_h2h", [128, 8, 1024])
                for dh in range(2):
                    p.dma("gpsimd", lambda e, dh=dh: e.dma_start(
                        out=hw[:], in_=din["l3a_hwT"][:, 512 * dh:512 * dh + 512].rearrange("(c p) d -> p c d", p=128)),
                          writes=[hw_b], tag="ldhw")
                    for s in range(8):
                        bank, bank_b = ctx.take_bank()
                        mm = [(hpT[:, fs, 128 * s:128 * s + 128], hw[:, fs, :],
                               [hpT_b, hw_b]) for fs in range(16)]
                        mm.append((ctx.ones_row_bf[0:1, :], hb_row[:, 512 * dh:512 * dh + 512],
                                   [ctx.onesrbf_buf, hb_row_b]))
                        ctx.matmul_acc(bank.ap(), bank_b, mm, tag="h2mm")
                        p.op("vector", lambda e, s=s, dh=dh, bank=bank: e.tensor_copy(
                            h2h[:, s, 512 * dh:512 * dh + 512], bank.ap()),
                             reads=[bank_b], writes=[h2h_b], tag="h2ev")
                for s in range(8):
                    p.dma("sync", lambda e, s=s: e.dma_start(out=hxstg[:], in_=d_hx[:, s, :]),
                          reads=[d_hx_b], writes=[hxstg_b], tag="unspill_hx")
                    p.dma("sync", lambda e, s=s: e.dma_start(out=ustg[:], in_=d_u[:, s, :]),
                          reads=[d_u_b], writes=[ustg_b], tag="unspill_u")
                    p.op("vector", lambda e, s=s: e.tensor_tensor(h2s[:], h2h[:, s, :], hxstg[:], op=ALU.add),
                         reads=[h2h_b, hxstg_b], writes=[h2s_b], tag="h2add")
                    p.op("scalar", lambda e: e.activation(h2s[:], h2s[:], AF.Silu, bias=0.0, scale=1.0),
                         reads=[h2s_b], writes=[h2s_b], tag="h2silu")
                    _ln_to_featmajor(ctx, x_t, x_bufs, None, None, tile_list=[8 * bt + s],
                                     xn_out=(res, [res_b]))
                    p.op("vector", lambda e, s=s: e.tensor_tensor(dd[:], h2s[:], res[:, 0, :], op=ALU.subtract),
                         reads=[h2s_b, res_b], writes=[dd_b], tag="fin1")
                    p.op("vector", lambda e, s=s: e.tensor_tensor(dd[:], dd[:], ustg[:], op=ALU.mult),
                         reads=[dd_b, ustg_b], writes=[dd_b], tag="fin2")
                    p.op("vector", lambda e, s=s: e.tensor_tensor(dd[:], dd[:], res[:, 0, :], op=ALU.add),
                         reads=[dd_b, res_b], writes=[dd_b], tag="fin3")
                    p.op("vector", lambda e, s=s, ti=8 * bt + s: e.tensor_tensor(
                        x_t[:, ti, :], x_t[:, ti, :], dd[:], op=ALU.add),
                         reads=[dd_b, x_bufs[8 * bt + s]], writes=[x_bufs[8 * bt + s]], tag="fin4")

    for bt in range(2):
        _bt_body(bt)



def _rope_tables():
    half = 32
    inv_freq = 1.0 / 10000.0 ** (np.arange(half, dtype=np.float64) * 2.0 / 64)
    ang = np.arange(2048, dtype=np.float64)[:, None] * inv_freq[None, :]
    return np.cos(ang).astype(np.float32), np.sin(ang).astype(np.float32)


def _rot_tables():
    import math
    half = 64
    freqs = np.exp(np.arange(half) * (-math.log(10000.0) / half))
    pos = np.arange(8)[:, None] * freqs[None, :]
    return np.cos(pos).astype(np.float32), np.sin(pos).astype(np.float32)


def _phase1_body(ctx, din, dout):
    ctx.causal_masks()
    x_t, x_bufs = load_stream(ctx, din["x"])
    with ctx.scope("B_l0a"):
        emit_L0_attn(ctx, din, x_t, x_bufs)
    with ctx.scope("B_l0f"):
        emit_L0_ffn(ctx, din, x_t, x_bufs)
    with ctx.scope("B_l1a"):
        emit_L1_attn(ctx, din, x_t, x_bufs)
    with ctx.scope("B_l1f"):
        emit_L1_ffn(ctx, din, x_t, x_bufs)
    with ctx.scope("B_l2a"):
        emit_L2_attn(ctx, din, x_t, x_bufs)
    store_stream(ctx, dout["out"], x_t, x_bufs)


def _phase2_body(ctx, din, dout):
    x_t, _ = ctx.sb("x", [128, 16, 1024])
    x_bufs = [ctx.prog.buf(f"x{i}") for i in range(16)]
    for bt in range(2):
        ctx.prog.dma("sync", lambda e, bt=bt: e.dma_start(
            out=x_t[:, 8 * bt:8 * bt + 8, :],
            in_=din["x2"][:, 128 * bt:128 * bt + 128, :].rearrange("s p d -> p s d")),
            writes=x_bufs[8 * bt:8 * bt + 8], tag="ldx")
    with ctx.scope("B_l2f"):
        emit_L2_ffn(ctx, din, x_t, x_bufs)
    with ctx.scope("B_l3a"):
        emit_L3_attn(ctx, din, x_t, x_bufs)
    with ctx.scope("B_l3f"):
        emit_L3_ffn(ctx, din, x_t, x_bufs)
    for bt in range(2):
        ctx.prog.dma("sync", lambda e, bt=bt: e.dma_start(
            out=dout["out"][:, 128 * bt:128 * bt + 128, :].rearrange("s p d -> p s d"),
            in_=x_t[:, 8 * bt:8 * bt + 8, :]),
            reads=x_bufs[8 * bt:8 * bt + 8], tag="stx")


_CACHE = {}


def _phase1_weights(I):
    C = np.ascontiguousarray
    sel8 = np.zeros((8, 128), np.float32)
    sel8[np.arange(128) // 16, np.arange(128)] = 1
    sel16 = np.zeros((16, 128), np.float32)
    sel16[np.arange(128) % 16, np.arange(128)] = 1
    cos, sin = _rope_tables()
    g1p = C(np.stack([I["L0_ffn_G1_R"], I["L0_ffn_G1_I"]], 0).reshape(2, 32, 4, 4, 8).transpose(4, 0, 2, 3, 1))
    g2p = C(np.stack([I["L0_ffn_G2_R"], I["L0_ffn_G2_I"]], 0).reshape(2, 4, 16, 16).transpose(3, 0, 1, 2))
    return {
        "l0a_wdqT": C(I["L0_attn_W_DQ"].T), "l0a_wuqT": C(I["L0_attn_W_UQ"].T),
        "l0a_wdkvT": C(I["L0_attn_W_DKV"].T), "l0a_wukvT": C(I["L0_attn_W_UKV"].T),
        "l0a_wqrT": C(I["L0_attn_W_QR"].T), "l0a_wkrT": C(I["L0_attn_W_KR"].T),
        "l0a_woT": C(I["L0_attn_W_O"].T),
        "l0a_normw": C(np.stack([I["L0_attn_q_norm_w"], I["L0_attn_k_norm_w"]], 1)),
        "l0f_g1p": g1p, "l0f_g2p": g2p,
        "l0f_br": I["L0_ffn_bias_real"].reshape(1, -1), "l0f_bi": I["L0_ffn_bias_imag"].reshape(1, -1),
        "l0f_sel8": sel8, "l0f_sel16": sel16,
        "l1a_wq": I["L1_attn_Wq"], "l1a_wk": I["L1_attn_Wk"],
        "l1a_wv": I["L1_attn_Wv"], "l1a_wp": I["L1_attn_Wp"],
        "l1a_bq": I["L1_attn_bq"].reshape(1, -1), "l1a_bk": I["L1_attn_bk"].reshape(1, -1),
        "l1a_bv": I["L1_attn_bv"].reshape(1, -1), "l1a_bp": I["L1_attn_bp"].reshape(1, -1),
        "ropecos": cos, "ropesin": sin,
        "l1f_gate_w": I["L1_ffn_gate_w"], "l1f_gate_b": I["L1_ffn_gate_b"].reshape(1, -1),
        "l1f_w1": I["L1_ffn_w1"],
        "l1f_b1c": C(I["L1_ffn_b1"].reshape(8, 32, 128).transpose(2, 0, 1)),
        "l1f_w2": I["L1_ffn_w2"], "l1f_b2": I["L1_ffn_b2"],
        "l2a_params": I["L2_attn_params"],
        "l2a_wvT": C(I["L2_attn_Wv"].T), "l2a_woT": C(I["L2_attn_Wo"].T),
        "l2a_bv": I["L2_attn_bv"].reshape(1, -1), "l2a_bo": I["L2_attn_bo"].reshape(1, -1),
    }


def _phase2_weights(I):
    C = np.ascontiguousarray
    rcos, rsin = _rot_tables()
    return {
        "l2f_ln1w": I["L2_ffn_ln1_w"].reshape(1, -1), "l2f_ln1b": I["L2_ffn_ln1_b"].reshape(1, -1),
        "l2f_ln2w": I["L2_ffn_ln2_w"].reshape(1, -1), "l2f_ln2b": I["L2_ffn_ln2_b"].reshape(1, -1),
        "l2f_inprojT": C(I["L2_ffn_in_proj_w"].T), "l2f_inprojb": I["L2_ffn_in_proj_b"].reshape(1, -1),
        "l2f_outprojT": C(I["L2_ffn_out_proj_w"].T), "l2f_outprojb": I["L2_ffn_out_proj_b"].reshape(1, -1),
        "l2f_routerT": C(I["L2_ffn_router_w"].T),
        "l2f_w1": I["L2_ffn_w1"], "l2f_w2": I["L2_ffn_w2"],
        "l3a_lnw": I["L3_attn_ln_w"].reshape(1, -1), "l3a_lnb": I["L3_attn_ln_b"].reshape(1, -1),
        "l3a_vwT": C(I["L3_attn_v_w"].T), "l3a_vb": I["L3_attn_v_b"].reshape(1, -1),
        "l3a_mxwT": C(I["L3_attn_mx_w"].T), "l3a_mxb": I["L3_attn_mx_b"].reshape(1, -1),
        "l3a_hwT": C(I["L3_attn_h_w"].T), "l3a_hb": I["L3_attn_h_b"].reshape(1, -1),
        "l3a_gambet": C(np.concatenate([I["L3_attn_qk_gamma"][0], I["L3_attn_qk_gamma"][1],
                                        I["L3_attn_qk_beta"][0], I["L3_attn_qk_beta"][1]]).reshape(1, -1)),
        "l3a_emaT": C(np.concatenate([I["L3_attn_ema_delta"][:, :, 0].T, I["L3_attn_ema_alpha"][:, :, 0].T,
                                      I["L3_attn_ema_beta"][:, :, 0].T, I["L3_attn_ema_gamma"].T], axis=1)),
        "l3a_omega": I["L3_attn_ema_omega"].reshape(1, -1),
        "l3a_rb": C(np.concatenate([I["L3_attn_rb_alpha"][0], I["L3_attn_rb_beta"][0]]).reshape(1, -1)),
        "l3a_rcos": rcos, "l3a_rsin": rsin,
        "l3f_win": I["L3_ffn_Win"], "l3f_bin": I["L3_ffn_bin_"].reshape(1, -1),
        "l3f_w1": I["L3_ffn_W1"].reshape(1, -1), "l3f_b1": I["L3_ffn_b1"].reshape(1, -1),
        "l3f_w2": I["L3_ffn_W2"].reshape(1, -1), "l3f_b2": I["L3_ffn_b2"].reshape(1, -1),
        "l3f_wout": I["L3_ffn_Wout"], "l3f_bout": I["L3_ffn_bout"].reshape(1, -1),
    }


def kernel(**inputs):
    I = inputs
    w1 = _phase1_weights(I)
    w2 = _phase2_weights(I)
    x = np.ascontiguousarray(I["L0_attn_x"]).astype(np.float32, copy=False)

    if "p1" not in _CACHE:
        L1_ = Launch("p1")
        specs1 = {k: v.shape for k, v in w1.items()}
        specs1["x"] = (2048, 1024)
        L1_.build(_phase1_body, specs1, {"out": (2048, 1024)})
        _CACHE["p1"] = L1_
    if "p2" not in _CACHE:
        L2_ = Launch("p2")
        specs2 = {k: v.shape for k, v in w2.items()}
        specs2["x2"] = (8, 256, 1024)
        L2_.build(_phase2_body, specs2, {"out": (8, 256, 1024)})
        _CACHE["p2"] = L2_

    in_maps1 = []
    for c in range(NCORES):
        m = dict(w1)
        m["x"] = np.ascontiguousarray(x[c])
        in_maps1.append(m)
    res1 = run_bass_kernel_spmd(_CACHE["p1"].nc, in_maps1, core_ids=list(range(NCORES)))
    mid = np.stack([r["out"] for r in res1.results], 0)

    in_maps2 = []
    for c in range(NCORES):
        m = dict(w2)
        m["x2"] = np.ascontiguousarray(mid[:, 256 * c:256 * c + 256, :])
        in_maps2.append(m)
    res2 = run_bass_kernel_spmd(_CACHE["p2"].nc, in_maps2, core_ids=list(range(NCORES)))
    out = np.concatenate([r["out"].reshape(8, 256, 1024) for r in res2.results], axis=1)
    return out.astype(np.float32)
